# version 24
# baseline (speedup 1.0000x reference)
import numpy as np
import ml_dtypes
import concourse.bass as bass
import concourse.mybir as mybir
from concourse.bass_utils import run_bass_kernel_spmd

F32 = mybir.dt.float32
BF16 = mybir.dt.bfloat16
AF = mybir.ActivationFunctionType
ALU = mybir.AluOpType

P = 128
TW = 1152
TR = 1024
WU = 128
D = 2048
DT = 16
NST = 16
NEXP = 64
ART = 8
JT = 16
R = 128
TCH = [(0, 512), (512, 512), (1024, 128)]
TCHR = [(128, 512), (640, 512)]


class Plan:

    AUTOE = ("vector", "scalar")

    def __init__(self, nc):
        self.nc = nc
        self.ops = {e: [] for e in ("sync", "scalar", "vector", "tensor", "gpsimd")}
        self.sems = {}
        self.counts = {}
        self.auto = {e: 0 for e in self.AUTOE}
        self.autosem = {e: nc.alloc_semaphore("auto_" + e) for e in self.AUTOE}
        self.nameeng = {}
        self.namevals = {}

    _MERGE = {"vr1": "vr", "vr2": "vr", "vr3": "vr", "sr1": "sr", "sr2": "sr",
              "tt2": "tt", "tre2": "tre", "vc": "vmisc", "vps": "vmisc",
              "gb": "gx", "gbd": "gx2", "gcd": "gx3"}

    def canon(self, name):
        base = name.split("_")[0]
        return self._MERGE.get(base, base)

    def sem(self, name):
        name = self.canon(name)
        if name not in self.sems:
            self.sems[name] = self.nc.alloc_semaphore("s_" + name)
            self.counts[name] = 0
        return self.sems[name]

    def op(self, eng, opfn, sem=None, dma=False, nowait=False):
        if eng in self.AUTOE and not dma:
            idx = self.auto[eng] + 1
            self.auto[eng] = idx
            a = self.autosem[eng]
            if idx > 1 and not nowait:
                self.ops[eng].append(lambda e, a=a, v=idx - 1: e.wait_ge(a, v))
            if sem is not None:
                cn = self.canon(sem)
                self.counts.setdefault(cn, 0)
                self.counts[cn] += 1
                self.nameeng.setdefault(cn, eng)
                assert self.nameeng[cn] == eng, (cn, eng)
                self.namevals.setdefault(cn, []).append(idx)
            self.ops[eng].append(lambda e, opfn=opfn, a=a: opfn(e).then_inc(a, 1))
            return
        if sem is not None:
            s = self.sem(sem)
            cn = self.canon(sem)
            self.counts[cn] += 16 if dma else 1
            self.nameeng.setdefault(cn, None)
            self.ops[eng].append(lambda e, opfn=opfn, s=s, k=(16 if dma else 1):
                                 opfn(e).then_inc(s, k))
        else:
            self.ops[eng].append(lambda e, opfn=opfn: opfn(e))

    def wait(self, eng, sem, val=None):
        cn = self.canon(sem)
        self.counts.setdefault(cn, 0)
        v = self.counts[cn] if val is None else val
        if v <= 0:
            return

        def resolved(e, eng=eng, cn=cn, v=v):
            if self.nameeng.get(cn) in self.AUTOE:
                src_eng = self.nameeng[cn]
                vals = self.namevals[cn]
                assert v <= len(vals), (cn, v, len(vals))
                tgt = vals[v - 1]
                if eng == src_eng:
                    return
                e.wait_ge(self.autosem[src_eng], tgt)
            else:
                e.wait_ge(self.sem(cn), v)
        self.ops[eng].append(resolved)

    def val(self, sem):
        cn = self.canon(sem)
        self.counts.setdefault(cn, 0)
        return self.counts[cn]

    def build(self, blk):
        for eng_name, deco in (("sync", blk.sync), ("scalar", blk.scalar),
                               ("vector", blk.vector), ("tensor", blk.tensor),
                               ("gpsimd", blk.gpsimd)):
            ops = self.ops[eng_name]

            def section(e, ops=ops):
                for fn in ops:
                    fn(e)
            deco(section)


def build_nc(phases=7, out_xln=False, ffn_sub=8):
    nc = bass.Bass("TRN2", target_bir_lowering=False)

    d_x = nc.dram_tensor("x_fm", [D, TW], F32, kind="ExternalInput")
    d_WxTa = nc.dram_tensor("WxTa", [D, R], F32, kind="ExternalInput")
    d_WxTb = nc.dram_tensor("WxTb", [D, 32], F32, kind="ExternalInput")
    d_WdT = nc.dram_tensor("WdT", [R, D], F32, kind="ExternalInput")
    d_bdt = nc.dram_tensor("bdt_cols", [P, DT], F32, kind="ExternalInput")
    d_Alog = nc.dram_tensor("Alog_cols", [P, DT * NST], F32, kind="ExternalInput")
    d_Dp = nc.dram_tensor("D_cols", [P, DT], F32, kind="ExternalInput")
    d_AlhsT = nc.dram_tensor("A_lhsT", [D, 1024], F32, kind="ExternalInput")
    d_WT = nc.dram_tensor("W_T", [D, D], F32, kind="ExternalInput")
    d_BfT = nc.dram_tensor("Bf_T", [1024, D], F32, kind="ExternalInput")
    d_bias = nc.dram_tensor("bias_row", [1, D], F32, kind="ExternalInput")
    d_ident = nc.dram_tensor("identC", [P, P], BF16, kind="ExternalInput")
    d_ones = nc.dram_tensor("onesC", [P, 512], BF16, kind="ExternalInput")
    d_selsq = nc.dram_tensor("selsqC", [1024, NEXP], BF16, kind="ExternalInput")
    d_repr = nc.dram_tensor("reprC", [NEXP, 1024], BF16, kind="ExternalInput")
    d_selr = nc.dram_tensor("selrC", [1024, NST], BF16, kind="ExternalInput")
    d_selrep = nc.dram_tensor("selrepC", [NST, 1024], BF16, kind="ExternalInput")
    d_bmask = nc.dram_tensor("bmask", [NST, TW], BF16, kind="ExternalInput")
    d_out = nc.dram_tensor("out_fm", [D, TR], F32, kind="ExternalOutput")
    d_rows = nc.dram_tensor("rowsD", [32, TW], BF16)
    d_lnrows = nc.dram_tensor("lnrowsD", [2, TW], BF16)

    A = nc.alloc_sbuf_tensor
    x = A("x", [P, DT * TW], F32)
    xln = A("xln", [P, DT * TW], BF16)
    xbr = A("xbr", [P, 2 * TW], BF16)
    sqr = A("sqr", [P, 2 * TW], BF16)
    vecs = A("vecs", [P, ART * TW], BF16)
    U = A("U", [P, ART * TW], BF16)
    mpool = A("mpool", [P, ART * 512 + 1024], BF16)
    masked = mpool[:, 0:ART * 512]
    vrepr = mpool[:, ART * 512:ART * 512 + 1024]
    arrT = A("arrT", [P, 9 * NEXP], BF16)
    routes = A("routes", [P, 9 * NEXP], BF16)
    max8 = A("max8", [P, 9 * 8], BF16)
    thr = A("thr", [P, 16], F32)
    arr_fm = A("arr_fm", [NEXP, TW], BF16)
    vsm = A("vsm", [NST, TW], BF16)
    wbufA = A("wbufA", [P, DT * P], BF16)
    wbufB = A("wbufB", [P, DT * P], BF16)
    wbufC = A("wbufC", [P, DT * P], BF16)
    wbufD = A("wbufD", [P, DT * P], BF16)
    ubufA = A("ubufA", [P, ART * P], BF16)
    ubufB = A("ubufB", [P, ART * P], BF16)
    biasrow = A("biasrow", [1, D], BF16)
    xz_a = A("xz_a", [P, TW], BF16)
    xz_b = A("xz_b", [32, TW], BF16)
    bdt = A("bdt", [P, DT], F32)
    acols = A("acols", [P, DT * NST], F32)
    dcols = A("dcols", [P, DT], F32)
    ident = A("identS", [P, P], BF16)
    onesc = A("onesS", [P, 512], BF16)
    selsq = A("selsqS", [P, ART * NEXP], BF16)
    reprm = A("reprS", [NEXP, 1024], BF16)
    selr = A("selrS", [P, ART * NST], BF16)
    selrep = A("selrepS", [NST, 1024], BF16)
    constv = A("constv", [P, 4], F32)

    lnr = vecs[:, 0:4608].bitcast(F32)
    lnrb = vecs[:, 4608:4608 + 3 * TW]
    rstd_rep = U[:, 0:TW]
    routesT = arr_fm
    mur_rep = U[:, TW:2 * TW]
    wxa = wbufA
    wdt_w = wbufC
    wxb = ubufA
    _dA0, _dA1 = 0, TW
    _bu = 2 * TW
    _hh = 3 * TW
    _ww0, _ww1 = 4 * TW, 4 * TW + TR
    _dxt0 = 4 * TW + 2 * TR
    _dxt1 = _dxt0 + TW
    assert _dxt1 + TW <= ART * TW

    pb = [nc.alloc_psum_tensor(f"pb{i}", [P, 512], F32) for i in range(8)]

    E = Plan(nc)

    def xt(k, c0=0, n=TW):
        return x[:, k * TW + c0: k * TW + c0 + n]

    def xlnt(k, c0=0, n=TW):
        return xln[:, k * TW + c0: k * TW + c0 + n]

    E.op("sync", lambda e: e.dma_start(out=ident[:], in_=d_ident.ap()), sem="dx", dma=True)
    E.op("sync", lambda e: e.dma_start(out=onesc[:], in_=d_ones.ap()), sem="dx", dma=True)
    for k in range(ART):
        E.op("sync", lambda e, k=k: e.dma_start(
            out=selsq[:, k * NEXP:(k + 1) * NEXP], in_=d_selsq.ap()[k * P:(k + 1) * P, :]),
            sem="dx", dma=True)
        E.op("sync", lambda e, k=k: e.dma_start(
            out=selr[:, k * NST:(k + 1) * NST], in_=d_selr.ap()[k * P:(k + 1) * P, :]),
            sem="dx", dma=True)
    E.op("sync", lambda e: e.dma_start(out=reprm[:], in_=d_repr.ap()), sem="dx", dma=True)
    E.op("sync", lambda e: e.dma_start(out=selrep[:], in_=d_selrep.ap()), sem="dx", dma=True)
    for dst, src in [(bdt, d_bdt), (acols, d_Alog), (dcols, d_Dp)]:
        E.op("sync", lambda e, dst=dst, src=src: e.dma_start(out=dst[:], in_=src.ap()),
             sem="dx", dma=True)
    E.op("gpsimd", lambda e: e.dma_start(out=biasrow[:], in_=d_bias.ap()), sem="dbias", dma=True)
    for kg in range(8):
        E.op("sync", lambda e, kg=kg: e.dma_start(
            out=x[:, kg * 2 * TW:(kg + 1) * 2 * TW].rearrange(
                "p (k t) -> p k t", k=2),
            in_=d_x.ap()[kg * 2 * P:(kg + 1) * 2 * P, :].rearrange(
                "(k p) t -> p k t", p=P)),
            sem=f"dxg{kg}", dma=True)
    dx_all = E.val("dx")

    E.op("vector", lambda e: e.memset(constv[:, 0:1], 1e-5))
    E.op("vector", lambda e: e.memset(constv[:, 1:2], -0.5))
    E.op("vector", lambda e: e.memset(constv[:, 2:3], 1.0), sem="vcst")
    E.wait("scalar", "dx", dx_all)
    E.wait("scalar", "vcst")
    E.op("scalar", lambda e: e.activation(acols[:], acols[:], AF.Exp), sem="sAx")
    E.wait("vector", "sAx")
    E.op("vector", lambda e: e.tensor_scalar_mul(acols[:], acols[:], -1.0), sem="sA")

    def emit_strip(src, col0, width, dst, sem, rows_tiles=DT):
        E.op("gpsimd", lambda e, src=src, col0=col0, width=width, dst=dst,
             rt=rows_tiles: e.dma_start(
                 out=dst[:, 0:rt * width].rearrange("p (k w) -> p k w", k=rt),
                 in_=src.ap()[0:rt * P, col0:col0 + width].rearrange(
                     "(k p) w -> p k w", p=P)),
             sem=sem, dma=True)

    _ln_marks = {}
    def ln_pass(tag, tok0, ntok, xdx=False):
        tch = []
        c = tok0
        while c < tok0 + ntok:
            n = min(512, tok0 + ntok - c)
            tch.append((c, n))
            c += n
        nch = len(tch)
        tst0 = E.val("tst_" + tag)
        E.wait("vector", "tv")
        E.wait("vector", "tb")
        E.wait("vector", "txz")
        E.wait("vector", "td")
        for k in range(DT):
            sl = k % 2
            if xdx:
                E.wait("vector", f"dxg{k // 2}", 16)
            E.op("vector", lambda e, k=k: e.tensor_copy(
                xlnt(k, tok0, ntok), xt(k, tok0, ntok)), sem="vc_" + tag,
                nowait=(k >= 1))
            E.wait("scalar", "vc_" + tag)
            if k >= 2:
                E.wait("scalar", "tst_" + tag, tst0 + (k - 1) * 2 * nch)
            E.op("scalar", lambda e, k=k, sl=sl: e.activation(
                sqr[:, sl * TW:sl * TW + ntok], xlnt(k, tok0, ntok),
                AF.Square), sem="ss_" + tag)
            E.wait("tensor", "ss_" + tag)
            for ci, (c0, n) in enumerate(tch):
                E.op("tensor", lambda e, k=k, ci=ci, c0=c0, n=n: e.matmul(
                    pb[2 + ci][0:1, 0:n], onesc[0:P, 0:1],
                    xlnt(k, c0, n),
                    start=(k == 0), stop=(k == DT - 1)), sem="tst_" + tag)
                E.op("tensor", lambda e, k=k, sl=sl, ci=ci, c0=c0, n=n: e.matmul(
                    pb[5 + ci][0:1, 0:n], onesc[0:P, 0:1],
                    sqr[:, sl * TW + c0 - tok0:sl * TW + c0 - tok0 + n],
                    start=(k == 0), stop=(k == DT - 1)), sem="tst_" + tag)
        E.wait("vector", "tst_" + tag)
        for ci, (c0, n) in enumerate(tch):
            E.op("vector", lambda e, ci=ci, c0=c0, n=n: e.tensor_copy(
                lnr[0:1, c0:c0 + n], pb[2 + ci][0:1, 0:n]))
            E.op("vector", lambda e, ci=ci, c0=c0, n=n: e.tensor_copy(
                lnr[0:1, TW + c0:TW + c0 + n], pb[5 + ci][0:1, 0:n]), sem="vps_" + tag)
        mur = lnr[0:1, tok0:tok0 + ntok]
        varr = lnr[0:1, TW + tok0:TW + tok0 + ntok]
        mu2b = lnrb[0:1, tok0:tok0 + ntok]
        rstdb = lnrb[0:1, TW + tok0:TW + tok0 + ntok]
        murb = lnrb[0:1, 2 * TW + tok0:2 * TW + tok0 + ntok]
        E.op("vector", lambda e: e.tensor_scalar_mul(mur, mur, 1.0 / D))
        E.op("vector", lambda e: e.tensor_scalar_mul(varr, varr, 1.0 / D), sem="vr1_" + tag)
        E.wait("scalar", "vr1_" + tag)
        E.op("scalar", lambda e: e.activation(mu2b, mur, AF.Square), sem="sr1_" + tag)
        E.wait("vector", "sr1_" + tag)
        E.op("vector", lambda e: e.tensor_tensor(varr, varr, mu2b, ALU.subtract),
             sem="vr2_" + tag)
        E.wait("scalar", "vr2_" + tag)
        E.op("scalar", lambda e: e.activation(varr, varr, AF.Ln, bias=constv[0:1, 0:1]))
        E.op("scalar", lambda e: e.activation(varr, varr, AF.Exp, scale=constv[0:1, 1:2]),
             sem="sr2_" + tag)
        E.wait("vector", "sr2_" + tag)
        E.op("vector", lambda e: e.tensor_copy(rstdb, varr))
        E.op("vector", lambda e: e.tensor_tensor(mur, mur, varr, ALU.mult))
        E.op("vector", lambda e: e.tensor_copy(murb, mur), sem="vr3_" + tag)
        E.wait("tensor", "vr3_" + tag)
        for ci, (c0, n) in enumerate(tch):
            E.op("tensor", lambda e, ci=ci, c0=c0, n=n: e.matmul(
                pb[2 + ci][:, 0:n], onesc[0:1, 0:P],
                lnrb[0:1, TW + c0:TW + c0 + n], start=True, stop=True),
                sem="trp_" + tag)
            E.op("tensor", lambda e, ci=ci, c0=c0, n=n: e.matmul(
                pb[5 + ci][:, 0:n], onesc[0:1, 0:P],
                lnrb[0:1, 2 * TW + c0:2 * TW + c0 + n], start=True, stop=True),
                sem="trp_" + tag)
        trp0 = E.val("trp_" + tag) - 2 * nch
        _ln_marks[tag] = (E.val("ln_" + tag), tch)
        for ci, (c0, n) in enumerate(tch):
            E.wait("vector", "trp_" + tag, trp0 + 2 * (ci + 1))
            E.op("vector", lambda e, ci=ci, c0=c0, n=n: e.tensor_copy(
                rstd_rep[:, c0:c0 + n], pb[2 + ci][:, 0:n]))
            E.op("vector", lambda e, ci=ci, c0=c0, n=n: e.tensor_copy(
                mur_rep[:, c0:c0 + n], pb[5 + ci][:, 0:n]), sem="vrp_" + tag)
            for k in range(DT):
                E.op("vector", lambda e, k=k, c0=c0, n=n: e.tensor_tensor(
                    xlnt(k, c0, n), xlnt(k, c0, n), rstd_rep[:, c0:c0 + n],
                    ALU.mult), nowait=(k >= 1))
                E.op("vector", lambda e, k=k, c0=c0, n=n: e.tensor_tensor(
                    xlnt(k, c0, n), xlnt(k, c0, n), mur_rep[:, c0:c0 + n],
                    ALU.subtract), sem="ln_" + tag)

    def ffn_pass(tag, tok0, ntok, sub=8, lntag=None):
        tch = []
        c = tok0
        while c < tok0 + ntok:
            n = min(512, tok0 + ntok - c)
            tch.append((c, n))
            c += n
        nch = len(tch)
        tiles = ntok // P
        t0i = tok0 // P
        tv0 = E.val("tv_" + tag)
        ve0 = E.val("ve_" + tag)
        taq0 = E.val("taq_" + tag)
        vat0 = E.val("vat_" + tag)
        vrt0 = E.val("vrt_" + tag)
        vU0 = E.val("vU_" + tag)
        tb0 = E.val("tb_" + tag)
        if sub <= 0:
            for art in range(ART):
                dstA = wbufA if art % 2 == 0 else wbufB
                emit_strip(d_AlhsT, art * P, P, dstA, f"wA{art % 2}_" + tag)
            return
        gidx = 0
        for art in range(ART):
            dstA = wbufA if art % 2 == 0 else wbufB
            if art >= 2:
                E.wait("gpsimd", "tv_" + tag, tv0 + (art - 1) * nch)
            emit_strip(d_AlhsT, art * P, P, dstA, f"wA{art % 2}_" + tag)
            E.wait("tensor", f"wA{art % 2}_" + tag)
            for ci, (c0, n) in enumerate(tch):
                bank = pb[gidx % 4]
                if gidx >= 4:
                    E.wait("tensor", "ve_" + tag, ve0 + gidx - 3)
                for k in range(DT):
                    if art == 0 and lntag is not None:
                        E.wait("tensor", "ln_" + tag,
                               _ln_marks[lntag][0] + DT * ci + k + 1)
                    E.op("tensor", lambda e, dstA=dstA, k=k, c0=c0, n=n, bank=bank:
                         e.matmul(bank[:, 0:n], dstA[:, k * P:(k + 1) * P],
                                  xlnt(k, c0, n), start=(k == 0), stop=(k == DT - 1)),
                         sem=("tv_" + tag) if k == DT - 1 else None)
                E.wait("scalar", "tv_" + tag)
                E.op("scalar", lambda e, art=art, c0=c0, n=n, bank=bank: e.activation(
                    vecs[:, art * TW + c0:art * TW + c0 + n], bank[:, 0:n], AF.Copy),
                    sem="ve_" + tag, nowait=True)
                gidx += 1
        if sub <= 1:
            return
        E.wait("tensor", "ve_" + tag)
        for k in range(ART):
            E.wait("scalar", "ve_" + tag, ve0 + (k + 1) * nch)
            if k >= 2:
                E.wait("scalar", "taq_" + tag, taq0 + (k - 1) * nch)
            E.op("scalar", lambda e, k=k: e.activation(
                sqr[:, (k % 2) * TW + tok0:(k % 2) * TW + tok0 + ntok],
                vecs[:, k * TW + tok0:k * TW + tok0 + ntok], AF.Square),
                sem="sq_" + tag, nowait=True)
            E.wait("tensor", "sq_" + tag)
            for ci, (c0, n) in enumerate(tch):
                E.op("tensor", lambda e, k=k, ci=ci, c0=c0, n=n: e.matmul(
                    pb[2 + ci][0:NEXP, 0:n], selsq[:, k * NEXP:(k + 1) * NEXP],
                    sqr[:, (k % 2) * TW + c0 - tok0 + tok0:(k % 2) * TW + c0 - tok0 + tok0 + n],
                    start=(k == 0), stop=(k == ART - 1)), sem="taq_" + tag)
        E.wait("scalar", "taq_" + tag)
        for ci, (c0, n) in enumerate(tch):
            E.op("scalar", lambda e, ci=ci, c0=c0, n=n: e.activation(
                arr_fm[0:NEXP, c0:c0 + n], pb[2 + ci][0:NEXP, 0:n], AF.Copy),
                sem="va_" + tag)
        if sub <= 2:
            return
        E.wait("tensor", "va_" + tag)
        for i, ti in enumerate(range(t0i, t0i + tiles)):
            E.op("tensor", lambda e, i=i, ti=ti: e.transpose(
                pb[3].bitcast(BF16)[0:P, i * NEXP:(i + 1) * NEXP],
                arr_fm[0:NEXP, ti * P:(ti + 1) * P],
                ident[0:NEXP, 0:NEXP]),
                sem=("tt_" + tag) if i == tiles - 1 else None)
        E.wait("vector", "tt_" + tag)
        E.op("vector", lambda e: e.tensor_copy(
            arrT[:, t0i * NEXP:(t0i + tiles) * NEXP],
            pb[3].bitcast(BF16)[0:P, 0:tiles * NEXP]), sem="vat_" + tag)
        if sub <= 3:
            return
        E.wait("vector", "vat_" + tag)
        for ti in range(t0i, t0i + tiles):
            E.op("vector", lambda e, ti=ti: e.max(
                max8[:, ti * 8:(ti + 1) * 8], arrT[:, ti * NEXP:(ti + 1) * NEXP]))
        for ti in range(t0i, t0i + tiles):
            E.op("vector", lambda e, ti=ti: e.tensor_copy(
                thr[:, ti:ti + 1], max8[:, ti * 8 + 3:ti * 8 + 4]))
        for ti in range(t0i, t0i + tiles):
            E.op("vector", lambda e, ti=ti: e.tensor_scalar(
                routes[:, ti * NEXP:(ti + 1) * NEXP], arrT[:, ti * NEXP:(ti + 1) * NEXP],
                thr[:, ti:ti + 1], 0.25, ALU.is_ge, ALU.mult), sem="vro_" + tag)
        if sub <= 4:
            return
        E.wait("tensor", "vro_" + tag)
        E.wait("tensor", "vat_" + tag)
        for i, ti in enumerate(range(t0i, t0i + tiles)):
            bk, bo = (pb[3], i) if i < 8 else (pb[2], i - 8)
            E.op("tensor", lambda e, ti=ti, bk=bk, bo=bo: e.transpose(
                bk.bitcast(BF16)[0:NEXP, bo * P:(bo + 1) * P],
                routes[:, ti * NEXP:(ti + 1) * NEXP],
                ident[:]), sem=("tt2_" + tag) if i == tiles - 1 else None)
        E.wait("vector", "tt2_" + tag)
        E.op("vector", lambda e: e.tensor_copy(
            routesT[0:NEXP, t0i * P:(t0i + min(tiles, 8)) * P],
            pb[3].bitcast(BF16)[0:NEXP, 0:min(tiles, 8) * P]), sem="vrt_" + tag)
        if tiles > 8:
            E.op("vector", lambda e: e.tensor_copy(
                routesT[0:NEXP, (t0i + 8) * P:(t0i + tiles) * P],
                pb[2].bitcast(BF16)[0:NEXP, 0:(tiles - 8) * P]), sem="vrt_" + tag)
        if sub <= 5:
            return
        E.wait("tensor", "vrt_" + tag)
        vm0 = E.val("vm_" + tag)
        midx = 0
        for ci, (c0, n) in enumerate(tch):
            for art in range(ART):
                rb = pb[midx % 4]
                if midx >= 4:
                    E.wait("tensor", "vm_" + tag, vm0 + midx - 3)
                E.op("tensor", lambda e, art=art, c0=c0, n=n, rb=rb: e.matmul(
                    rb[:, 0:n], reprm[0:NEXP, art * P:(art + 1) * P],
                    routesT[0:NEXP, c0:c0 + n], start=True, stop=True), sem="tre_" + tag)
                E.wait("vector", "tre_" + tag)
                E.op("vector", lambda e, art=art, c0=c0, n=n, rb=rb: e.scalar_tensor_tensor(
                    masked[:, art * 512:art * 512 + n], rb[:, 0:n], 1.0,
                    vecs[:, art * TW + c0:art * TW + c0 + n], ALU.mult, ALU.mult),
                    sem="vm_" + tag, nowait=True)
                midx += 1
            E.wait("tensor", "vm_" + tag)
            E.wait("tensor", "vvs_" + tag)
            for k in range(ART):
                E.op("tensor", lambda e, k=k, c0=c0, n=n: e.matmul(
                    pb[4][0:NST, 0:n], selr[:, k * NST:(k + 1) * NST],
                    masked[:, k * 512:k * 512 + n],
                    start=(k == 0), stop=(k == ART - 1)),
                    sem=("tvv_" + tag) if k == ART - 1 else None)
            E.wait("scalar", "tvv_" + tag)
            E.op("scalar", lambda e, c0=c0, n=n: e.activation(
                vsm[0:NST, c0:c0 + n], pb[4][0:NST, 0:n], AF.Copy), sem="vvs_" + tag,
                nowait=True)
        if sub <= 6:
            return
        E.wait("tensor", "vvs_" + tag)
        vvr0 = E.val("vvr_" + tag)
        for art in range(ART):
            for ci, (c0, n) in enumerate(tch):
                idx = art * nch + ci
                sl = idx % 2
                vb = pb[4 + idx % 4]
                rb = pb[idx % 4]
                if idx >= 4:
                    E.wait("tensor", "vvr_" + tag, vvr0 + idx - 3)
                    E.wait("tensor", "vU_" + tag, vU0 + idx - 3)
                E.op("tensor", lambda e, art=art, c0=c0, n=n, vb=vb: e.matmul(
                    vb[:, 0:n], selrep[0:NST, art * P:(art + 1) * P],
                    vsm[0:NST, c0:c0 + n], start=True, stop=True))
                E.op("tensor", lambda e, art=art, c0=c0, n=n, rb=rb: e.matmul(
                    rb[:, 0:n], reprm[0:NEXP, art * P:(art + 1) * P],
                    routesT[0:NEXP, c0:c0 + n], start=True, stop=True), sem="tre2_" + tag)
                E.wait("scalar", "tre2_" + tag)
                if idx >= 2:
                    E.wait("scalar", "vU_" + tag, vU0 + idx - 1)
                E.op("scalar", lambda e, sl=sl, n=n, vb=vb: e.activation(
                    vrepr[:, sl * 512:sl * 512 + n], vb[:, 0:n], AF.Copy),
                    sem="vvr_" + tag, nowait=True)
                E.wait("vector", "vvr_" + tag, vvr0 + idx + 1)
                E.wait("vector", "tre2_" + tag)
                E.op("vector", lambda e, art=art, c0=c0, n=n, sl=sl, rb=rb:
                     e.scalar_tensor_tensor(
                         U[:, art * TW + c0:art * TW + c0 + n], rb[:, 0:n], 1.0,
                         vrepr[:, sl * 512:sl * 512 + n], ALU.mult, ALU.mult),
                     sem="vU_" + tag, nowait=True)
        if sub <= 7:
            return
        jidx = 0
        E.wait("tensor", "vU_" + tag)
        E.wait("tensor", "vvr_" + tag)
        E.wait("tensor", "dbias", 16)
        for j in range(JT):
            dstW = wbufC if j % 2 == 0 else wbufD
            dstU = ubufA if j % 2 == 0 else ubufB
            if j >= 2:
                E.wait("gpsimd", "tb_" + tag, tb0 + (j - 1) * nch)
            emit_strip(d_WT, j * P, P, dstW, f"wW{j % 2}_" + tag)
            emit_strip(d_BfT, j * P, P, dstU, f"wB{j % 2}_" + tag, rows_tiles=ART)
            E.wait("tensor", f"wW{j % 2}_" + tag)
            E.wait("tensor", f"wB{j % 2}_" + tag)
            if j >= 2:
                E.wait("tensor", "vres_" + tag, (j - 1) * nch)
            for ci, (c0, n) in enumerate(tch):
                bank = pb[4 + jidx % 4]
                jidx += 1
                E.wait("tensor", "vres_" + tag, max(0, E.val("vres_" + tag) - 3))
                for k in range(DT):
                    E.op("tensor", lambda e, dstW=dstW, k=k, c0=c0, n=n, bank=bank:
                         e.matmul(bank[:, 0:n], dstW[:, k * P:(k + 1) * P],
                                  xlnt(k, c0, n), start=(k == 0), stop=False))
                for k in range(ART):
                    E.op("tensor", lambda e, dstU=dstU, k=k, c0=c0, n=n, bank=bank:
                         e.matmul(bank[:, 0:n], dstU[:, k * P:(k + 1) * P],
                                  U[:, k * TW + c0:k * TW + c0 + n],
                                  start=False, stop=False))
                E.op("tensor", lambda e, j=j, c0=c0, n=n, bank=bank: e.matmul(
                    bank[:, 0:n], biasrow[0:1, j * P:(j + 1) * P],
                    onesc[0:1, 0:n], start=False, stop=True), sem="tb_" + tag)
                E.wait("vector", "tb_" + tag)
                E.op("vector", lambda e, j=j, c0=c0, n=n, bank=bank:
                     e.scalar_tensor_tensor(
                         xt(j, c0, n), bank[:, 0:n], 0.5, xt(j, c0, n),
                         ALU.mult, ALU.add), sem="vres_" + tag, nowait=True)

    def _mamba(E):
        E.wait("gpsimd", "tb_f1")
        E.wait("gpsimd", "tv_f1")
        E.op("gpsimd", lambda e: e.dma_start(
            out=wxa[:, 0:DT * R].rearrange("p (k w) -> p k w", k=DT),
            in_=d_WxTa.ap()[0:DT * P, :].rearrange("(k p) w -> p k w", p=P)),
            sem="dm", dma=True)
        E.op("gpsimd", lambda e: e.dma_start(
            out=wxb[:, 0:DT * 32].rearrange("p (k w) -> p k w", k=DT),
            in_=d_WxTb.ap()[0:DT * P, :].rearrange("(k p) w -> p k w", p=P)),
            sem="dm", dma=True)
        E.op("gpsimd", lambda e: e.dma_start(
            out=wdt_w[:, 0:DT * P], in_=d_WdT.ap()[:, 0:DT * P]),
            sem="dm", dma=True)
        dm_all = E.val("dm")

        E.wait("tensor", "dm", dm_all)
        for ci, (c0, n) in enumerate(TCH):
            if ci >= 1:
                E.wait("tensor", "vxz", 2 * ci)
            for k in range(DT):
                E.wait("tensor", "ln_l2", _ln_marks["l2"][0] + DT * ci + k + 1)
                E.op("tensor", lambda e, k=k, c0=c0, n=n: e.matmul(
                    pb[0][:, 0:n], wxa[:, k * R:(k + 1) * R], xlnt(k, c0, n),
                    start=(k == 0), stop=(k == DT - 1)))
            for k in range(DT):
                E.op("tensor", lambda e, k=k, c0=c0, n=n: e.matmul(
                    pb[1][0:32, 0:n], wxb[:, k * 32:(k + 1) * 32], xlnt(k, c0, n),
                    start=(k == 0), stop=(k == DT - 1)),
                    sem="txz" if k == DT - 1 else None)
            E.wait("vector", "txz")
            E.op("vector", lambda e, c0=c0, n=n: e.tensor_copy(xz_a[:, c0:c0 + n],
                                                               pb[0][:, 0:n]), sem="vxz")
            E.op("vector", lambda e, c0=c0, n=n: e.tensor_copy(xz_b[0:32, c0:c0 + n],
                                                               pb[1][0:32, 0:n]), sem="vxz")
        E.wait("sync", "tre")
        E.op("sync", lambda e: e.dma_start(out=vsm[0:NST, :], in_=d_bmask.ap()),
             sem="dxm", dma=True)
        E.wait("vector", "dxm", 16)
        E.op("vector", lambda e: e.tensor_tensor(
            xz_b[0:NST, :], xz_b[0:NST, :], vsm[0:NST, :], ALU.mult), sem="vxz")
        E.wait("gpsimd", "vxz")
        E.op("gpsimd", lambda e: e.dma_start(out=d_rows.ap(), in_=xz_b[:]),
             sem="gbd", dma=True)

        E.wait("scalar", "sA")
        vbu_cnt = 0
        vw_cnt = 0
        vh_cnt = 0
        bq_emitted = 0
        cp_emitted = 0
        td_cnt = 0
        sd_cnt = 0

        def bq_load(g):
            nonlocal bq_emitted
            q = g % 4
            buf = g % 2
            E.wait("gpsimd", "gbd", 16)
            if g >= 2:
                E.wait("gpsimd", "vbu", (g - 1) * 8)
            E.op("gpsimd", lambda e, q=q, buf=buf: e.dma_start(
                out=vecs[:, buf * 4 * TW:(buf + 1) * 4 * TW],
                in_=d_rows.ap()[4 * q:4 * q + 4, :].unsqueeze(0).broadcast_to((P, 4, TW))),
                sem=f"dBq{buf}", dma=True)
            bq_emitted += 1

        def cp_load(h):
            nonlocal cp_emitted
            p = h % 8
            dst = xbr if h % 2 == 0 else sqr
            E.wait("gpsimd", "gbd", 16)
            for r in range(2):
                if h >= 2:
                    E.wait("gpsimd", "vw", (h - 2) * 4 + 2 * (r + 1))
                E.op("gpsimd", lambda e, p=p, dst=dst, r=r: e.dma_start(
                    out=dst[:, r * TW:(r + 1) * TW].unsqueeze(1),
                    in_=d_rows.ap()[16 + 2 * p + r:16 + 2 * p + r + 1, :]
                    .unsqueeze(0).broadcast_to((P, 1, TW))),
                    sem=f"dCp{h % 2}r{r}", dma=True)
            cp_emitted += 1

        deltaf = mpool.bitcast(F32)
        tdi = 0

        def emit_td(dtb_t, lo=0, hi=6):
            nonlocal td_cnt
            items = [(k, c0, n) for k in (2 * dtb_t, 2 * dtb_t + 1)
                     for (c0, n) in TCH]
            for k, c0, n in items[lo:hi]:
                tdb = pb[2 + td_cnt % 2]
                if td_cnt > 1:
                    E.wait("tensor", "sd", td_cnt - 1)
                E.op("tensor", lambda e, k=k, c0=c0, n=n, tdb=tdb: e.matmul(
                    tdb[:, 0:n], wdt_w[:, k * P:(k + 1) * P], xz_a[:, c0:c0 + n],
                    start=True, stop=True), sem="td")
                td_cnt += 1

        E.wait("tensor", "vps_l2")
        E.wait("tensor", "vxz")
        bq_load(0)
        bq_load(1)
        cp_load(0)
        cp_load(1)

        emit_td(0)
        for dtb in range(8):
            dts = (2 * dtb, 2 * dtb + 1)
            if dtb >= 1:
                emit_td(dtb, lo=2)
            for pi, k in enumerate(dts):
                sl = pi
                if dtb >= 1 and pi == 0:
                    E.wait("scalar", "vdx", 2 * dtb)
                for ci, (c0, n) in enumerate(TCH):
                    tdi += 1
                    E.wait("scalar", "td", tdi)
                    tdb = pb[2 + (tdi - 1) % 2]
                    E.op("scalar", lambda e, k=k, sl=sl, c0=c0, n=n, tdb=tdb: e.activation(
                        deltaf[:, sl * TW + c0:sl * TW + c0 + n], tdb[:, 0:n], AF.Exp,
                        bias=bdt[:, k:k + 1]), nowait=True)
                    E.op("scalar", lambda e, sl=sl, c0=c0, n=n: e.activation(
                        deltaf[:, sl * TW + c0:sl * TW + c0 + n],
                        deltaf[:, sl * TW + c0:sl * TW + c0 + n], AF.Ln, bias=constv[:, 2:3]),
                        sem="sd")
                    sd_cnt += 1
                E.wait("vector", "sd", sd_cnt)
                dxoff = _dxt0 if sl == 0 else _dxt1
                E.op("vector", lambda e, sl=sl, k=k, dxoff=dxoff: e.tensor_tensor(
                    U[:, dxoff:dxoff + TW], deltaf[:, sl * TW:(sl + 1) * TW], xlnt(k),
                    ALU.mult), sem="vdx", nowait=True)
            banks = {0: (pb[4], pb[5]), 1: (pb[6], pb[7])}
            for pi, k in enumerate(dts):
                if dtb >= 2 and pi == 0:
                    E.wait("vector", "tns", 32 * (dtb - 1))
                E.op("vector", lambda e, pi=pi, k=k, dtb=dtb: e.tensor_scalar_mul(
                    mpool[:, 4608 + (dtb % 2) * 256 + pi * P:
                           4608 + (dtb % 2) * 256 + (pi + 1) * P],
                    ident[:], dcols[:, k:k + 1]), sem="vdg")
            if dtb >= 1:
                E.wait("tensor", "vy", dtb * 2)
            for ni in range(NST):
                q = ni // 4
                g = dtb * 4 + q
                h = dtb * 8 + ni // 2
                if ni % 2 == 0 and h + 2 < 64:
                    cp_load(h + 2)
                if ni % 4 == 1 and g + 2 < 32:
                    bq_load(g + 2)
                for pi, k in enumerate(dts):
                    step = (dtb * NST + ni) * 2 + pi
                    dab = (U[:, _dA0:_dA0 + TW], U[:, _dA1:_dA1 + TW],
                           wbufD[:, 0:TW])[step % 3]
                    if step >= 3:
                        E.wait("scalar", "vh", step - 2)
                    E.op("scalar", lambda e, k=k, ni=ni, dab=dab, pi=pi: e.activation(
                        dab, deltaf[:, pi * TW:(pi + 1) * TW], AF.Exp,
                        scale=acols[:, k * NST + ni:k * NST + ni + 1]), sem="sdA",
                        nowait=(step % 32 != 0))
                    dxoff = _dxt0 if pi == 0 else _dxt1
                    if ni % 4 == 0 and pi == 0:
                        E.wait("vector", f"dBq{g % 2}", 16 * (g // 2 + 1))
                    boff = (g % 2) * 4 * TW + (ni % 4) * TW
                    E.op("vector", lambda e, dxoff=dxoff, boff=boff: e.tensor_tensor(
                        U[:, _bu:_bu + TW], U[:, dxoff:dxoff + TW],
                        vecs[:, boff:boff + TW], ALU.mult), sem="vbu",
                        nowait=(step % 32 != 0))
                    vbu_cnt += 1
                    E.wait("vector", "sdA", step + 1)
                    E.op("vector", lambda e, dab=dab: e.tensor_tensor_scan(
                        U[:, _hh:_hh + TW], dab, U[:, _bu:_bu + TW],
                        0.0, ALU.mult, ALU.add), sem="vh")
                    vh_cnt += 1
                    wsl = step % 2
                    woff = _ww0 if wsl == 0 else _ww1
                    cbuf = xbr if h % 2 == 0 else sqr
                    coff = (ni % 2) * TW
                    if pi == 0:
                        E.wait("vector", f"dCp{h % 2}r{ni % 2}",
                               16 * (h // 2 + 1))
                    if step >= 2:
                        E.wait("vector", "tns", step - 1)
                    E.op("vector", lambda e, woff=woff, cbuf=cbuf, coff=coff: e.tensor_tensor(
                        U[:, woff:woff + TR], U[:, _hh + WU:_hh + TW],
                        cbuf[:, coff + WU:coff + TW], ALU.mult), sem="vw")
                    vw_cnt += 1
                    E.wait("tensor", "vw", vw_cnt)
                    b0, b1 = banks[pi]
                    E.op("tensor", lambda e, woff=woff, ni=ni, b0=b0: e.matmul(
                        b0[:, 0:512], ident[:], U[:, woff:woff + 512],
                        start=(ni == 0), stop=(ni == NST - 1)))
                    E.op("tensor", lambda e, woff=woff, ni=ni, b1=b1: e.matmul(
                        b1[:, 0:512], ident[:], U[:, woff + 512:woff + 1024],
                        start=(ni == 0), stop=(ni == NST - 1)), sem="tns")
                    if ni == 0:
                        E.wait("tensor", "vdg", E.val("vdg"))
                        dg = mpool[:, 4608 + (dtb % 2) * 256 + pi * P:
                                   4608 + (dtb % 2) * 256 + (pi + 1) * P]
                        E.op("tensor", lambda e, b0=b0, dg=dg, k=k: e.matmul(
                            b0[:, 0:512], dg, xlnt(k, WU, 512),
                            start=False, stop=False))
                        E.op("tensor", lambda e, b1=b1, dg=dg, k=k: e.matmul(
                            b1[:, 0:512], dg, xlnt(k, WU + 512, 512),
                            start=False, stop=False))
                    if ni == 15 and pi == 1 and dtb < 7:
                        emit_td(dtb + 1, hi=2)
            E.wait("vector", "tns", 32 * (dtb + 1))
            for pi, k in enumerate(dts):
                b0, b1 = banks[pi]
                E.op("vector", lambda e, k=k, b0=b0: e.scalar_tensor_tensor(
                    xt(k, WU, 512), b0[:, 0:512], 1.0, xt(k, WU, 512), ALU.mult, ALU.add),
                    nowait=True)
                E.op("vector", lambda e, k=k, b1=b1: e.scalar_tensor_tensor(
                    xt(k, WU + 512, 512), b1[:, 0:512], 1.0, xt(k, WU + 512, 512),
                    ALU.mult, ALU.add), sem="vy", nowait=True)

        E.wait("vector", "vy")

    E.wait("tensor", "dx", dx_all)
    if phases >= 1:
        ln_pass("l1", 0, TW, xdx=True)
    if phases >= 2:
        ffn_pass("f1", 0, TW, sub=ffn_sub, lntag="l1")
        if ffn_sub >= 8:
            E.wait("vector", "vres_f1")
    if phases >= 3:
        ln_pass("l2", 0, TW)

    if phases >= 4:
        _mamba(E)
    if phases >= 6:
        ln_pass("l3", WU, TR)
    if phases >= 7:
        E.wait("gpsimd", "txz")
        E.wait("gpsimd", "td", 48)
        vres_pre_f2 = E.val("vres_f2")
        ffn_pass("f2", WU, TR, lntag="l3")

    if phases == 2 and ffn_sub < 8:
        lastsem = {0: "wA0_f1", 1: "ve_f1", 2: "va_f1", 3: "vat_f1", 4: "vro_f1",
                   5: "vrt_f1", 6: "vvs_f1", 7: "vU_f1"}[ffn_sub]
    else:
        lastsem = {0: "dx", 1: "ln_l1", 2: "vres_f1", 3: "ln_l2", 4: "vxz",
                   5: "vy", 6: "ln_l3", 7: "vres_f2"}[min(phases, 7)]
    if out_xln:
        E.wait("sync", lastsem)
        for k in range(DT):
            E.op("sync", lambda e, k=k: e.dma_start(
                out=d_out.ap().bitcast(BF16)[k * P:(k + 1) * P, 0:TR],
                in_=xlnt(k, WU, TR)), sem="dout", dma=True)
    elif phases >= 7:
        for k in range(DT):
            E.wait("sync", "vres_f2", vres_pre_f2 + (k + 1) * 2)
            E.op("sync", lambda e, k=k: e.dma_start(
                out=d_out.ap()[k * P:(k + 1) * P, :], in_=xt(k, WU, TR)),
                sem="dout", dma=True)
    else:
        E.wait("sync", lastsem)
        E.op("sync", lambda e: e.dma_start(
            out=d_out.ap()[0:D, :].rearrange("(k p) t -> p k t", p=P),
            in_=x[:, 0:DT * TW].rearrange("p (k t) -> p k t", k=DT)[:, :, WU:WU + TR]),
            sem="dout", dma=True)
    E.wait("sync", "dout")

    with nc.Block() as blk:
        E.build(blk)
    return nc





_NC_CACHE = {}


def _get_nc():
    if "nc" not in _NC_CACHE:
        _NC_CACHE["nc"] = build_nc()
    return _NC_CACHE["nc"]


def _selsq():
    s = np.zeros((1024, NEXP), dtype=ml_dtypes.bfloat16)
    for i in range(1024):
        s[i, i // NST] = 1
    return s


def _repr_mat():
    s = np.zeros((NEXP, 1024), dtype=ml_dtypes.bfloat16)
    for j in range(1024):
        s[j // NST, j] = 1
    return s


def _selr():
    s = np.zeros((1024, NST), dtype=ml_dtypes.bfloat16)
    for i in range(1024):
        s[i, i % NST] = 1
    return s


def _selrep():
    s = np.zeros((NST, 1024), dtype=ml_dtypes.bfloat16)
    for j in range(1024):
        s[j % NST, j] = 1
    return s


def _host_inputs(inputs):
    x = np.asarray(inputs["L0_attn_x"])
    Wx = np.asarray(inputs["L0_attn_W_xproj"])
    Wd = np.asarray(inputs["L0_attn_W_dt"])
    bdt = np.asarray(inputs["L0_attn_b_dt"])
    Alog = np.asarray(inputs["L0_attn_A_log"])
    Dp = np.asarray(inputs["L0_attn_D_param"])
    A = np.asarray(inputs["L0_ffn_A"])
    B = np.asarray(inputs["L0_ffn_B"])
    W = np.asarray(inputs["L0_ffn_W"])
    bias = np.asarray(inputs["L0_ffn_bias"])

    shared = {
        "WxTa": np.ascontiguousarray(Wx[:R, :].T),
        "WxTb": np.ascontiguousarray(Wx[R:, :].T),
        "WdT": np.ascontiguousarray(Wd.T),
        "bdt_cols": np.ascontiguousarray(bdt.reshape(DT, P).T),
        "Alog_cols": np.ascontiguousarray(
            Alog.reshape(DT, P, NST).transpose(1, 0, 2).reshape(P, DT * NST)),
        "D_cols": np.ascontiguousarray(Dp.reshape(DT, P).T),
        "A_lhsT": np.ascontiguousarray(A.transpose(2, 0, 1).reshape(D, 1024)),
        "W_T": np.ascontiguousarray(W.T),
        "Bf_T": np.ascontiguousarray(B.transpose(0, 2, 1).reshape(1024, D)),
        "bias_row": np.ascontiguousarray(bias.reshape(1, D)),
        "identC": np.eye(P, dtype=ml_dtypes.bfloat16),
        "onesC": np.ones((P, 512), dtype=ml_dtypes.bfloat16),
        "selsqC": _selsq(),
        "reprC": _repr_mat(),
        "selrC": _selr(),
        "selrepC": _selrep(),
    }
    in_maps = []
    for c in range(8):
        b, q = c // 4, c % 4
        t0 = q * TR
        if q == 0:
            xs = np.concatenate([np.zeros((WU, D), np.float32), x[b, 0:TR]], axis=0)
        else:
            xs = x[b, t0 - WU:t0 + TR]
        m = dict(shared)
        m["x_fm"] = np.ascontiguousarray(xs.T)
        bm = np.ones((NST, TW), dtype=ml_dtypes.bfloat16)
        if q == 0:
            bm[:, 0:WU] = 0
        m["bmask"] = bm
        in_maps.append(m)
    return in_maps


def kernel(**inputs):
    in_maps = _host_inputs(inputs)
    nc = _get_nc()
    res = run_bass_kernel_spmd(nc, in_maps, core_ids=list(range(8)))
    out = np.empty((2, 4096, D), dtype=np.float32)
    for c in range(8):
        b, q = c // 4, c % 4
        out[b, q * TR:(q + 1) * TR, :] = res.results[c]["out_fm"].T
    return out


# version 25
# speedup vs baseline: 1.0016x; 1.0016x over previous
import numpy as np
import ml_dtypes
import concourse.bass as bass
import concourse.mybir as mybir
from concourse.bass_utils import run_bass_kernel_spmd

F32 = mybir.dt.float32
BF16 = mybir.dt.bfloat16
AF = mybir.ActivationFunctionType
ALU = mybir.AluOpType

P = 128
TW = 1152
TR = 1024
WU = 128
D = 2048
DT = 16
NST = 16
NEXP = 64
ART = 8
JT = 16
R = 128
TCH = [(0, 512), (512, 512), (1024, 128)]
TCHR = [(128, 512), (640, 512)]


class Plan:

    AUTOE = ("vector", "scalar")

    def __init__(self, nc):
        self.nc = nc
        self.ops = {e: [] for e in ("sync", "scalar", "vector", "tensor", "gpsimd")}
        self.sems = {}
        self.counts = {}
        self.auto = {e: 0 for e in self.AUTOE}
        self.autosem = {e: nc.alloc_semaphore("auto_" + e) for e in self.AUTOE}
        self.nameeng = {}
        self.namevals = {}

    _MERGE = {"vr1": "vr", "vr2": "vr", "vr3": "vr", "sr1": "sr", "sr2": "sr",
              "tt2": "tt", "tre2": "tre", "vc": "vmisc", "vps": "vmisc",
              "gb": "gx", "gbd": "gx2", "gcd": "gx3"}

    def canon(self, name):
        base = name.split("_")[0]
        return self._MERGE.get(base, base)

    def sem(self, name):
        name = self.canon(name)
        if name not in self.sems:
            self.sems[name] = self.nc.alloc_semaphore("s_" + name)
            self.counts[name] = 0
        return self.sems[name]

    def op(self, eng, opfn, sem=None, dma=False, nowait=False):
        if eng in self.AUTOE and not dma:
            idx = self.auto[eng] + 1
            self.auto[eng] = idx
            a = self.autosem[eng]
            if idx > 1 and not nowait:
                self.ops[eng].append(lambda e, a=a, v=idx - 1: e.wait_ge(a, v))
            if sem is not None:
                cn = self.canon(sem)
                self.counts.setdefault(cn, 0)
                self.counts[cn] += 1
                self.nameeng.setdefault(cn, eng)
                assert self.nameeng[cn] == eng, (cn, eng)
                self.namevals.setdefault(cn, []).append(idx)
            self.ops[eng].append(lambda e, opfn=opfn, a=a: opfn(e).then_inc(a, 1))
            return
        if sem is not None:
            s = self.sem(sem)
            cn = self.canon(sem)
            self.counts[cn] += 16 if dma else 1
            self.nameeng.setdefault(cn, None)
            self.ops[eng].append(lambda e, opfn=opfn, s=s, k=(16 if dma else 1):
                                 opfn(e).then_inc(s, k))
        else:
            self.ops[eng].append(lambda e, opfn=opfn: opfn(e))

    def wait(self, eng, sem, val=None):
        cn = self.canon(sem)
        self.counts.setdefault(cn, 0)
        v = self.counts[cn] if val is None else val
        if v <= 0:
            return

        def resolved(e, eng=eng, cn=cn, v=v):
            if self.nameeng.get(cn) in self.AUTOE:
                src_eng = self.nameeng[cn]
                vals = self.namevals[cn]
                assert v <= len(vals), (cn, v, len(vals))
                tgt = vals[v - 1]
                if eng == src_eng:
                    return
                e.wait_ge(self.autosem[src_eng], tgt)
            else:
                e.wait_ge(self.sem(cn), v)
        self.ops[eng].append(resolved)

    def val(self, sem):
        cn = self.canon(sem)
        self.counts.setdefault(cn, 0)
        return self.counts[cn]

    def build(self, blk):
        for eng_name, deco in (("sync", blk.sync), ("scalar", blk.scalar),
                               ("vector", blk.vector), ("tensor", blk.tensor),
                               ("gpsimd", blk.gpsimd)):
            ops = self.ops[eng_name]

            def section(e, ops=ops):
                for fn in ops:
                    fn(e)
            deco(section)


def build_nc(phases=7, out_xln=False, ffn_sub=8):
    nc = bass.Bass("TRN2", target_bir_lowering=False)

    d_x = nc.dram_tensor("x_fm", [D, TW], F32, kind="ExternalInput")
    d_WxTa = nc.dram_tensor("WxTa", [D, R], F32, kind="ExternalInput")
    d_WxTb = nc.dram_tensor("WxTb", [D, 32], F32, kind="ExternalInput")
    d_WdT = nc.dram_tensor("WdT", [R, D], F32, kind="ExternalInput")
    d_bdt = nc.dram_tensor("bdt_cols", [P, DT], F32, kind="ExternalInput")
    d_Alog = nc.dram_tensor("Alog_cols", [P, DT * NST], F32, kind="ExternalInput")
    d_Dp = nc.dram_tensor("D_cols", [P, DT], F32, kind="ExternalInput")
    d_AlhsT = nc.dram_tensor("A_lhsT", [D, 1024], F32, kind="ExternalInput")
    d_WT = nc.dram_tensor("W_T", [D, D], F32, kind="ExternalInput")
    d_BfT = nc.dram_tensor("Bf_T", [1024, D], F32, kind="ExternalInput")
    d_bias = nc.dram_tensor("bias_row", [1, D], F32, kind="ExternalInput")
    d_ident = nc.dram_tensor("identC", [P, P], BF16, kind="ExternalInput")
    d_ones = nc.dram_tensor("onesC", [P, 512], BF16, kind="ExternalInput")
    d_selsq = nc.dram_tensor("selsqC", [1024, NEXP], BF16, kind="ExternalInput")
    d_repr = nc.dram_tensor("reprC", [NEXP, 1024], BF16, kind="ExternalInput")
    d_selr = nc.dram_tensor("selrC", [1024, NST], BF16, kind="ExternalInput")
    d_selrep = nc.dram_tensor("selrepC", [NST, 1024], BF16, kind="ExternalInput")
    d_bmask = nc.dram_tensor("bmask", [NST, TW], BF16, kind="ExternalInput")
    d_out = nc.dram_tensor("out_fm", [D, TR], F32, kind="ExternalOutput")
    d_rows = nc.dram_tensor("rowsD", [32, TW], BF16)
    d_lnrows = nc.dram_tensor("lnrowsD", [2, TW], BF16)

    A = nc.alloc_sbuf_tensor
    x = A("x", [P, DT * TW], F32)
    xln = A("xln", [P, DT * TW], BF16)
    xbr = A("xbr", [P, 2 * TW], BF16)
    sqr = A("sqr", [P, 2 * TW], BF16)
    vecs = A("vecs", [P, ART * TW], BF16)
    U = A("U", [P, ART * TW], BF16)
    mpool = A("mpool", [P, ART * 512 + 1024], BF16)
    masked = mpool[:, 0:ART * 512]
    vrepr = mpool[:, ART * 512:ART * 512 + 1024]
    arrT = A("arrT", [P, 9 * NEXP], BF16)
    routes = A("routes", [P, 9 * NEXP], BF16)
    max8 = A("max8", [P, 9 * 8], BF16)
    thr = A("thr", [P, 16], F32)
    arr_fm = A("arr_fm", [NEXP, TW], BF16)
    vsm = A("vsm", [NST, TW], BF16)
    wbufA = A("wbufA", [P, DT * P], BF16)
    wbufB = A("wbufB", [P, DT * P], BF16)
    wbufC = A("wbufC", [P, DT * P], BF16)
    wbufD = A("wbufD", [P, DT * P], BF16)
    ubufA = A("ubufA", [P, ART * P], BF16)
    ubufB = A("ubufB", [P, ART * P], BF16)
    biasrow = A("biasrow", [1, D], BF16)
    xz_a = A("xz_a", [P, TW], BF16)
    xz_b = A("xz_b", [32, TW], BF16)
    bdt = A("bdt", [P, DT], F32)
    acols = A("acols", [P, DT * NST], F32)
    dcols = A("dcols", [P, DT], F32)
    ident = A("identS", [P, P], BF16)
    onesc = A("onesS", [P, 512], BF16)
    selsq = A("selsqS", [P, ART * NEXP], BF16)
    reprm = A("reprS", [NEXP, 1024], BF16)
    selr = A("selrS", [P, ART * NST], BF16)
    selrep = A("selrepS", [NST, 1024], BF16)
    constv = A("constv", [P, 4], F32)

    lnr = vecs[:, 0:4608].bitcast(F32)
    lnrb = vecs[:, 4608:4608 + 3 * TW]
    rstd_rep = U[:, 0:TW]
    routesT = arr_fm
    mur_rep = U[:, TW:2 * TW]
    wxa = wbufA
    wdt_w = wbufC
    wxb = ubufA
    _dA0, _dA1 = 0, TW
    _bu = 2 * TW
    _hh = 3 * TW
    _ww0, _ww1 = 4 * TW, 4 * TW + TR
    _dxt0 = 4 * TW + 2 * TR
    _dxt1 = _dxt0 + TW
    assert _dxt1 + TW <= ART * TW

    pb = [nc.alloc_psum_tensor(f"pb{i}", [P, 512], F32) for i in range(8)]

    E = Plan(nc)

    def xt(k, c0=0, n=TW):
        return x[:, k * TW + c0: k * TW + c0 + n]

    def xlnt(k, c0=0, n=TW):
        return xln[:, k * TW + c0: k * TW + c0 + n]

    E.op("sync", lambda e: e.dma_start(out=ident[:], in_=d_ident.ap()), sem="dx", dma=True)
    E.op("sync", lambda e: e.dma_start(out=onesc[:], in_=d_ones.ap()), sem="dx", dma=True)
    for k in range(ART):
        E.op("sync", lambda e, k=k: e.dma_start(
            out=selsq[:, k * NEXP:(k + 1) * NEXP], in_=d_selsq.ap()[k * P:(k + 1) * P, :]),
            sem="dx", dma=True)
        E.op("sync", lambda e, k=k: e.dma_start(
            out=selr[:, k * NST:(k + 1) * NST], in_=d_selr.ap()[k * P:(k + 1) * P, :]),
            sem="dx", dma=True)
    E.op("sync", lambda e: e.dma_start(out=reprm[:], in_=d_repr.ap()), sem="dx", dma=True)
    E.op("sync", lambda e: e.dma_start(out=selrep[:], in_=d_selrep.ap()), sem="dx", dma=True)
    for dst, src in [(bdt, d_bdt), (acols, d_Alog), (dcols, d_Dp)]:
        E.op("sync", lambda e, dst=dst, src=src: e.dma_start(out=dst[:], in_=src.ap()),
             sem="dx", dma=True)
    E.op("gpsimd", lambda e: e.dma_start(out=biasrow[:], in_=d_bias.ap()), sem="dbias", dma=True)
    for kg in range(8):
        E.op("sync", lambda e, kg=kg: e.dma_start(
            out=x[:, kg * 2 * TW:(kg + 1) * 2 * TW].rearrange(
                "p (k t) -> p k t", k=2),
            in_=d_x.ap()[kg * 2 * P:(kg + 1) * 2 * P, :].rearrange(
                "(k p) t -> p k t", p=P)),
            sem=f"dxg{kg}", dma=True)
    dx_all = E.val("dx")

    E.op("vector", lambda e: e.memset(constv[:, 0:1], 1e-5))
    E.op("vector", lambda e: e.memset(constv[:, 1:2], -0.5))
    E.op("vector", lambda e: e.memset(constv[:, 2:3], 1.0), sem="vcst")
    E.wait("scalar", "dx", dx_all)
    E.wait("scalar", "vcst")
    E.op("scalar", lambda e: e.activation(acols[:], acols[:], AF.Exp), sem="sAx")
    E.wait("vector", "sAx")
    E.op("vector", lambda e: e.tensor_scalar_mul(acols[:], acols[:], -1.0), sem="sA")

    def emit_strip(src, col0, width, dst, sem, rows_tiles=DT):
        E.op("gpsimd", lambda e, src=src, col0=col0, width=width, dst=dst,
             rt=rows_tiles: e.dma_start(
                 out=dst[:, 0:rt * width].rearrange("p (k w) -> p k w", k=rt),
                 in_=src.ap()[0:rt * P, col0:col0 + width].rearrange(
                     "(k p) w -> p k w", p=P)),
             sem=sem, dma=True)

    _ln_marks = {}
    def ln_pass(tag, tok0, ntok, xdx=False):
        tch = []
        c = tok0
        while c < tok0 + ntok:
            n = min(512, tok0 + ntok - c)
            tch.append((c, n))
            c += n
        nch = len(tch)
        tst0 = E.val("tst_" + tag)
        E.wait("vector", "tv")
        E.wait("vector", "tb")
        E.wait("vector", "txz")
        E.wait("vector", "td")
        for k in range(DT):
            sl = k % 2
            if xdx:
                E.wait("vector", f"dxg{k // 2}", 16)
            E.op("vector", lambda e, k=k: e.tensor_copy(
                xlnt(k, tok0, ntok), xt(k, tok0, ntok)), sem="vc_" + tag,
                nowait=(k >= 1))
            E.wait("scalar", "vc_" + tag)
            if k >= 2:
                E.wait("scalar", "tst_" + tag, tst0 + (k - 1) * 2 * nch)
            E.op("scalar", lambda e, k=k, sl=sl: e.activation(
                sqr[:, sl * TW:sl * TW + ntok], xlnt(k, tok0, ntok),
                AF.Square), sem="ss_" + tag)
            E.wait("tensor", "ss_" + tag)
            for ci, (c0, n) in enumerate(tch):
                E.op("tensor", lambda e, k=k, ci=ci, c0=c0, n=n: e.matmul(
                    pb[2 + ci][0:1, 0:n], onesc[0:P, 0:1],
                    xlnt(k, c0, n),
                    start=(k == 0), stop=(k == DT - 1)), sem="tst_" + tag)
                E.op("tensor", lambda e, k=k, sl=sl, ci=ci, c0=c0, n=n: e.matmul(
                    pb[5 + ci][0:1, 0:n], onesc[0:P, 0:1],
                    sqr[:, sl * TW + c0 - tok0:sl * TW + c0 - tok0 + n],
                    start=(k == 0), stop=(k == DT - 1)), sem="tst_" + tag)
        E.wait("vector", "tst_" + tag)
        for ci, (c0, n) in enumerate(tch):
            E.op("vector", lambda e, ci=ci, c0=c0, n=n: e.tensor_copy(
                lnr[0:1, c0:c0 + n], pb[2 + ci][0:1, 0:n]))
            E.op("vector", lambda e, ci=ci, c0=c0, n=n: e.tensor_copy(
                lnr[0:1, TW + c0:TW + c0 + n], pb[5 + ci][0:1, 0:n]), sem="vps_" + tag)
        mur = lnr[0:1, tok0:tok0 + ntok]
        varr = lnr[0:1, TW + tok0:TW + tok0 + ntok]
        mu2b = lnrb[0:1, tok0:tok0 + ntok]
        rstdb = lnrb[0:1, TW + tok0:TW + tok0 + ntok]
        murb = lnrb[0:1, 2 * TW + tok0:2 * TW + tok0 + ntok]
        E.op("vector", lambda e: e.tensor_scalar_mul(mur, mur, 1.0 / D))
        E.op("vector", lambda e: e.tensor_scalar_mul(varr, varr, 1.0 / D), sem="vr1_" + tag)
        E.wait("scalar", "vr1_" + tag)
        E.op("scalar", lambda e: e.activation(mu2b, mur, AF.Square), sem="sr1_" + tag)
        E.wait("vector", "sr1_" + tag)
        E.op("vector", lambda e: e.tensor_tensor(varr, varr, mu2b, ALU.subtract),
             sem="vr2_" + tag)
        E.wait("scalar", "vr2_" + tag)
        E.op("scalar", lambda e: e.activation(varr, varr, AF.Ln, bias=constv[0:1, 0:1]))
        E.op("scalar", lambda e: e.activation(varr, varr, AF.Exp, scale=constv[0:1, 1:2]),
             sem="sr2_" + tag)
        E.wait("vector", "sr2_" + tag)
        E.op("vector", lambda e: e.tensor_copy(rstdb, varr))
        E.op("vector", lambda e: e.tensor_tensor(mur, mur, varr, ALU.mult))
        E.op("vector", lambda e: e.tensor_copy(murb, mur), sem="vr3_" + tag)
        E.wait("tensor", "vr3_" + tag)
        for ci, (c0, n) in enumerate(tch):
            E.op("tensor", lambda e, ci=ci, c0=c0, n=n: e.matmul(
                pb[2 + ci][:, 0:n], onesc[0:1, 0:P],
                lnrb[0:1, TW + c0:TW + c0 + n], start=True, stop=True),
                sem="trp_" + tag)
            E.op("tensor", lambda e, ci=ci, c0=c0, n=n: e.matmul(
                pb[5 + ci][:, 0:n], onesc[0:1, 0:P],
                lnrb[0:1, 2 * TW + c0:2 * TW + c0 + n], start=True, stop=True),
                sem="trp_" + tag)
        trp0 = E.val("trp_" + tag) - 2 * nch
        _ln_marks[tag] = (E.val("ln_" + tag), tch)
        for ci, (c0, n) in enumerate(tch):
            E.wait("vector", "trp_" + tag, trp0 + 2 * (ci + 1))
            E.op("vector", lambda e, ci=ci, c0=c0, n=n: e.tensor_copy(
                rstd_rep[:, c0:c0 + n], pb[2 + ci][:, 0:n]))
            E.op("vector", lambda e, ci=ci, c0=c0, n=n: e.tensor_copy(
                mur_rep[:, c0:c0 + n], pb[5 + ci][:, 0:n]), sem="vrp_" + tag)
            for k in range(DT):
                E.op("vector", lambda e, k=k, c0=c0, n=n: e.tensor_tensor(
                    xlnt(k, c0, n), xlnt(k, c0, n), rstd_rep[:, c0:c0 + n],
                    ALU.mult), nowait=(k >= 1))
                E.op("vector", lambda e, k=k, c0=c0, n=n: e.tensor_tensor(
                    xlnt(k, c0, n), xlnt(k, c0, n), mur_rep[:, c0:c0 + n],
                    ALU.subtract), sem="ln_" + tag)

    def ffn_pass(tag, tok0, ntok, sub=8, lntag=None):
        tch = []
        c = tok0
        while c < tok0 + ntok:
            n = min(512, tok0 + ntok - c)
            tch.append((c, n))
            c += n
        nch = len(tch)
        tiles = ntok // P
        t0i = tok0 // P
        tv0 = E.val("tv_" + tag)
        ve0 = E.val("ve_" + tag)
        taq0 = E.val("taq_" + tag)
        vat0 = E.val("vat_" + tag)
        vrt0 = E.val("vrt_" + tag)
        vU0 = E.val("vU_" + tag)
        tb0 = E.val("tb_" + tag)
        if sub <= 0:
            for art in range(ART):
                dstA = wbufA if art % 2 == 0 else wbufB
                emit_strip(d_AlhsT, art * P, P, dstA, f"wA{art % 2}_" + tag)
            return
        gidx = 0
        for art in range(ART):
            dstA = wbufA if art % 2 == 0 else wbufB
            if art >= 2:
                E.wait("gpsimd", "tv_" + tag, tv0 + (art - 1) * nch)
            emit_strip(d_AlhsT, art * P, P, dstA, f"wA{art % 2}_" + tag)
            E.wait("tensor", f"wA{art % 2}_" + tag)
            for ci, (c0, n) in enumerate(tch):
                bank = pb[gidx % 4]
                if gidx >= 4:
                    E.wait("tensor", "ve_" + tag, ve0 + gidx - 3)
                for k in range(DT):
                    if art == 0 and lntag is not None:
                        E.wait("tensor", "ln_" + tag,
                               _ln_marks[lntag][0] + DT * ci + k + 1)
                    E.op("tensor", lambda e, dstA=dstA, k=k, c0=c0, n=n, bank=bank:
                         e.matmul(bank[:, 0:n], dstA[:, k * P:(k + 1) * P],
                                  xlnt(k, c0, n), start=(k == 0), stop=(k == DT - 1)),
                         sem=("tv_" + tag) if k == DT - 1 else None)
                E.wait("scalar", "tv_" + tag)
                E.op("scalar", lambda e, art=art, c0=c0, n=n, bank=bank: e.activation(
                    vecs[:, art * TW + c0:art * TW + c0 + n], bank[:, 0:n], AF.Copy),
                    sem="ve_" + tag, nowait=True)
                gidx += 1
        if sub <= 1:
            return
        E.wait("tensor", "ve_" + tag)
        for k in range(ART):
            E.wait("scalar", "ve_" + tag, ve0 + (k + 1) * nch)
            if k >= 2:
                E.wait("scalar", "taq_" + tag, taq0 + (k - 1) * nch)
            E.op("scalar", lambda e, k=k: e.activation(
                sqr[:, (k % 2) * TW + tok0:(k % 2) * TW + tok0 + ntok],
                vecs[:, k * TW + tok0:k * TW + tok0 + ntok], AF.Square),
                sem="sq_" + tag, nowait=True)
            E.wait("tensor", "sq_" + tag)
            for ci, (c0, n) in enumerate(tch):
                E.op("tensor", lambda e, k=k, ci=ci, c0=c0, n=n: e.matmul(
                    pb[2 + ci][0:NEXP, 0:n], selsq[:, k * NEXP:(k + 1) * NEXP],
                    sqr[:, (k % 2) * TW + c0 - tok0 + tok0:(k % 2) * TW + c0 - tok0 + tok0 + n],
                    start=(k == 0), stop=(k == ART - 1)), sem="taq_" + tag)
        E.wait("scalar", "taq_" + tag)
        for ci, (c0, n) in enumerate(tch):
            E.op("scalar", lambda e, ci=ci, c0=c0, n=n: e.activation(
                arr_fm[0:NEXP, c0:c0 + n], pb[2 + ci][0:NEXP, 0:n], AF.Copy),
                sem="va_" + tag)
        if sub <= 2:
            return
        E.wait("tensor", "va_" + tag)
        for i, ti in enumerate(range(t0i, t0i + tiles)):
            E.op("tensor", lambda e, i=i, ti=ti: e.transpose(
                pb[3].bitcast(BF16)[0:P, i * NEXP:(i + 1) * NEXP],
                arr_fm[0:NEXP, ti * P:(ti + 1) * P],
                ident[0:NEXP, 0:NEXP]),
                sem=("tt_" + tag) if i == tiles - 1 else None)
        E.wait("vector", "tt_" + tag)
        E.op("vector", lambda e: e.tensor_copy(
            arrT[:, t0i * NEXP:(t0i + tiles) * NEXP],
            pb[3].bitcast(BF16)[0:P, 0:tiles * NEXP]), sem="vat_" + tag)
        if sub <= 3:
            return
        E.wait("vector", "vat_" + tag)
        for ti in range(t0i, t0i + tiles):
            E.op("vector", lambda e, ti=ti: e.max(
                max8[:, ti * 8:(ti + 1) * 8], arrT[:, ti * NEXP:(ti + 1) * NEXP]))
        for ti in range(t0i, t0i + tiles):
            E.op("vector", lambda e, ti=ti: e.tensor_copy(
                thr[:, ti:ti + 1], max8[:, ti * 8 + 3:ti * 8 + 4]))
        for ti in range(t0i, t0i + tiles):
            E.op("vector", lambda e, ti=ti: e.tensor_scalar(
                routes[:, ti * NEXP:(ti + 1) * NEXP], arrT[:, ti * NEXP:(ti + 1) * NEXP],
                thr[:, ti:ti + 1], 0.25, ALU.is_ge, ALU.mult), sem="vro_" + tag)
        if sub <= 4:
            return
        E.wait("tensor", "vro_" + tag)
        E.wait("tensor", "vat_" + tag)
        for i, ti in enumerate(range(t0i, t0i + tiles)):
            bk, bo = (pb[3], i) if i < 8 else (pb[2], i - 8)
            E.op("tensor", lambda e, ti=ti, bk=bk, bo=bo: e.transpose(
                bk.bitcast(BF16)[0:NEXP, bo * P:(bo + 1) * P],
                routes[:, ti * NEXP:(ti + 1) * NEXP],
                ident[:]), sem=("tt2_" + tag) if i == tiles - 1 else None)
        E.wait("vector", "tt2_" + tag)
        E.op("vector", lambda e: e.tensor_copy(
            routesT[0:NEXP, t0i * P:(t0i + min(tiles, 8)) * P],
            pb[3].bitcast(BF16)[0:NEXP, 0:min(tiles, 8) * P]), sem="vrt_" + tag)
        if tiles > 8:
            E.op("vector", lambda e: e.tensor_copy(
                routesT[0:NEXP, (t0i + 8) * P:(t0i + tiles) * P],
                pb[2].bitcast(BF16)[0:NEXP, 0:(tiles - 8) * P]), sem="vrt_" + tag)
        if sub <= 5:
            return
        E.wait("tensor", "vrt_" + tag)
        vm0 = E.val("vm_" + tag)
        midx = 0
        for ci, (c0, n) in enumerate(tch):
            for art in range(ART):
                rb = pb[midx % 4]
                if midx >= 4:
                    E.wait("tensor", "vm_" + tag, vm0 + midx - 3)
                E.op("tensor", lambda e, art=art, c0=c0, n=n, rb=rb: e.matmul(
                    rb[:, 0:n], reprm[0:NEXP, art * P:(art + 1) * P],
                    routesT[0:NEXP, c0:c0 + n], start=True, stop=True), sem="tre_" + tag)
                E.wait("vector", "tre_" + tag)
                E.op("vector", lambda e, art=art, c0=c0, n=n, rb=rb: e.scalar_tensor_tensor(
                    masked[:, art * 512:art * 512 + n], rb[:, 0:n], 1.0,
                    vecs[:, art * TW + c0:art * TW + c0 + n], ALU.mult, ALU.mult),
                    sem="vm_" + tag, nowait=True)
                midx += 1
            E.wait("tensor", "vm_" + tag)
            E.wait("tensor", "vvs_" + tag)
            for k in range(ART):
                E.op("tensor", lambda e, k=k, c0=c0, n=n: e.matmul(
                    pb[4][0:NST, 0:n], selr[:, k * NST:(k + 1) * NST],
                    masked[:, k * 512:k * 512 + n],
                    start=(k == 0), stop=(k == ART - 1)),
                    sem=("tvv_" + tag) if k == ART - 1 else None)
            E.wait("scalar", "tvv_" + tag)
            E.op("scalar", lambda e, c0=c0, n=n: e.activation(
                vsm[0:NST, c0:c0 + n], pb[4][0:NST, 0:n], AF.Copy), sem="vvs_" + tag,
                nowait=True)
        if sub <= 6:
            return
        E.wait("tensor", "vvs_" + tag)
        vvr0 = E.val("vvr_" + tag)
        vslot = (vrepr[:, 0:512], vrepr[:, 512:1024], masked[:, 0:512])
        for art in range(ART):
            for ci, (c0, n) in enumerate(tch):
                idx = art * nch + ci
                sl = idx % 3
                vb = pb[4 + idx % 4]
                rb = pb[idx % 4]
                if idx >= 4:
                    E.wait("tensor", "vvr_" + tag, vvr0 + idx - 3)
                    E.wait("tensor", "vU_" + tag, vU0 + idx - 3)
                E.op("tensor", lambda e, art=art, c0=c0, n=n, vb=vb: e.matmul(
                    vb[:, 0:n], selrep[0:NST, art * P:(art + 1) * P],
                    vsm[0:NST, c0:c0 + n], start=True, stop=True))
                E.op("tensor", lambda e, art=art, c0=c0, n=n, rb=rb: e.matmul(
                    rb[:, 0:n], reprm[0:NEXP, art * P:(art + 1) * P],
                    routesT[0:NEXP, c0:c0 + n], start=True, stop=True), sem="tre2_" + tag)
                E.wait("scalar", "tre2_" + tag)
                if idx >= 3:
                    E.wait("scalar", "vU_" + tag, vU0 + idx - 2)
                E.op("scalar", lambda e, sl=sl, n=n, vb=vb: e.activation(
                    vslot[sl][:, 0:n], vb[:, 0:n], AF.Copy),
                    sem="vvr_" + tag, nowait=True)
                E.wait("vector", "vvr_" + tag, vvr0 + idx + 1)
                E.wait("vector", "tre2_" + tag)
                E.op("vector", lambda e, art=art, c0=c0, n=n, sl=sl, rb=rb:
                     e.scalar_tensor_tensor(
                         U[:, art * TW + c0:art * TW + c0 + n], rb[:, 0:n], 1.0,
                         vslot[sl][:, 0:n], ALU.mult, ALU.mult),
                     sem="vU_" + tag, nowait=True)
        if sub <= 7:
            return
        jidx = 0
        E.wait("tensor", "vU_" + tag)
        E.wait("tensor", "vvr_" + tag)
        E.wait("tensor", "dbias", 16)
        for j in range(JT):
            dstW = wbufC if j % 2 == 0 else wbufD
            dstU = ubufA if j % 2 == 0 else ubufB
            if j >= 2:
                E.wait("gpsimd", "tb_" + tag, tb0 + (j - 1) * nch)
            emit_strip(d_WT, j * P, P, dstW, f"wW{j % 2}_" + tag)
            emit_strip(d_BfT, j * P, P, dstU, f"wB{j % 2}_" + tag, rows_tiles=ART)
            E.wait("tensor", f"wW{j % 2}_" + tag)
            E.wait("tensor", f"wB{j % 2}_" + tag)
            if j >= 2:
                E.wait("tensor", "vres_" + tag, (j - 1) * nch)
            for ci, (c0, n) in enumerate(tch):
                bank = pb[4 + jidx % 4]
                jidx += 1
                E.wait("tensor", "vres_" + tag, max(0, E.val("vres_" + tag) - 3))
                for k in range(DT):
                    E.op("tensor", lambda e, dstW=dstW, k=k, c0=c0, n=n, bank=bank:
                         e.matmul(bank[:, 0:n], dstW[:, k * P:(k + 1) * P],
                                  xlnt(k, c0, n), start=(k == 0), stop=False))
                for k in range(ART):
                    E.op("tensor", lambda e, dstU=dstU, k=k, c0=c0, n=n, bank=bank:
                         e.matmul(bank[:, 0:n], dstU[:, k * P:(k + 1) * P],
                                  U[:, k * TW + c0:k * TW + c0 + n],
                                  start=False, stop=False))
                E.op("tensor", lambda e, j=j, c0=c0, n=n, bank=bank: e.matmul(
                    bank[:, 0:n], biasrow[0:1, j * P:(j + 1) * P],
                    onesc[0:1, 0:n], start=False, stop=True), sem="tb_" + tag)
                E.wait("vector", "tb_" + tag)
                E.op("vector", lambda e, j=j, c0=c0, n=n, bank=bank:
                     e.scalar_tensor_tensor(
                         xt(j, c0, n), bank[:, 0:n], 0.5, xt(j, c0, n),
                         ALU.mult, ALU.add), sem="vres_" + tag, nowait=True)

    def _mamba(E):
        E.wait("gpsimd", "tb_f1")
        E.wait("gpsimd", "tv_f1")
        E.op("gpsimd", lambda e: e.dma_start(
            out=wxa[:, 0:DT * R].rearrange("p (k w) -> p k w", k=DT),
            in_=d_WxTa.ap()[0:DT * P, :].rearrange("(k p) w -> p k w", p=P)),
            sem="dm", dma=True)
        E.op("gpsimd", lambda e: e.dma_start(
            out=wxb[:, 0:DT * 32].rearrange("p (k w) -> p k w", k=DT),
            in_=d_WxTb.ap()[0:DT * P, :].rearrange("(k p) w -> p k w", p=P)),
            sem="dm", dma=True)
        E.op("gpsimd", lambda e: e.dma_start(
            out=wdt_w[:, 0:DT * P], in_=d_WdT.ap()[:, 0:DT * P]),
            sem="dm", dma=True)
        dm_all = E.val("dm")

        E.wait("tensor", "dm", dm_all)
        for ci, (c0, n) in enumerate(TCH):
            if ci >= 1:
                E.wait("tensor", "vxz", 2 * ci)
            for k in range(DT):
                E.wait("tensor", "ln_l2", _ln_marks["l2"][0] + DT * ci + k + 1)
                E.op("tensor", lambda e, k=k, c0=c0, n=n: e.matmul(
                    pb[0][:, 0:n], wxa[:, k * R:(k + 1) * R], xlnt(k, c0, n),
                    start=(k == 0), stop=(k == DT - 1)))
            for k in range(DT):
                E.op("tensor", lambda e, k=k, c0=c0, n=n: e.matmul(
                    pb[1][0:32, 0:n], wxb[:, k * 32:(k + 1) * 32], xlnt(k, c0, n),
                    start=(k == 0), stop=(k == DT - 1)),
                    sem="txz" if k == DT - 1 else None)
            E.wait("vector", "txz")
            E.op("vector", lambda e, c0=c0, n=n: e.tensor_copy(xz_a[:, c0:c0 + n],
                                                               pb[0][:, 0:n]), sem="vxz")
            E.op("vector", lambda e, c0=c0, n=n: e.tensor_copy(xz_b[0:32, c0:c0 + n],
                                                               pb[1][0:32, 0:n]), sem="vxz")
        E.wait("sync", "tre")
        E.op("sync", lambda e: e.dma_start(out=vsm[0:NST, :], in_=d_bmask.ap()),
             sem="dxm", dma=True)
        E.wait("vector", "dxm", 16)
        E.op("vector", lambda e: e.tensor_tensor(
            xz_b[0:NST, :], xz_b[0:NST, :], vsm[0:NST, :], ALU.mult), sem="vxz")
        E.wait("gpsimd", "vxz")
        E.op("gpsimd", lambda e: e.dma_start(out=d_rows.ap(), in_=xz_b[:]),
             sem="gbd", dma=True)

        E.wait("scalar", "sA")
        vbu_cnt = 0
        vw_cnt = 0
        vh_cnt = 0
        bq_emitted = 0
        cp_emitted = 0
        td_cnt = 0
        sd_cnt = 0

        def bq_load(g):
            nonlocal bq_emitted
            q = g % 4
            buf = g % 2
            E.wait("gpsimd", "gbd", 16)
            if g >= 2:
                E.wait("gpsimd", "vbu", (g - 1) * 8)
            E.op("gpsimd", lambda e, q=q, buf=buf: e.dma_start(
                out=vecs[:, buf * 4 * TW:(buf + 1) * 4 * TW],
                in_=d_rows.ap()[4 * q:4 * q + 4, :].unsqueeze(0).broadcast_to((P, 4, TW))),
                sem=f"dBq{buf}", dma=True)
            bq_emitted += 1

        def cp_load(h):
            nonlocal cp_emitted
            p = h % 8
            dst = xbr if h % 2 == 0 else sqr
            E.wait("gpsimd", "gbd", 16)
            for r in range(2):
                if h >= 2:
                    E.wait("gpsimd", "vw", (h - 2) * 4 + 2 * (r + 1))
                E.op("gpsimd", lambda e, p=p, dst=dst, r=r: e.dma_start(
                    out=dst[:, r * TW:(r + 1) * TW].unsqueeze(1),
                    in_=d_rows.ap()[16 + 2 * p + r:16 + 2 * p + r + 1, :]
                    .unsqueeze(0).broadcast_to((P, 1, TW))),
                    sem=f"dCp{h % 2}r{r}", dma=True)
            cp_emitted += 1

        deltaf = mpool.bitcast(F32)
        tdi = 0

        def emit_td(dtb_t, lo=0, hi=6):
            nonlocal td_cnt
            items = [(k, c0, n) for k in (2 * dtb_t, 2 * dtb_t + 1)
                     for (c0, n) in TCH]
            for k, c0, n in items[lo:hi]:
                tdb = pb[2 + td_cnt % 2]
                if td_cnt > 1:
                    E.wait("tensor", "sd", td_cnt - 1)
                E.op("tensor", lambda e, k=k, c0=c0, n=n, tdb=tdb: e.matmul(
                    tdb[:, 0:n], wdt_w[:, k * P:(k + 1) * P], xz_a[:, c0:c0 + n],
                    start=True, stop=True), sem="td")
                td_cnt += 1

        E.wait("tensor", "vps_l2")
        E.wait("tensor", "vxz")
        bq_load(0)
        bq_load(1)
        cp_load(0)
        cp_load(1)

        emit_td(0)
        for dtb in range(8):
            dts = (2 * dtb, 2 * dtb + 1)
            if dtb >= 1:
                emit_td(dtb, lo=2)
            for pi, k in enumerate(dts):
                sl = pi
                if dtb >= 1 and pi == 0:
                    E.wait("scalar", "vdx", 2 * dtb)
                for ci, (c0, n) in enumerate(TCH):
                    tdi += 1
                    E.wait("scalar", "td", tdi)
                    tdb = pb[2 + (tdi - 1) % 2]
                    E.op("scalar", lambda e, k=k, sl=sl, c0=c0, n=n, tdb=tdb: e.activation(
                        deltaf[:, sl * TW + c0:sl * TW + c0 + n], tdb[:, 0:n], AF.Exp,
                        bias=bdt[:, k:k + 1]), nowait=True)
                    E.op("scalar", lambda e, sl=sl, c0=c0, n=n: e.activation(
                        deltaf[:, sl * TW + c0:sl * TW + c0 + n],
                        deltaf[:, sl * TW + c0:sl * TW + c0 + n], AF.Ln, bias=constv[:, 2:3]),
                        sem="sd")
                    sd_cnt += 1
                E.wait("vector", "sd", sd_cnt)
                dxoff = _dxt0 if sl == 0 else _dxt1
                E.op("vector", lambda e, sl=sl, k=k, dxoff=dxoff: e.tensor_tensor(
                    U[:, dxoff:dxoff + TW], deltaf[:, sl * TW:(sl + 1) * TW], xlnt(k),
                    ALU.mult), sem="vdx", nowait=True)
            banks = {0: (pb[4], pb[5]), 1: (pb[6], pb[7])}
            for pi, k in enumerate(dts):
                if dtb >= 2 and pi == 0:
                    E.wait("vector", "tns", 32 * (dtb - 1))
                E.op("vector", lambda e, pi=pi, k=k, dtb=dtb: e.tensor_scalar_mul(
                    mpool[:, 4608 + (dtb % 2) * 256 + pi * P:
                           4608 + (dtb % 2) * 256 + (pi + 1) * P],
                    ident[:], dcols[:, k:k + 1]), sem="vdg")
            if dtb >= 1:
                E.wait("tensor", "vy", dtb * 2)
            for ni in range(NST):
                q = ni // 4
                g = dtb * 4 + q
                h = dtb * 8 + ni // 2
                if ni % 2 == 0 and h + 2 < 64:
                    cp_load(h + 2)
                if ni % 4 == 1 and g + 2 < 32:
                    bq_load(g + 2)
                for pi, k in enumerate(dts):
                    step = (dtb * NST + ni) * 2 + pi
                    dab = (U[:, _dA0:_dA0 + TW], U[:, _dA1:_dA1 + TW],
                           wbufD[:, 0:TW])[step % 3]
                    if step >= 3:
                        E.wait("scalar", "vh", step - 2)
                    E.op("scalar", lambda e, k=k, ni=ni, dab=dab, pi=pi: e.activation(
                        dab, deltaf[:, pi * TW:(pi + 1) * TW], AF.Exp,
                        scale=acols[:, k * NST + ni:k * NST + ni + 1]), sem="sdA",
                        nowait=(step % 32 != 0))
                    dxoff = _dxt0 if pi == 0 else _dxt1
                    if ni % 4 == 0 and pi == 0:
                        E.wait("vector", f"dBq{g % 2}", 16 * (g // 2 + 1))
                    boff = (g % 2) * 4 * TW + (ni % 4) * TW
                    E.op("vector", lambda e, dxoff=dxoff, boff=boff: e.tensor_tensor(
                        U[:, _bu:_bu + TW], U[:, dxoff:dxoff + TW],
                        vecs[:, boff:boff + TW], ALU.mult), sem="vbu",
                        nowait=(step % 32 != 0))
                    vbu_cnt += 1
                    E.wait("vector", "sdA", step + 1)
                    E.op("vector", lambda e, dab=dab: e.tensor_tensor_scan(
                        U[:, _hh:_hh + TW], dab, U[:, _bu:_bu + TW],
                        0.0, ALU.mult, ALU.add), sem="vh")
                    vh_cnt += 1
                    wsl = step % 2
                    woff = _ww0 if wsl == 0 else _ww1
                    cbuf = xbr if h % 2 == 0 else sqr
                    coff = (ni % 2) * TW
                    if pi == 0:
                        E.wait("vector", f"dCp{h % 2}r{ni % 2}",
                               16 * (h // 2 + 1))
                    if step >= 2:
                        E.wait("vector", "tns", step - 1)
                    E.op("vector", lambda e, woff=woff, cbuf=cbuf, coff=coff: e.tensor_tensor(
                        U[:, woff:woff + TR], U[:, _hh + WU:_hh + TW],
                        cbuf[:, coff + WU:coff + TW], ALU.mult), sem="vw")
                    vw_cnt += 1
                    E.wait("tensor", "vw", vw_cnt)
                    b0, b1 = banks[pi]
                    E.op("tensor", lambda e, woff=woff, ni=ni, b0=b0: e.matmul(
                        b0[:, 0:512], ident[:], U[:, woff:woff + 512],
                        start=(ni == 0), stop=(ni == NST - 1)))
                    E.op("tensor", lambda e, woff=woff, ni=ni, b1=b1: e.matmul(
                        b1[:, 0:512], ident[:], U[:, woff + 512:woff + 1024],
                        start=(ni == 0), stop=(ni == NST - 1)), sem="tns")
                    if ni == 0:
                        E.wait("tensor", "vdg", E.val("vdg"))
                        dg = mpool[:, 4608 + (dtb % 2) * 256 + pi * P:
                                   4608 + (dtb % 2) * 256 + (pi + 1) * P]
                        E.op("tensor", lambda e, b0=b0, dg=dg, k=k: e.matmul(
                            b0[:, 0:512], dg, xlnt(k, WU, 512),
                            start=False, stop=False))
                        E.op("tensor", lambda e, b1=b1, dg=dg, k=k: e.matmul(
                            b1[:, 0:512], dg, xlnt(k, WU + 512, 512),
                            start=False, stop=False))
                    if ni == 15 and pi == 1 and dtb < 7:
                        emit_td(dtb + 1, hi=2)
            E.wait("vector", "tns", 32 * (dtb + 1))
            for pi, k in enumerate(dts):
                b0, b1 = banks[pi]
                E.op("vector", lambda e, k=k, b0=b0: e.scalar_tensor_tensor(
                    xt(k, WU, 512), b0[:, 0:512], 1.0, xt(k, WU, 512), ALU.mult, ALU.add),
                    nowait=True)
                E.op("vector", lambda e, k=k, b1=b1: e.scalar_tensor_tensor(
                    xt(k, WU + 512, 512), b1[:, 0:512], 1.0, xt(k, WU + 512, 512),
                    ALU.mult, ALU.add), sem="vy", nowait=True)

        E.wait("vector", "vy")

    E.wait("tensor", "dx", dx_all)
    if phases >= 1:
        ln_pass("l1", 0, TW, xdx=True)
    if phases >= 2:
        ffn_pass("f1", 0, TW, sub=ffn_sub, lntag="l1")
        if ffn_sub >= 8:
            E.wait("vector", "vres_f1")
    if phases >= 3:
        ln_pass("l2", 0, TW)

    if phases >= 4:
        _mamba(E)
    if phases >= 6:
        ln_pass("l3", WU, TR)
    if phases >= 7:
        E.wait("gpsimd", "txz")
        E.wait("gpsimd", "td", 48)
        vres_pre_f2 = E.val("vres_f2")
        ffn_pass("f2", WU, TR, lntag="l3")

    if phases == 2 and ffn_sub < 8:
        lastsem = {0: "wA0_f1", 1: "ve_f1", 2: "va_f1", 3: "vat_f1", 4: "vro_f1",
                   5: "vrt_f1", 6: "vvs_f1", 7: "vU_f1"}[ffn_sub]
    else:
        lastsem = {0: "dx", 1: "ln_l1", 2: "vres_f1", 3: "ln_l2", 4: "vxz",
                   5: "vy", 6: "ln_l3", 7: "vres_f2"}[min(phases, 7)]
    if out_xln:
        E.wait("sync", lastsem)
        for k in range(DT):
            E.op("sync", lambda e, k=k: e.dma_start(
                out=d_out.ap().bitcast(BF16)[k * P:(k + 1) * P, 0:TR],
                in_=xlnt(k, WU, TR)), sem="dout", dma=True)
    elif phases >= 7:
        for k in range(DT):
            E.wait("sync", "vres_f2", vres_pre_f2 + (k + 1) * 2)
            E.op("sync", lambda e, k=k: e.dma_start(
                out=d_out.ap()[k * P:(k + 1) * P, :], in_=xt(k, WU, TR)),
                sem="dout", dma=True)
    else:
        E.wait("sync", lastsem)
        E.op("sync", lambda e: e.dma_start(
            out=d_out.ap()[0:D, :].rearrange("(k p) t -> p k t", p=P),
            in_=x[:, 0:DT * TW].rearrange("p (k t) -> p k t", k=DT)[:, :, WU:WU + TR]),
            sem="dout", dma=True)
    E.wait("sync", "dout")

    with nc.Block() as blk:
        E.build(blk)
    return nc





_NC_CACHE = {}


def _get_nc():
    if "nc" not in _NC_CACHE:
        _NC_CACHE["nc"] = build_nc()
    return _NC_CACHE["nc"]


def _selsq():
    s = np.zeros((1024, NEXP), dtype=ml_dtypes.bfloat16)
    for i in range(1024):
        s[i, i // NST] = 1
    return s


def _repr_mat():
    s = np.zeros((NEXP, 1024), dtype=ml_dtypes.bfloat16)
    for j in range(1024):
        s[j // NST, j] = 1
    return s


def _selr():
    s = np.zeros((1024, NST), dtype=ml_dtypes.bfloat16)
    for i in range(1024):
        s[i, i % NST] = 1
    return s


def _selrep():
    s = np.zeros((NST, 1024), dtype=ml_dtypes.bfloat16)
    for j in range(1024):
        s[j % NST, j] = 1
    return s


def _host_inputs(inputs):
    x = np.asarray(inputs["L0_attn_x"])
    Wx = np.asarray(inputs["L0_attn_W_xproj"])
    Wd = np.asarray(inputs["L0_attn_W_dt"])
    bdt = np.asarray(inputs["L0_attn_b_dt"])
    Alog = np.asarray(inputs["L0_attn_A_log"])
    Dp = np.asarray(inputs["L0_attn_D_param"])
    A = np.asarray(inputs["L0_ffn_A"])
    B = np.asarray(inputs["L0_ffn_B"])
    W = np.asarray(inputs["L0_ffn_W"])
    bias = np.asarray(inputs["L0_ffn_bias"])

    shared = {
        "WxTa": np.ascontiguousarray(Wx[:R, :].T),
        "WxTb": np.ascontiguousarray(Wx[R:, :].T),
        "WdT": np.ascontiguousarray(Wd.T),
        "bdt_cols": np.ascontiguousarray(bdt.reshape(DT, P).T),
        "Alog_cols": np.ascontiguousarray(
            Alog.reshape(DT, P, NST).transpose(1, 0, 2).reshape(P, DT * NST)),
        "D_cols": np.ascontiguousarray(Dp.reshape(DT, P).T),
        "A_lhsT": np.ascontiguousarray(A.transpose(2, 0, 1).reshape(D, 1024)),
        "W_T": np.ascontiguousarray(W.T),
        "Bf_T": np.ascontiguousarray(B.transpose(0, 2, 1).reshape(1024, D)),
        "bias_row": np.ascontiguousarray(bias.reshape(1, D)),
        "identC": np.eye(P, dtype=ml_dtypes.bfloat16),
        "onesC": np.ones((P, 512), dtype=ml_dtypes.bfloat16),
        "selsqC": _selsq(),
        "reprC": _repr_mat(),
        "selrC": _selr(),
        "selrepC": _selrep(),
    }
    in_maps = []
    for c in range(8):
        b, q = c // 4, c % 4
        t0 = q * TR
        if q == 0:
            xs = np.concatenate([np.zeros((WU, D), np.float32), x[b, 0:TR]], axis=0)
        else:
            xs = x[b, t0 - WU:t0 + TR]
        m = dict(shared)
        m["x_fm"] = np.ascontiguousarray(xs.T)
        bm = np.ones((NST, TW), dtype=ml_dtypes.bfloat16)
        if q == 0:
            bm[:, 0:WU] = 0
        m["bmask"] = bm
        in_maps.append(m)
    return in_maps


def kernel(**inputs):
    in_maps = _host_inputs(inputs)
    nc = _get_nc()
    res = run_bass_kernel_spmd(nc, in_maps, core_ids=list(range(8)))
    out = np.empty((2, 4096, D), dtype=np.float32)
    for c in range(8):
        b, q = c // 4, c % 4
        out[b, q * TR:(q + 1) * TR, :] = res.results[c]["out_fm"].T
    return out


# version 27
# speedup vs baseline: 1.0075x; 1.0059x over previous
import numpy as np
import ml_dtypes
import concourse.bass as bass
import concourse.mybir as mybir
from concourse.bass_utils import run_bass_kernel_spmd

F32 = mybir.dt.float32
BF16 = mybir.dt.bfloat16
AF = mybir.ActivationFunctionType
ALU = mybir.AluOpType

P = 128
TW = 1152
TR = 1024
WU = 128
D = 2048
DT = 16
NST = 16
NEXP = 64
ART = 8
JT = 16
R = 128
TCH = [(0, 512), (512, 512), (1024, 128)]
TCHR = [(128, 512), (640, 512)]


class Plan:

    AUTOE = ("vector", "scalar")

    def __init__(self, nc):
        self.nc = nc
        self.ops = {e: [] for e in ("sync", "scalar", "vector", "tensor", "gpsimd")}
        self.sems = {}
        self.counts = {}
        self.auto = {e: 0 for e in self.AUTOE}
        self.autosem = {e: nc.alloc_semaphore("auto_" + e) for e in self.AUTOE}
        self.nameeng = {}
        self.namevals = {}

    _MERGE = {"vr1": "vr", "vr2": "vr", "vr3": "vr", "sr1": "sr", "sr2": "sr",
              "tt2": "tt", "tre2": "tre", "vc": "vmisc", "vps": "vmisc",
              "gb": "gx", "gbd": "gx2", "gcd": "gx3"}

    def canon(self, name):
        base = name.split("_")[0]
        return self._MERGE.get(base, base)

    def sem(self, name):
        name = self.canon(name)
        if name not in self.sems:
            self.sems[name] = self.nc.alloc_semaphore("s_" + name)
            self.counts[name] = 0
        return self.sems[name]

    def op(self, eng, opfn, sem=None, dma=False, nowait=False):
        if eng in self.AUTOE and not dma:
            idx = self.auto[eng] + 1
            self.auto[eng] = idx
            a = self.autosem[eng]
            if idx > 1 and not nowait:
                self.ops[eng].append(lambda e, a=a, v=idx - 1: e.wait_ge(a, v))
            if sem is not None:
                cn = self.canon(sem)
                self.counts.setdefault(cn, 0)
                self.counts[cn] += 1
                self.nameeng.setdefault(cn, eng)
                assert self.nameeng[cn] == eng, (cn, eng)
                self.namevals.setdefault(cn, []).append(idx)
            self.ops[eng].append(lambda e, opfn=opfn, a=a: opfn(e).then_inc(a, 1))
            return
        if sem is not None:
            s = self.sem(sem)
            cn = self.canon(sem)
            self.counts[cn] += 16 if dma else 1
            self.nameeng.setdefault(cn, None)
            self.ops[eng].append(lambda e, opfn=opfn, s=s, k=(16 if dma else 1):
                                 opfn(e).then_inc(s, k))
        else:
            self.ops[eng].append(lambda e, opfn=opfn: opfn(e))

    def wait(self, eng, sem, val=None):
        cn = self.canon(sem)
        self.counts.setdefault(cn, 0)
        v = self.counts[cn] if val is None else val
        if v <= 0:
            return

        def resolved(e, eng=eng, cn=cn, v=v):
            if self.nameeng.get(cn) in self.AUTOE:
                src_eng = self.nameeng[cn]
                vals = self.namevals[cn]
                assert v <= len(vals), (cn, v, len(vals))
                tgt = vals[v - 1]
                if eng == src_eng:
                    return
                e.wait_ge(self.autosem[src_eng], tgt)
            else:
                e.wait_ge(self.sem(cn), v)
        self.ops[eng].append(resolved)

    def val(self, sem):
        cn = self.canon(sem)
        self.counts.setdefault(cn, 0)
        return self.counts[cn]

    def build(self, blk):
        for eng_name, deco in (("sync", blk.sync), ("scalar", blk.scalar),
                               ("vector", blk.vector), ("tensor", blk.tensor),
                               ("gpsimd", blk.gpsimd)):
            ops = self.ops[eng_name]

            def section(e, ops=ops):
                for fn in ops:
                    fn(e)
            deco(section)


def build_nc(phases=7, out_xln=False, ffn_sub=8):
    nc = bass.Bass("TRN2", target_bir_lowering=False)

    d_x = nc.dram_tensor("x_fm", [D, TW], F32, kind="ExternalInput")
    d_WxTa = nc.dram_tensor("WxTa", [D, R], F32, kind="ExternalInput")
    d_WxTb = nc.dram_tensor("WxTb", [D, 32], F32, kind="ExternalInput")
    d_WdT = nc.dram_tensor("WdT", [R, D], F32, kind="ExternalInput")
    d_bdt = nc.dram_tensor("bdt_cols", [P, DT], F32, kind="ExternalInput")
    d_Alog = nc.dram_tensor("Alog_cols", [P, DT * NST], F32, kind="ExternalInput")
    d_Dp = nc.dram_tensor("D_cols", [P, DT], F32, kind="ExternalInput")
    d_AlhsT = nc.dram_tensor("A_lhsT", [D, 1024], F32, kind="ExternalInput")
    d_WT = nc.dram_tensor("W_T", [D, D], F32, kind="ExternalInput")
    d_BfT = nc.dram_tensor("Bf_T", [1024, D], F32, kind="ExternalInput")
    d_bias = nc.dram_tensor("bias_row", [1, D], F32, kind="ExternalInput")
    d_ident = nc.dram_tensor("identC", [P, P], BF16, kind="ExternalInput")
    d_ones = nc.dram_tensor("onesC", [P, 512], BF16, kind="ExternalInput")
    d_selsq = nc.dram_tensor("selsqC", [1024, NEXP], BF16, kind="ExternalInput")
    d_repr = nc.dram_tensor("reprC", [NEXP, 1024], BF16, kind="ExternalInput")
    d_selr = nc.dram_tensor("selrC", [1024, NST], BF16, kind="ExternalInput")
    d_selrep = nc.dram_tensor("selrepC", [NST, 1024], BF16, kind="ExternalInput")
    d_bmask = nc.dram_tensor("bmask", [NST, TW], BF16, kind="ExternalInput")
    d_out = nc.dram_tensor("out_fm", [D, TR], F32, kind="ExternalOutput")
    d_rows = nc.dram_tensor("rowsD", [32, TW], BF16)
    d_lnrows = nc.dram_tensor("lnrowsD", [2, TW], BF16)

    A = nc.alloc_sbuf_tensor
    x = A("x", [P, DT * TW], F32)
    xln = A("xln", [P, DT * TW], BF16)
    xbr = A("xbr", [P, 2 * TW], BF16)
    sqr = A("sqr", [P, 2 * TW], BF16)
    vecs = A("vecs", [P, ART * TW], BF16)
    U = A("U", [P, ART * TW], BF16)
    mpool = A("mpool", [P, ART * 512 + 1024], BF16)
    masked = mpool[:, 0:ART * 512]
    vrepr = mpool[:, ART * 512:ART * 512 + 1024]
    arrT = A("arrT", [P, 9 * NEXP], BF16)
    routes = A("routes", [P, 9 * NEXP], BF16)
    max8 = A("max8", [P, 9 * 8], BF16)
    thr = A("thr", [P, 16], F32)
    arr_fm = A("arr_fm", [NEXP, TW], BF16)
    vsm = A("vsm", [NST, TW], BF16)
    wbufA = A("wbufA", [P, DT * P], BF16)
    wbufB = A("wbufB", [P, DT * P], BF16)
    wbufC = A("wbufC", [P, DT * P], BF16)
    wbufD = A("wbufD", [P, DT * P], BF16)
    ubufA = A("ubufA", [P, ART * P], BF16)
    ubufB = A("ubufB", [P, ART * P], BF16)
    biasrow = A("biasrow", [1, D], BF16)
    xz_a = A("xz_a", [P, TW], BF16)
    xz_b = A("xz_b", [32, TW], BF16)
    bdt = A("bdt", [P, DT], F32)
    acols = A("acols", [P, DT * NST], F32)
    dcols = A("dcols", [P, DT], F32)
    ident = A("identS", [P, P], BF16)
    onesc = A("onesS", [P, 512], BF16)
    selsq = A("selsqS", [P, ART * NEXP], BF16)
    reprm = A("reprS", [NEXP, 1024], BF16)
    selr = A("selrS", [P, ART * NST], BF16)
    selrep = A("selrepS", [NST, 1024], BF16)
    constv = A("constv", [P, 4], F32)

    lnr = vecs[:, 0:4608].bitcast(F32)
    lnrb = vecs[:, 4608:4608 + 3 * TW]
    rstd_rep = U[:, 0:TW]
    routesT = arr_fm
    mur_rep = U[:, TW:2 * TW]
    wxa = wbufA
    wdt_w = wbufC
    wxb = ubufA
    _dA0, _dA1 = 0, TW
    _bu = 2 * TW
    _hh = 3 * TW
    _ww0, _ww1 = 4 * TW, 4 * TW + TR
    _dxt0 = 4 * TW + 2 * TR
    _dxt1 = _dxt0 + TW
    assert _dxt1 + TW <= ART * TW

    pb = [nc.alloc_psum_tensor(f"pb{i}", [P, 512], F32) for i in range(8)]

    E = Plan(nc)

    def xt(k, c0=0, n=TW):
        return x[:, k * TW + c0: k * TW + c0 + n]

    def xlnt(k, c0=0, n=TW):
        return xln[:, k * TW + c0: k * TW + c0 + n]

    E.op("sync", lambda e: e.dma_start(out=ident[:], in_=d_ident.ap()), sem="dx", dma=True)
    E.op("sync", lambda e: e.dma_start(out=onesc[:], in_=d_ones.ap()), sem="dx", dma=True)
    for k in range(ART):
        E.op("sync", lambda e, k=k: e.dma_start(
            out=selsq[:, k * NEXP:(k + 1) * NEXP], in_=d_selsq.ap()[k * P:(k + 1) * P, :]),
            sem="dx", dma=True)
        E.op("sync", lambda e, k=k: e.dma_start(
            out=selr[:, k * NST:(k + 1) * NST], in_=d_selr.ap()[k * P:(k + 1) * P, :]),
            sem="dx", dma=True)
    E.op("sync", lambda e: e.dma_start(out=reprm[:], in_=d_repr.ap()), sem="dx", dma=True)
    E.op("sync", lambda e: e.dma_start(out=selrep[:], in_=d_selrep.ap()), sem="dx", dma=True)
    for dst, src in [(bdt, d_bdt), (acols, d_Alog), (dcols, d_Dp)]:
        E.op("sync", lambda e, dst=dst, src=src: e.dma_start(out=dst[:], in_=src.ap()),
             sem="dx", dma=True)
    E.op("gpsimd", lambda e: e.dma_start(out=biasrow[:], in_=d_bias.ap()), sem="dbias", dma=True)
    for kg in range(8):
        E.op("sync", lambda e, kg=kg: e.dma_start(
            out=x[:, kg * 2 * TW:(kg + 1) * 2 * TW].rearrange(
                "p (k t) -> p k t", k=2),
            in_=d_x.ap()[kg * 2 * P:(kg + 1) * 2 * P, :].rearrange(
                "(k p) t -> p k t", p=P)),
            sem=f"dxg{kg}", dma=True)
    dx_all = E.val("dx")

    E.op("vector", lambda e: e.memset(constv[:, 0:1], 1e-5))
    E.op("vector", lambda e: e.memset(constv[:, 1:2], -0.5))
    E.op("vector", lambda e: e.memset(constv[:, 2:3], 1.0), sem="vcst")
    E.wait("scalar", "dx", dx_all)
    E.wait("scalar", "vcst")
    E.op("scalar", lambda e: e.activation(acols[:], acols[:], AF.Exp), sem="sAx")
    E.wait("vector", "sAx")
    E.op("vector", lambda e: e.tensor_scalar_mul(acols[:], acols[:], -1.0), sem="sA")

    def emit_strip(src, col0, width, dst, sem, rows_tiles=DT):
        E.op("gpsimd", lambda e, src=src, col0=col0, width=width, dst=dst,
             rt=rows_tiles: e.dma_start(
                 out=dst[:, 0:rt * width].rearrange("p (k w) -> p k w", k=rt),
                 in_=src.ap()[0:rt * P, col0:col0 + width].rearrange(
                     "(k p) w -> p k w", p=P)),
             sem=sem, dma=True)

    _ln_marks = {}
    def ln_pass(tag, tok0, ntok, xdx=False):
        tch = []
        c = tok0
        while c < tok0 + ntok:
            n = min(512, tok0 + ntok - c)
            tch.append((c, n))
            c += n
        nch = len(tch)
        tst0 = E.val("tst_" + tag)
        E.wait("vector", "tv")
        E.wait("vector", "tb")
        E.wait("vector", "txz")
        E.wait("vector", "td")
        for k in range(DT):
            sl = k % 2
            if xdx:
                E.wait("vector", f"dxg{k // 2}", 16)
            E.op("vector", lambda e, k=k: e.tensor_copy(
                xlnt(k, tok0, ntok), xt(k, tok0, ntok)), sem="vc_" + tag,
                nowait=(k >= 1))
            E.wait("scalar", "vc_" + tag)
            if k >= 2:
                E.wait("scalar", "tst_" + tag, tst0 + (k - 1) * 2 * nch)
            E.op("scalar", lambda e, k=k, sl=sl: e.activation(
                sqr[:, sl * TW:sl * TW + ntok], xlnt(k, tok0, ntok),
                AF.Square), sem="ss_" + tag)
            E.wait("tensor", "ss_" + tag)
            for ci, (c0, n) in enumerate(tch):
                E.op("tensor", lambda e, k=k, ci=ci, c0=c0, n=n: e.matmul(
                    pb[2 + ci][0:1, 0:n], onesc[0:P, 0:1],
                    xlnt(k, c0, n),
                    start=(k == 0), stop=(k == DT - 1)), sem="tst_" + tag)
                E.op("tensor", lambda e, k=k, sl=sl, ci=ci, c0=c0, n=n: e.matmul(
                    pb[5 + ci][0:1, 0:n], onesc[0:P, 0:1],
                    sqr[:, sl * TW + c0 - tok0:sl * TW + c0 - tok0 + n],
                    start=(k == 0), stop=(k == DT - 1)), sem="tst_" + tag)
        E.wait("vector", "tst_" + tag)
        for ci, (c0, n) in enumerate(tch):
            E.op("vector", lambda e, ci=ci, c0=c0, n=n: e.tensor_copy(
                lnr[0:1, c0:c0 + n], pb[2 + ci][0:1, 0:n]), nowait=(ci >= 1))
            E.op("vector", lambda e, ci=ci, c0=c0, n=n: e.tensor_copy(
                lnr[0:1, TW + c0:TW + c0 + n], pb[5 + ci][0:1, 0:n]), sem="vps_" + tag,
                nowait=True)
        mur = lnr[0:1, tok0:tok0 + ntok]
        varr = lnr[0:1, TW + tok0:TW + tok0 + ntok]
        mu2b = lnrb[0:1, tok0:tok0 + ntok]
        rstdb = lnrb[0:1, TW + tok0:TW + tok0 + ntok]
        murb = lnrb[0:1, 2 * TW + tok0:2 * TW + tok0 + ntok]
        E.op("vector", lambda e: e.tensor_scalar_mul(mur, mur, 1.0 / D))
        E.op("vector", lambda e: e.tensor_scalar_mul(varr, varr, 1.0 / D), sem="vr1_" + tag,
             nowait=True)
        E.wait("scalar", "vr1_" + tag)
        E.op("scalar", lambda e: e.activation(mu2b, mur, AF.Square), sem="sr1_" + tag)
        E.wait("vector", "sr1_" + tag)
        E.op("vector", lambda e: e.tensor_tensor(varr, varr, mu2b, ALU.subtract),
             sem="vr2_" + tag)
        E.wait("scalar", "vr2_" + tag)
        E.op("scalar", lambda e: e.activation(varr, varr, AF.Ln, bias=constv[0:1, 0:1]))
        E.op("scalar", lambda e: e.activation(varr, varr, AF.Exp, scale=constv[0:1, 1:2]),
             sem="sr2_" + tag)
        E.wait("vector", "sr2_" + tag)
        E.op("vector", lambda e: e.tensor_copy(rstdb, varr))
        E.op("vector", lambda e: e.tensor_tensor(mur, mur, varr, ALU.mult))
        E.op("vector", lambda e: e.tensor_copy(murb, mur), sem="vr3_" + tag)
        E.wait("tensor", "vr3_" + tag)
        for ci, (c0, n) in enumerate(tch):
            E.op("tensor", lambda e, ci=ci, c0=c0, n=n: e.matmul(
                pb[2 + ci][:, 0:n], onesc[0:1, 0:P],
                lnrb[0:1, TW + c0:TW + c0 + n], start=True, stop=True),
                sem="trp_" + tag)
            E.op("tensor", lambda e, ci=ci, c0=c0, n=n: e.matmul(
                pb[5 + ci][:, 0:n], onesc[0:1, 0:P],
                lnrb[0:1, 2 * TW + c0:2 * TW + c0 + n], start=True, stop=True),
                sem="trp_" + tag)
        trp0 = E.val("trp_" + tag) - 2 * nch
        _ln_marks[tag] = (E.val("ln_" + tag), tch)
        for ci, (c0, n) in enumerate(tch):
            E.wait("vector", "trp_" + tag, trp0 + 2 * (ci + 1))
            E.op("vector", lambda e, ci=ci, c0=c0, n=n: e.tensor_copy(
                rstd_rep[:, c0:c0 + n], pb[2 + ci][:, 0:n]), nowait=True)
            E.op("vector", lambda e, ci=ci, c0=c0, n=n: e.tensor_copy(
                mur_rep[:, c0:c0 + n], pb[5 + ci][:, 0:n]), sem="vrp_" + tag,
                nowait=True)
            for k in range(DT):
                E.op("vector", lambda e, k=k, c0=c0, n=n: e.tensor_tensor(
                    xlnt(k, c0, n), xlnt(k, c0, n), rstd_rep[:, c0:c0 + n],
                    ALU.mult), nowait=(k >= 1))
                E.op("vector", lambda e, k=k, c0=c0, n=n: e.tensor_tensor(
                    xlnt(k, c0, n), xlnt(k, c0, n), mur_rep[:, c0:c0 + n],
                    ALU.subtract), sem="ln_" + tag)

    def ffn_pass(tag, tok0, ntok, sub=8, lntag=None):
        tch = []
        c = tok0
        while c < tok0 + ntok:
            n = min(512, tok0 + ntok - c)
            tch.append((c, n))
            c += n
        nch = len(tch)
        tiles = ntok // P
        t0i = tok0 // P
        tv0 = E.val("tv_" + tag)
        ve0 = E.val("ve_" + tag)
        taq0 = E.val("taq_" + tag)
        vat0 = E.val("vat_" + tag)
        vrt0 = E.val("vrt_" + tag)
        vU0 = E.val("vU_" + tag)
        tb0 = E.val("tb_" + tag)
        if sub <= 0:
            for art in range(ART):
                dstA = wbufA if art % 2 == 0 else wbufB
                emit_strip(d_AlhsT, art * P, P, dstA, f"wA{art % 2}_" + tag)
            return
        gidx = 0
        for art in range(ART):
            dstA = wbufA if art % 2 == 0 else wbufB
            if art >= 2:
                E.wait("gpsimd", "tv_" + tag, tv0 + (art - 1) * nch)
            emit_strip(d_AlhsT, art * P, P, dstA, f"wA{art % 2}_" + tag)
            E.wait("tensor", f"wA{art % 2}_" + tag)
            for ci, (c0, n) in enumerate(tch):
                bank = pb[gidx % 4]
                if gidx >= 4:
                    E.wait("tensor", "ve_" + tag, ve0 + gidx - 3)
                for k in range(DT):
                    if art == 0 and lntag is not None:
                        E.wait("tensor", "ln_" + tag,
                               _ln_marks[lntag][0] + DT * ci + k + 1)
                    E.op("tensor", lambda e, dstA=dstA, k=k, c0=c0, n=n, bank=bank:
                         e.matmul(bank[:, 0:n], dstA[:, k * P:(k + 1) * P],
                                  xlnt(k, c0, n), start=(k == 0), stop=(k == DT - 1)),
                         sem=("tv_" + tag) if k == DT - 1 else None)
                E.wait("scalar", "tv_" + tag)
                E.op("scalar", lambda e, art=art, c0=c0, n=n, bank=bank: e.activation(
                    vecs[:, art * TW + c0:art * TW + c0 + n], bank[:, 0:n], AF.Copy),
                    sem="ve_" + tag, nowait=True)
                gidx += 1
        if sub <= 1:
            return
        E.wait("tensor", "ve_" + tag)
        for k in range(ART):
            E.wait("scalar", "ve_" + tag, ve0 + (k + 1) * nch)
            if k >= 2:
                E.wait("scalar", "taq_" + tag, taq0 + (k - 1) * nch)
            E.op("scalar", lambda e, k=k: e.activation(
                sqr[:, (k % 2) * TW + tok0:(k % 2) * TW + tok0 + ntok],
                vecs[:, k * TW + tok0:k * TW + tok0 + ntok], AF.Square),
                sem="sq_" + tag, nowait=True)
            E.wait("tensor", "sq_" + tag)
            for ci, (c0, n) in enumerate(tch):
                E.op("tensor", lambda e, k=k, ci=ci, c0=c0, n=n: e.matmul(
                    pb[2 + ci][0:NEXP, 0:n], selsq[:, k * NEXP:(k + 1) * NEXP],
                    sqr[:, (k % 2) * TW + c0 - tok0 + tok0:(k % 2) * TW + c0 - tok0 + tok0 + n],
                    start=(k == 0), stop=(k == ART - 1)), sem="taq_" + tag)
        E.wait("scalar", "taq_" + tag)
        for ci, (c0, n) in enumerate(tch):
            E.op("scalar", lambda e, ci=ci, c0=c0, n=n: e.activation(
                arr_fm[0:NEXP, c0:c0 + n], pb[2 + ci][0:NEXP, 0:n], AF.Copy),
                sem="va_" + tag)
        if sub <= 2:
            return
        E.wait("tensor", "va_" + tag)
        for i, ti in enumerate(range(t0i, t0i + tiles)):
            E.op("tensor", lambda e, i=i, ti=ti: e.transpose(
                pb[3].bitcast(BF16)[0:P, i * NEXP:(i + 1) * NEXP],
                arr_fm[0:NEXP, ti * P:(ti + 1) * P],
                ident[0:NEXP, 0:NEXP]),
                sem=("tt_" + tag) if i == tiles - 1 else None)
        E.wait("vector", "tt_" + tag)
        E.op("vector", lambda e: e.tensor_copy(
            arrT[:, t0i * NEXP:(t0i + tiles) * NEXP],
            pb[3].bitcast(BF16)[0:P, 0:tiles * NEXP]), sem="vat_" + tag)
        if sub <= 3:
            return
        E.wait("vector", "vat_" + tag)
        for i, ti in enumerate(range(t0i, t0i + tiles)):
            E.op("vector", lambda e, ti=ti: e.max(
                max8[:, ti * 8:(ti + 1) * 8], arrT[:, ti * NEXP:(ti + 1) * NEXP]),
                nowait=(i >= 1))
        for i, ti in enumerate(range(t0i, t0i + tiles)):
            E.op("vector", lambda e, ti=ti: e.tensor_copy(
                thr[:, ti:ti + 1], max8[:, ti * 8 + 3:ti * 8 + 4]),
                nowait=(i >= 1))
        for i, ti in enumerate(range(t0i, t0i + tiles)):
            E.op("vector", lambda e, ti=ti: e.tensor_scalar(
                routes[:, ti * NEXP:(ti + 1) * NEXP], arrT[:, ti * NEXP:(ti + 1) * NEXP],
                thr[:, ti:ti + 1], 0.25, ALU.is_ge, ALU.mult), sem="vro_" + tag,
                nowait=(i >= 1))
        if sub <= 4:
            return
        E.wait("tensor", "vro_" + tag)
        E.wait("tensor", "vat_" + tag)
        for i, ti in enumerate(range(t0i, t0i + tiles)):
            bk, bo = (pb[3], i) if i < 8 else (pb[2], i - 8)
            E.op("tensor", lambda e, ti=ti, bk=bk, bo=bo: e.transpose(
                bk.bitcast(BF16)[0:NEXP, bo * P:(bo + 1) * P],
                routes[:, ti * NEXP:(ti + 1) * NEXP],
                ident[:]), sem=("tt2_" + tag) if i == tiles - 1 else None)
        E.wait("vector", "tt2_" + tag)
        E.op("vector", lambda e: e.tensor_copy(
            routesT[0:NEXP, t0i * P:(t0i + min(tiles, 8)) * P],
            pb[3].bitcast(BF16)[0:NEXP, 0:min(tiles, 8) * P]), sem="vrt_" + tag)
        if tiles > 8:
            E.op("vector", lambda e: e.tensor_copy(
                routesT[0:NEXP, (t0i + 8) * P:(t0i + tiles) * P],
                pb[2].bitcast(BF16)[0:NEXP, 0:(tiles - 8) * P]), sem="vrt_" + tag)
        if sub <= 5:
            return
        E.wait("tensor", "vrt_" + tag)
        vm0 = E.val("vm_" + tag)
        midx = 0
        for ci, (c0, n) in enumerate(tch):
            for art in range(ART):
                rb = pb[midx % 4]
                if midx >= 4:
                    E.wait("tensor", "vm_" + tag, vm0 + midx - 3)
                E.op("tensor", lambda e, art=art, c0=c0, n=n, rb=rb: e.matmul(
                    rb[:, 0:n], reprm[0:NEXP, art * P:(art + 1) * P],
                    routesT[0:NEXP, c0:c0 + n], start=True, stop=True), sem="tre_" + tag)
                E.wait("vector", "tre_" + tag)
                E.op("vector", lambda e, art=art, c0=c0, n=n, rb=rb: e.scalar_tensor_tensor(
                    masked[:, art * 512:art * 512 + n], rb[:, 0:n], 1.0,
                    vecs[:, art * TW + c0:art * TW + c0 + n], ALU.mult, ALU.mult),
                    sem="vm_" + tag, nowait=True)
                midx += 1
            E.wait("tensor", "vm_" + tag)
            E.wait("tensor", "vvs_" + tag)
            for k in range(ART):
                E.op("tensor", lambda e, k=k, c0=c0, n=n: e.matmul(
                    pb[4][0:NST, 0:n], selr[:, k * NST:(k + 1) * NST],
                    masked[:, k * 512:k * 512 + n],
                    start=(k == 0), stop=(k == ART - 1)),
                    sem=("tvv_" + tag) if k == ART - 1 else None)
            E.wait("scalar", "tvv_" + tag)
            E.op("scalar", lambda e, c0=c0, n=n: e.activation(
                vsm[0:NST, c0:c0 + n], pb[4][0:NST, 0:n], AF.Copy), sem="vvs_" + tag,
                nowait=True)
        if sub <= 6:
            return
        E.wait("tensor", "vvs_" + tag)
        vvr0 = E.val("vvr_" + tag)
        vslot = (vrepr[:, 0:512], vrepr[:, 512:1024], masked[:, 0:512])
        for art in range(ART):
            for ci, (c0, n) in enumerate(tch):
                idx = art * nch + ci
                sl = idx % 3
                vb = pb[4 + idx % 4]
                rb = pb[idx % 4]
                if idx >= 4:
                    E.wait("tensor", "vvr_" + tag, vvr0 + idx - 3)
                    E.wait("tensor", "vU_" + tag, vU0 + idx - 3)
                E.op("tensor", lambda e, art=art, c0=c0, n=n, vb=vb: e.matmul(
                    vb[:, 0:n], selrep[0:NST, art * P:(art + 1) * P],
                    vsm[0:NST, c0:c0 + n], start=True, stop=True))
                E.op("tensor", lambda e, art=art, c0=c0, n=n, rb=rb: e.matmul(
                    rb[:, 0:n], reprm[0:NEXP, art * P:(art + 1) * P],
                    routesT[0:NEXP, c0:c0 + n], start=True, stop=True), sem="tre2_" + tag)
                E.wait("scalar", "tre2_" + tag)
                if idx >= 3:
                    E.wait("scalar", "vU_" + tag, vU0 + idx - 2)
                E.op("scalar", lambda e, sl=sl, n=n, vb=vb: e.activation(
                    vslot[sl][:, 0:n], vb[:, 0:n], AF.Copy),
                    sem="vvr_" + tag, nowait=True)
                E.wait("vector", "vvr_" + tag, vvr0 + idx + 1)
                E.wait("vector", "tre2_" + tag)
                E.op("vector", lambda e, art=art, c0=c0, n=n, sl=sl, rb=rb:
                     e.scalar_tensor_tensor(
                         U[:, art * TW + c0:art * TW + c0 + n], rb[:, 0:n], 1.0,
                         vslot[sl][:, 0:n], ALU.mult, ALU.mult),
                     sem="vU_" + tag, nowait=True)
        if sub <= 7:
            return
        jidx = 0
        E.wait("tensor", "vU_" + tag)
        E.wait("tensor", "vvr_" + tag)
        E.wait("tensor", "dbias", 16)
        for j in range(JT):
            dstW = wbufC if j % 2 == 0 else wbufD
            dstU = ubufA if j % 2 == 0 else ubufB
            if j >= 2:
                E.wait("gpsimd", "tb_" + tag, tb0 + (j - 1) * nch)
            emit_strip(d_WT, j * P, P, dstW, f"wW{j % 2}_" + tag)
            emit_strip(d_BfT, j * P, P, dstU, f"wB{j % 2}_" + tag, rows_tiles=ART)
            E.wait("tensor", f"wW{j % 2}_" + tag)
            E.wait("tensor", f"wB{j % 2}_" + tag)
            if j >= 2:
                E.wait("tensor", "vres_" + tag, (j - 1) * nch)
            for ci, (c0, n) in enumerate(tch):
                bank = pb[4 + jidx % 4]
                jidx += 1
                E.wait("tensor", "vres_" + tag, max(0, E.val("vres_" + tag) - 3))
                for k in range(DT):
                    E.op("tensor", lambda e, dstW=dstW, k=k, c0=c0, n=n, bank=bank:
                         e.matmul(bank[:, 0:n], dstW[:, k * P:(k + 1) * P],
                                  xlnt(k, c0, n), start=(k == 0), stop=False))
                for k in range(ART):
                    E.op("tensor", lambda e, dstU=dstU, k=k, c0=c0, n=n, bank=bank:
                         e.matmul(bank[:, 0:n], dstU[:, k * P:(k + 1) * P],
                                  U[:, k * TW + c0:k * TW + c0 + n],
                                  start=False, stop=False))
                E.op("tensor", lambda e, j=j, c0=c0, n=n, bank=bank: e.matmul(
                    bank[:, 0:n], biasrow[0:1, j * P:(j + 1) * P],
                    onesc[0:1, 0:n], start=False, stop=True), sem="tb_" + tag)
                E.wait("vector", "tb_" + tag)
                E.op("vector", lambda e, j=j, c0=c0, n=n, bank=bank:
                     e.scalar_tensor_tensor(
                         xt(j, c0, n), bank[:, 0:n], 0.5, xt(j, c0, n),
                         ALU.mult, ALU.add), sem="vres_" + tag, nowait=True)

    def _mamba(E):
        E.wait("gpsimd", "tb_f1")
        E.wait("gpsimd", "tv_f1")
        E.op("gpsimd", lambda e: e.dma_start(
            out=wxa[:, 0:DT * R].rearrange("p (k w) -> p k w", k=DT),
            in_=d_WxTa.ap()[0:DT * P, :].rearrange("(k p) w -> p k w", p=P)),
            sem="dm", dma=True)
        E.op("gpsimd", lambda e: e.dma_start(
            out=wxb[:, 0:DT * 32].rearrange("p (k w) -> p k w", k=DT),
            in_=d_WxTb.ap()[0:DT * P, :].rearrange("(k p) w -> p k w", p=P)),
            sem="dm", dma=True)
        E.op("gpsimd", lambda e: e.dma_start(
            out=wdt_w[:, 0:DT * P], in_=d_WdT.ap()[:, 0:DT * P]),
            sem="dm", dma=True)
        dm_all = E.val("dm")

        E.wait("tensor", "dm", dm_all)
        for ci, (c0, n) in enumerate(TCH):
            if ci >= 1:
                E.wait("tensor", "vxz", 2 * ci)
            for k in range(DT):
                E.wait("tensor", "ln_l2", _ln_marks["l2"][0] + DT * ci + k + 1)
                E.op("tensor", lambda e, k=k, c0=c0, n=n: e.matmul(
                    pb[0][:, 0:n], wxa[:, k * R:(k + 1) * R], xlnt(k, c0, n),
                    start=(k == 0), stop=(k == DT - 1)))
            for k in range(DT):
                E.op("tensor", lambda e, k=k, c0=c0, n=n: e.matmul(
                    pb[1][0:32, 0:n], wxb[:, k * 32:(k + 1) * 32], xlnt(k, c0, n),
                    start=(k == 0), stop=(k == DT - 1)),
                    sem="txz" if k == DT - 1 else None)
            E.wait("vector", "txz")
            E.op("vector", lambda e, c0=c0, n=n: e.tensor_copy(xz_a[:, c0:c0 + n],
                                                               pb[0][:, 0:n]), sem="vxz")
            E.op("vector", lambda e, c0=c0, n=n: e.tensor_copy(xz_b[0:32, c0:c0 + n],
                                                               pb[1][0:32, 0:n]), sem="vxz",
                 nowait=True)
        E.wait("sync", "tre")
        E.op("sync", lambda e: e.dma_start(out=vsm[0:NST, :], in_=d_bmask.ap()),
             sem="dxm", dma=True)
        E.wait("vector", "dxm", 16)
        E.op("vector", lambda e: e.tensor_tensor(
            xz_b[0:NST, :], xz_b[0:NST, :], vsm[0:NST, :], ALU.mult), sem="vxz")
        E.wait("gpsimd", "vxz")
        E.op("gpsimd", lambda e: e.dma_start(out=d_rows.ap(), in_=xz_b[:]),
             sem="gbd", dma=True)

        E.wait("scalar", "sA")
        vbu_cnt = 0
        vw_cnt = 0
        vh_cnt = 0
        bq_emitted = 0
        cp_emitted = 0
        td_cnt = 0
        sd_cnt = 0

        def bq_load(g):
            nonlocal bq_emitted
            q = g % 4
            buf = g % 2
            E.wait("gpsimd", "gbd", 16)
            if g >= 2:
                E.wait("gpsimd", "vbu", (g - 1) * 8)
            E.op("gpsimd", lambda e, q=q, buf=buf: e.dma_start(
                out=vecs[:, buf * 4 * TW:(buf + 1) * 4 * TW],
                in_=d_rows.ap()[4 * q:4 * q + 4, :].unsqueeze(0).broadcast_to((P, 4, TW))),
                sem=f"dBq{buf}", dma=True)
            bq_emitted += 1

        def cp_load(h):
            nonlocal cp_emitted
            p = h % 8
            dst = xbr if h % 2 == 0 else sqr
            E.wait("gpsimd", "gbd", 16)
            for r in range(2):
                if h >= 2:
                    E.wait("gpsimd", "vw", (h - 2) * 4 + 2 * (r + 1))
                E.op("gpsimd", lambda e, p=p, dst=dst, r=r: e.dma_start(
                    out=dst[:, r * TW:(r + 1) * TW].unsqueeze(1),
                    in_=d_rows.ap()[16 + 2 * p + r:16 + 2 * p + r + 1, :]
                    .unsqueeze(0).broadcast_to((P, 1, TW))),
                    sem=f"dCp{h % 2}r{r}", dma=True)
            cp_emitted += 1

        deltaf = mpool.bitcast(F32)
        tdi = 0

        def emit_td(dtb_t, lo=0, hi=6):
            nonlocal td_cnt
            items = [(k, c0, n) for k in (2 * dtb_t, 2 * dtb_t + 1)
                     for (c0, n) in TCH]
            for k, c0, n in items[lo:hi]:
                tdb = pb[2 + td_cnt % 2]
                if td_cnt > 1:
                    E.wait("tensor", "sd", td_cnt - 1)
                E.op("tensor", lambda e, k=k, c0=c0, n=n, tdb=tdb: e.matmul(
                    tdb[:, 0:n], wdt_w[:, k * P:(k + 1) * P], xz_a[:, c0:c0 + n],
                    start=True, stop=True), sem="td")
                td_cnt += 1

        E.wait("tensor", "vps_l2")
        E.wait("tensor", "vxz")
        bq_load(0)
        bq_load(1)
        cp_load(0)
        cp_load(1)

        emit_td(0)
        for dtb in range(8):
            dts = (2 * dtb, 2 * dtb + 1)
            if dtb >= 1:
                emit_td(dtb, lo=2)
            for pi, k in enumerate(dts):
                sl = pi
                if dtb >= 1 and pi == 0:
                    E.wait("scalar", "vdx", 2 * dtb)
                for ci, (c0, n) in enumerate(TCH):
                    tdi += 1
                    E.wait("scalar", "td", tdi)
                    tdb = pb[2 + (tdi - 1) % 2]
                    E.op("scalar", lambda e, k=k, sl=sl, c0=c0, n=n, tdb=tdb: e.activation(
                        deltaf[:, sl * TW + c0:sl * TW + c0 + n], tdb[:, 0:n], AF.Exp,
                        bias=bdt[:, k:k + 1]), nowait=True)
                    E.op("scalar", lambda e, sl=sl, c0=c0, n=n: e.activation(
                        deltaf[:, sl * TW + c0:sl * TW + c0 + n],
                        deltaf[:, sl * TW + c0:sl * TW + c0 + n], AF.Ln, bias=constv[:, 2:3]),
                        sem="sd")
                    sd_cnt += 1
                E.wait("vector", "sd", sd_cnt)
                dxoff = _dxt0 if sl == 0 else _dxt1
                E.op("vector", lambda e, sl=sl, k=k, dxoff=dxoff: e.tensor_tensor(
                    U[:, dxoff:dxoff + TW], deltaf[:, sl * TW:(sl + 1) * TW], xlnt(k),
                    ALU.mult), sem="vdx", nowait=True)
            banks = {0: (pb[4], pb[5]), 1: (pb[6], pb[7])}
            for pi, k in enumerate(dts):
                if dtb >= 2 and pi == 0:
                    E.wait("vector", "tns", 32 * (dtb - 1))
                E.op("vector", lambda e, pi=pi, k=k, dtb=dtb: e.tensor_scalar_mul(
                    mpool[:, 4608 + (dtb % 2) * 256 + pi * P:
                           4608 + (dtb % 2) * 256 + (pi + 1) * P],
                    ident[:], dcols[:, k:k + 1]), sem="vdg")
            if dtb >= 1:
                E.wait("tensor", "vy", dtb * 2)
            for ni in range(NST):
                q = ni // 4
                g = dtb * 4 + q
                h = dtb * 8 + ni // 2
                if ni % 2 == 0 and h + 2 < 64:
                    cp_load(h + 2)
                if ni % 4 == 1 and g + 2 < 32:
                    bq_load(g + 2)
                for pi, k in enumerate(dts):
                    step = (dtb * NST + ni) * 2 + pi
                    dab = (U[:, _dA0:_dA0 + TW], U[:, _dA1:_dA1 + TW],
                           wbufD[:, 0:TW])[step % 3]
                    if step >= 3:
                        E.wait("scalar", "vh", step - 2)
                    E.op("scalar", lambda e, k=k, ni=ni, dab=dab, pi=pi: e.activation(
                        dab, deltaf[:, pi * TW:(pi + 1) * TW], AF.Exp,
                        scale=acols[:, k * NST + ni:k * NST + ni + 1]), sem="sdA",
                        nowait=(step % 32 != 0))
                    dxoff = _dxt0 if pi == 0 else _dxt1
                    if ni % 4 == 0 and pi == 0:
                        E.wait("vector", f"dBq{g % 2}", 16 * (g // 2 + 1))
                    boff = (g % 2) * 4 * TW + (ni % 4) * TW
                    E.op("vector", lambda e, dxoff=dxoff, boff=boff: e.tensor_tensor(
                        U[:, _bu:_bu + TW], U[:, dxoff:dxoff + TW],
                        vecs[:, boff:boff + TW], ALU.mult), sem="vbu",
                        nowait=(step % 32 != 0))
                    vbu_cnt += 1
                    E.wait("vector", "sdA", step + 1)
                    E.op("vector", lambda e, dab=dab: e.tensor_tensor_scan(
                        U[:, _hh:_hh + TW], dab, U[:, _bu:_bu + TW],
                        0.0, ALU.mult, ALU.add), sem="vh")
                    vh_cnt += 1
                    wsl = step % 2
                    woff = _ww0 if wsl == 0 else _ww1
                    cbuf = xbr if h % 2 == 0 else sqr
                    coff = (ni % 2) * TW
                    if pi == 0:
                        E.wait("vector", f"dCp{h % 2}r{ni % 2}",
                               16 * (h // 2 + 1))
                    if step >= 2:
                        E.wait("vector", "tns", step - 1)
                    E.op("vector", lambda e, woff=woff, cbuf=cbuf, coff=coff: e.tensor_tensor(
                        U[:, woff:woff + TR], U[:, _hh + WU:_hh + TW],
                        cbuf[:, coff + WU:coff + TW], ALU.mult), sem="vw")
                    vw_cnt += 1
                    E.wait("tensor", "vw", vw_cnt)
                    b0, b1 = banks[pi]
                    E.op("tensor", lambda e, woff=woff, ni=ni, b0=b0: e.matmul(
                        b0[:, 0:512], ident[:], U[:, woff:woff + 512],
                        start=(ni == 0), stop=(ni == NST - 1)))
                    E.op("tensor", lambda e, woff=woff, ni=ni, b1=b1: e.matmul(
                        b1[:, 0:512], ident[:], U[:, woff + 512:woff + 1024],
                        start=(ni == 0), stop=(ni == NST - 1)), sem="tns")
                    if ni == 0:
                        E.wait("tensor", "vdg", E.val("vdg"))
                        dg = mpool[:, 4608 + (dtb % 2) * 256 + pi * P:
                                   4608 + (dtb % 2) * 256 + (pi + 1) * P]
                        E.op("tensor", lambda e, b0=b0, dg=dg, k=k: e.matmul(
                            b0[:, 0:512], dg, xlnt(k, WU, 512),
                            start=False, stop=False))
                        E.op("tensor", lambda e, b1=b1, dg=dg, k=k: e.matmul(
                            b1[:, 0:512], dg, xlnt(k, WU + 512, 512),
                            start=False, stop=False))
                    if ni == 15 and pi == 1 and dtb < 7:
                        emit_td(dtb + 1, hi=2)
            E.wait("vector", "tns", 32 * (dtb + 1))
            for pi, k in enumerate(dts):
                b0, b1 = banks[pi]
                E.op("vector", lambda e, k=k, b0=b0: e.scalar_tensor_tensor(
                    xt(k, WU, 512), b0[:, 0:512], 1.0, xt(k, WU, 512), ALU.mult, ALU.add),
                    nowait=True)
                E.op("vector", lambda e, k=k, b1=b1: e.scalar_tensor_tensor(
                    xt(k, WU + 512, 512), b1[:, 0:512], 1.0, xt(k, WU + 512, 512),
                    ALU.mult, ALU.add), sem="vy", nowait=True)

        E.wait("vector", "vy")

    E.wait("tensor", "dx", dx_all)
    if phases >= 1:
        ln_pass("l1", 0, TW, xdx=True)
    if phases >= 2:
        ffn_pass("f1", 0, TW, sub=ffn_sub, lntag="l1")
        if ffn_sub >= 8:
            E.wait("vector", "vres_f1")
    if phases >= 3:
        ln_pass("l2", 0, TW)

    if phases >= 4:
        _mamba(E)
    if phases >= 6:
        ln_pass("l3", WU, TR)
    if phases >= 7:
        E.wait("gpsimd", "txz")
        E.wait("gpsimd", "td", 48)
        vres_pre_f2 = E.val("vres_f2")
        ffn_pass("f2", WU, TR, lntag="l3")

    if phases == 2 and ffn_sub < 8:
        lastsem = {0: "wA0_f1", 1: "ve_f1", 2: "va_f1", 3: "vat_f1", 4: "vro_f1",
                   5: "vrt_f1", 6: "vvs_f1", 7: "vU_f1"}[ffn_sub]
    else:
        lastsem = {0: "dx", 1: "ln_l1", 2: "vres_f1", 3: "ln_l2", 4: "vxz",
                   5: "vy", 6: "ln_l3", 7: "vres_f2"}[min(phases, 7)]
    if out_xln:
        E.wait("sync", lastsem)
        for k in range(DT):
            E.op("sync", lambda e, k=k: e.dma_start(
                out=d_out.ap().bitcast(BF16)[k * P:(k + 1) * P, 0:TR],
                in_=xlnt(k, WU, TR)), sem="dout", dma=True)
    elif phases >= 7:
        for k in range(DT):
            E.wait("sync", "vres_f2", vres_pre_f2 + (k + 1) * 2)
            E.op("sync", lambda e, k=k: e.dma_start(
                out=d_out.ap()[k * P:(k + 1) * P, :], in_=xt(k, WU, TR)),
                sem="dout", dma=True)
    else:
        E.wait("sync", lastsem)
        E.op("sync", lambda e: e.dma_start(
            out=d_out.ap()[0:D, :].rearrange("(k p) t -> p k t", p=P),
            in_=x[:, 0:DT * TW].rearrange("p (k t) -> p k t", k=DT)[:, :, WU:WU + TR]),
            sem="dout", dma=True)
    E.wait("sync", "dout")

    with nc.Block() as blk:
        E.build(blk)
    return nc





_NC_CACHE = {}


def _get_nc():
    if "nc" not in _NC_CACHE:
        _NC_CACHE["nc"] = build_nc()
    return _NC_CACHE["nc"]


def _selsq():
    s = np.zeros((1024, NEXP), dtype=ml_dtypes.bfloat16)
    for i in range(1024):
        s[i, i // NST] = 1
    return s


def _repr_mat():
    s = np.zeros((NEXP, 1024), dtype=ml_dtypes.bfloat16)
    for j in range(1024):
        s[j // NST, j] = 1
    return s


def _selr():
    s = np.zeros((1024, NST), dtype=ml_dtypes.bfloat16)
    for i in range(1024):
        s[i, i % NST] = 1
    return s


def _selrep():
    s = np.zeros((NST, 1024), dtype=ml_dtypes.bfloat16)
    for j in range(1024):
        s[j % NST, j] = 1
    return s


def _host_inputs(inputs):
    x = np.asarray(inputs["L0_attn_x"])
    Wx = np.asarray(inputs["L0_attn_W_xproj"])
    Wd = np.asarray(inputs["L0_attn_W_dt"])
    bdt = np.asarray(inputs["L0_attn_b_dt"])
    Alog = np.asarray(inputs["L0_attn_A_log"])
    Dp = np.asarray(inputs["L0_attn_D_param"])
    A = np.asarray(inputs["L0_ffn_A"])
    B = np.asarray(inputs["L0_ffn_B"])
    W = np.asarray(inputs["L0_ffn_W"])
    bias = np.asarray(inputs["L0_ffn_bias"])

    shared = {
        "WxTa": np.ascontiguousarray(Wx[:R, :].T),
        "WxTb": np.ascontiguousarray(Wx[R:, :].T),
        "WdT": np.ascontiguousarray(Wd.T),
        "bdt_cols": np.ascontiguousarray(bdt.reshape(DT, P).T),
        "Alog_cols": np.ascontiguousarray(
            Alog.reshape(DT, P, NST).transpose(1, 0, 2).reshape(P, DT * NST)),
        "D_cols": np.ascontiguousarray(Dp.reshape(DT, P).T),
        "A_lhsT": np.ascontiguousarray(A.transpose(2, 0, 1).reshape(D, 1024)),
        "W_T": np.ascontiguousarray(W.T),
        "Bf_T": np.ascontiguousarray(B.transpose(0, 2, 1).reshape(1024, D)),
        "bias_row": np.ascontiguousarray(bias.reshape(1, D)),
        "identC": np.eye(P, dtype=ml_dtypes.bfloat16),
        "onesC": np.ones((P, 512), dtype=ml_dtypes.bfloat16),
        "selsqC": _selsq(),
        "reprC": _repr_mat(),
        "selrC": _selr(),
        "selrepC": _selrep(),
    }
    in_maps = []
    for c in range(8):
        b, q = c // 4, c % 4
        t0 = q * TR
        if q == 0:
            xs = np.concatenate([np.zeros((WU, D), np.float32), x[b, 0:TR]], axis=0)
        else:
            xs = x[b, t0 - WU:t0 + TR]
        m = dict(shared)
        m["x_fm"] = np.ascontiguousarray(xs.T)
        bm = np.ones((NST, TW), dtype=ml_dtypes.bfloat16)
        if q == 0:
            bm[:, 0:WU] = 0
        m["bmask"] = bm
        in_maps.append(m)
    return in_maps


def kernel(**inputs):
    in_maps = _host_inputs(inputs)
    nc = _get_nc()
    res = run_bass_kernel_spmd(nc, in_maps, core_ids=list(range(8)))
    out = np.empty((2, 4096, D), dtype=np.float32)
    for c in range(8):
        b, q = c // 4, c % 4
        out[b, q * TR:(q + 1) * TR, :] = res.results[c]["out_fm"].T
    return out


# version 28 (final)
# speedup vs baseline: 1.0187x; 1.0111x over previous
import numpy as np
import ml_dtypes
import concourse.bass as bass
import concourse.mybir as mybir
from concourse.bass_utils import run_bass_kernel_spmd

F32 = mybir.dt.float32
BF16 = mybir.dt.bfloat16
AF = mybir.ActivationFunctionType
ALU = mybir.AluOpType

P = 128
TW = 1152
TR = 1024
WU = 128
D = 2048
DT = 16
NST = 16
NEXP = 64
ART = 8
JT = 16
R = 128
TCH = [(0, 512), (512, 512), (1024, 128)]
TCHR = [(128, 512), (640, 512)]


class Plan:

    AUTOE = ("vector", "scalar")

    def __init__(self, nc):
        self.nc = nc
        self.ops = {e: [] for e in ("sync", "scalar", "vector", "tensor", "gpsimd")}
        self.sems = {}
        self.counts = {}
        self.auto = {e: 0 for e in self.AUTOE}
        self.autosem = {e: nc.alloc_semaphore("auto_" + e) for e in self.AUTOE}
        self.nameeng = {}
        self.namevals = {}

    _MERGE = {"vr1": "vr", "vr2": "vr", "vr3": "vr", "sr1": "sr", "sr2": "sr",
              "tt2": "tt", "tre2": "tre", "vc": "vmisc", "vps": "vmisc",
              "gb": "gx", "gbd": "gx2", "gcd": "gx3"}

    def canon(self, name):
        base = name.split("_")[0]
        return self._MERGE.get(base, base)

    def sem(self, name):
        name = self.canon(name)
        if name not in self.sems:
            self.sems[name] = self.nc.alloc_semaphore("s_" + name)
            self.counts[name] = 0
        return self.sems[name]

    def op(self, eng, opfn, sem=None, dma=False, nowait=False):
        if eng in self.AUTOE and not dma:
            idx = self.auto[eng] + 1
            self.auto[eng] = idx
            a = self.autosem[eng]
            if idx > 1 and not nowait:
                self.ops[eng].append(lambda e, a=a, v=idx - 1: e.wait_ge(a, v))
            if sem is not None:
                cn = self.canon(sem)
                self.counts.setdefault(cn, 0)
                self.counts[cn] += 1
                self.nameeng.setdefault(cn, eng)
                assert self.nameeng[cn] == eng, (cn, eng)
                self.namevals.setdefault(cn, []).append(idx)
            self.ops[eng].append(lambda e, opfn=opfn, a=a: opfn(e).then_inc(a, 1))
            return
        if sem is not None:
            s = self.sem(sem)
            cn = self.canon(sem)
            self.counts[cn] += 16 if dma else 1
            self.nameeng.setdefault(cn, None)
            self.ops[eng].append(lambda e, opfn=opfn, s=s, k=(16 if dma else 1):
                                 opfn(e).then_inc(s, k))
        else:
            self.ops[eng].append(lambda e, opfn=opfn: opfn(e))

    def wait(self, eng, sem, val=None):
        cn = self.canon(sem)
        self.counts.setdefault(cn, 0)
        v = self.counts[cn] if val is None else val
        if v <= 0:
            return

        def resolved(e, eng=eng, cn=cn, v=v):
            if self.nameeng.get(cn) in self.AUTOE:
                src_eng = self.nameeng[cn]
                vals = self.namevals[cn]
                assert v <= len(vals), (cn, v, len(vals))
                tgt = vals[v - 1]
                if eng == src_eng:
                    return
                e.wait_ge(self.autosem[src_eng], tgt)
            else:
                e.wait_ge(self.sem(cn), v)
        self.ops[eng].append(resolved)

    def val(self, sem):
        cn = self.canon(sem)
        self.counts.setdefault(cn, 0)
        return self.counts[cn]

    def build(self, blk):
        for eng_name, deco in (("sync", blk.sync), ("scalar", blk.scalar),
                               ("vector", blk.vector), ("tensor", blk.tensor),
                               ("gpsimd", blk.gpsimd)):
            ops = self.ops[eng_name]

            def section(e, ops=ops):
                for fn in ops:
                    fn(e)
            deco(section)


def build_nc(phases=7, out_xln=False, ffn_sub=8):
    nc = bass.Bass("TRN2", target_bir_lowering=False)

    d_x = nc.dram_tensor("x_fm", [D, TW], F32, kind="ExternalInput")
    d_WxTa = nc.dram_tensor("WxTa", [D, R], F32, kind="ExternalInput")
    d_WxTb = nc.dram_tensor("WxTb", [D, 32], F32, kind="ExternalInput")
    d_WdT = nc.dram_tensor("WdT", [R, D], F32, kind="ExternalInput")
    d_bdt = nc.dram_tensor("bdt_cols", [P, DT], F32, kind="ExternalInput")
    d_Alog = nc.dram_tensor("Alog_cols", [P, DT * NST], F32, kind="ExternalInput")
    d_Dp = nc.dram_tensor("D_cols", [P, DT], F32, kind="ExternalInput")
    d_AlhsT = nc.dram_tensor("A_lhsT", [D, 1024], F32, kind="ExternalInput")
    d_WT = nc.dram_tensor("W_T", [D, D], F32, kind="ExternalInput")
    d_BfT = nc.dram_tensor("Bf_T", [1024, D], F32, kind="ExternalInput")
    d_bias = nc.dram_tensor("bias_row", [1, D], F32, kind="ExternalInput")
    d_ident = nc.dram_tensor("identC", [P, P], BF16, kind="ExternalInput")
    d_ones = nc.dram_tensor("onesC", [P, 512], BF16, kind="ExternalInput")
    d_selsq = nc.dram_tensor("selsqC", [1024, NEXP], BF16, kind="ExternalInput")
    d_repr = nc.dram_tensor("reprC", [NEXP, 1024], BF16, kind="ExternalInput")
    d_selr = nc.dram_tensor("selrC", [1024, NST], BF16, kind="ExternalInput")
    d_selrep = nc.dram_tensor("selrepC", [NST, 1024], BF16, kind="ExternalInput")
    d_bmask = nc.dram_tensor("bmask", [NST, TW], BF16, kind="ExternalInput")
    d_out = nc.dram_tensor("out_fm", [D, TR], F32, kind="ExternalOutput")
    d_rows = nc.dram_tensor("rowsD", [32, TW], BF16)
    d_lnrows = nc.dram_tensor("lnrowsD", [2, TW], BF16)

    A = nc.alloc_sbuf_tensor
    x = A("x", [P, DT * TW], F32)
    xln = A("xln", [P, DT * TW], BF16)
    xbr = A("xbr", [P, 2 * TW], BF16)
    sqr = A("sqr", [P, 2 * TW], BF16)
    vecs = A("vecs", [P, ART * TW], BF16)
    U = A("U", [P, ART * TW], BF16)
    mpool = A("mpool", [P, ART * 512 + 1024], BF16)
    masked = mpool[:, 0:ART * 512]
    vrepr = mpool[:, ART * 512:ART * 512 + 1024]
    arrT = A("arrT", [P, 9 * NEXP], BF16)
    routes = A("routes", [P, 9 * NEXP], BF16)
    max8 = A("max8", [P, 9 * 8], BF16)
    thr = A("thr", [P, 16], F32)
    arr_fm = A("arr_fm", [NEXP, TW], BF16)
    vsm = A("vsm", [NST, TW], BF16)
    wbufA = A("wbufA", [P, DT * P], BF16)
    wbufB = A("wbufB", [P, DT * P], BF16)
    wbufC = A("wbufC", [P, DT * P], BF16)
    wbufD = A("wbufD", [P, DT * P], BF16)
    ubufA = A("ubufA", [P, ART * P], BF16)
    ubufB = A("ubufB", [P, ART * P], BF16)
    biasrow = A("biasrow", [1, D], BF16)
    xz_a = A("xz_a", [P, TW], BF16)
    xz_b = A("xz_b", [32, TW], BF16)
    bdt = A("bdt", [P, DT], F32)
    acols = A("acols", [P, DT * NST], F32)
    dcols = A("dcols", [P, DT], F32)
    ident = A("identS", [P, P], BF16)
    onesc = A("onesS", [P, 512], BF16)
    selsq = A("selsqS", [P, ART * NEXP], BF16)
    reprm = A("reprS", [NEXP, 1024], BF16)
    selr = A("selrS", [P, ART * NST], BF16)
    selrep = A("selrepS", [NST, 1024], BF16)
    constv = A("constv", [P, 4], F32)

    lnr = vecs[:, 0:4608].bitcast(F32)
    lnrb = vecs[:, 4608:4608 + 3 * TW]
    rstd_rep = U[:, 0:TW]
    routesT = arr_fm
    mur_rep = U[:, TW:2 * TW]
    wxa = wbufA
    wdt_w = wbufC
    wxb = ubufA
    _dA0, _dA1 = 0, TW
    _bu = 2 * TW
    _hh = 3 * TW
    _ww0, _ww1 = 4 * TW, 4 * TW + TR
    _dxt0 = 4 * TW + 2 * TR
    _dxt1 = _dxt0 + TW
    assert _dxt1 + TW <= ART * TW

    pb = [nc.alloc_psum_tensor(f"pb{i}", [P, 512], F32) for i in range(8)]

    E = Plan(nc)

    def xt(k, c0=0, n=TW):
        return x[:, k * TW + c0: k * TW + c0 + n]

    def xlnt(k, c0=0, n=TW):
        return xln[:, k * TW + c0: k * TW + c0 + n]

    E.op("sync", lambda e: e.dma_start(out=ident[:], in_=d_ident.ap()), sem="dx", dma=True)
    E.op("sync", lambda e: e.dma_start(out=onesc[:], in_=d_ones.ap()), sem="dx", dma=True)
    for k in range(ART):
        E.op("sync", lambda e, k=k: e.dma_start(
            out=selsq[:, k * NEXP:(k + 1) * NEXP], in_=d_selsq.ap()[k * P:(k + 1) * P, :]),
            sem="dx", dma=True)
        E.op("sync", lambda e, k=k: e.dma_start(
            out=selr[:, k * NST:(k + 1) * NST], in_=d_selr.ap()[k * P:(k + 1) * P, :]),
            sem="dx", dma=True)
    E.op("sync", lambda e: e.dma_start(out=reprm[:], in_=d_repr.ap()), sem="dx", dma=True)
    E.op("sync", lambda e: e.dma_start(out=selrep[:], in_=d_selrep.ap()), sem="dx", dma=True)
    for dst, src in [(bdt, d_bdt), (acols, d_Alog), (dcols, d_Dp)]:
        E.op("sync", lambda e, dst=dst, src=src: e.dma_start(out=dst[:], in_=src.ap()),
             sem="dx", dma=True)
    E.op("gpsimd", lambda e: e.dma_start(out=biasrow[:], in_=d_bias.ap()), sem="dbias", dma=True)
    for kg in range(8):
        E.op("sync", lambda e, kg=kg: e.dma_start(
            out=x[:, kg * 2 * TW:(kg + 1) * 2 * TW].rearrange(
                "p (k t) -> p k t", k=2),
            in_=d_x.ap()[kg * 2 * P:(kg + 1) * 2 * P, :].rearrange(
                "(k p) t -> p k t", p=P)),
            sem=f"dxg{kg}", dma=True)
    dx_all = E.val("dx")

    E.op("vector", lambda e: e.memset(constv[:, 0:1], 1e-5))
    E.op("vector", lambda e: e.memset(constv[:, 1:2], -0.5))
    E.op("vector", lambda e: e.memset(constv[:, 2:3], 1.0), sem="vcst")
    E.wait("scalar", "dx", dx_all)
    E.wait("scalar", "vcst")
    E.op("scalar", lambda e: e.activation(acols[:], acols[:], AF.Exp), sem="sAx")
    E.wait("vector", "sAx")
    E.op("vector", lambda e: e.tensor_scalar_mul(acols[:], acols[:], -1.0), sem="sA")

    def emit_strip(src, col0, width, dst, sem, rows_tiles=DT):
        h = rows_tiles // 2
        for k0, kn in ((0, h), (h, rows_tiles - h)):
            E.op("gpsimd", lambda e, src=src, col0=col0, width=width, dst=dst,
                 k0=k0, kn=kn: e.dma_start(
                     out=dst[:, k0 * width:(k0 + kn) * width].rearrange(
                         "p (k w) -> p k w", k=kn),
                     in_=src.ap()[k0 * P:(k0 + kn) * P, col0:col0 + width].rearrange(
                         "(k p) w -> p k w", p=P)),
                 sem=sem, dma=True)

    _ln_marks = {}
    def ln_pass(tag, tok0, ntok, xdx=False):
        tch = []
        c = tok0
        while c < tok0 + ntok:
            n = min(512, tok0 + ntok - c)
            tch.append((c, n))
            c += n
        nch = len(tch)
        tst0 = E.val("tst_" + tag)
        E.wait("vector", "tv")
        E.wait("vector", "tb")
        E.wait("vector", "txz")
        E.wait("vector", "td")
        for k in range(DT):
            sl = k % 2
            if xdx:
                E.wait("vector", f"dxg{k // 2}", 16)
            E.op("vector", lambda e, k=k: e.tensor_copy(
                xlnt(k, tok0, ntok), xt(k, tok0, ntok)), sem="vc_" + tag,
                nowait=(k >= 1))
            E.wait("scalar", "vc_" + tag)
            if k >= 2:
                E.wait("scalar", "tst_" + tag, tst0 + (k - 1) * 2 * nch)
            E.op("scalar", lambda e, k=k, sl=sl: e.activation(
                sqr[:, sl * TW:sl * TW + ntok], xlnt(k, tok0, ntok),
                AF.Square), sem="ss_" + tag)
            E.wait("tensor", "ss_" + tag)
            for ci, (c0, n) in enumerate(tch):
                E.op("tensor", lambda e, k=k, ci=ci, c0=c0, n=n: e.matmul(
                    pb[2 + ci][0:1, 0:n], onesc[0:P, 0:1],
                    xlnt(k, c0, n),
                    start=(k == 0), stop=(k == DT - 1)), sem="tst_" + tag)
                E.op("tensor", lambda e, k=k, sl=sl, ci=ci, c0=c0, n=n: e.matmul(
                    pb[5 + ci][0:1, 0:n], onesc[0:P, 0:1],
                    sqr[:, sl * TW + c0 - tok0:sl * TW + c0 - tok0 + n],
                    start=(k == 0), stop=(k == DT - 1)), sem="tst_" + tag)
        E.wait("vector", "tst_" + tag)
        for ci, (c0, n) in enumerate(tch):
            E.op("vector", lambda e, ci=ci, c0=c0, n=n: e.tensor_copy(
                lnr[0:1, c0:c0 + n], pb[2 + ci][0:1, 0:n]), nowait=(ci >= 1))
            E.op("vector", lambda e, ci=ci, c0=c0, n=n: e.tensor_copy(
                lnr[0:1, TW + c0:TW + c0 + n], pb[5 + ci][0:1, 0:n]), sem="vps_" + tag,
                nowait=True)
        mur = lnr[0:1, tok0:tok0 + ntok]
        varr = lnr[0:1, TW + tok0:TW + tok0 + ntok]
        mu2b = lnrb[0:1, tok0:tok0 + ntok]
        rstdb = lnrb[0:1, TW + tok0:TW + tok0 + ntok]
        murb = lnrb[0:1, 2 * TW + tok0:2 * TW + tok0 + ntok]
        E.op("vector", lambda e: e.tensor_scalar_mul(mur, mur, 1.0 / D))
        E.op("vector", lambda e: e.tensor_scalar_mul(varr, varr, 1.0 / D), sem="vr1_" + tag,
             nowait=True)
        E.wait("scalar", "vr1_" + tag)
        E.op("scalar", lambda e: e.activation(mu2b, mur, AF.Square), sem="sr1_" + tag)
        E.wait("vector", "sr1_" + tag)
        E.op("vector", lambda e: e.tensor_tensor(varr, varr, mu2b, ALU.subtract),
             sem="vr2_" + tag)
        E.wait("scalar", "vr2_" + tag)
        E.op("scalar", lambda e: e.activation(varr, varr, AF.Ln, bias=constv[0:1, 0:1]))
        E.op("scalar", lambda e: e.activation(varr, varr, AF.Exp, scale=constv[0:1, 1:2]),
             sem="sr2_" + tag)
        E.wait("vector", "sr2_" + tag)
        E.op("vector", lambda e: e.tensor_copy(rstdb, varr))
        E.op("vector", lambda e: e.tensor_tensor(mur, mur, varr, ALU.mult))
        E.op("vector", lambda e: e.tensor_copy(murb, mur), sem="vr3_" + tag)
        E.wait("tensor", "vr3_" + tag)
        for ci, (c0, n) in enumerate(tch):
            E.op("tensor", lambda e, ci=ci, c0=c0, n=n: e.matmul(
                pb[2 + ci][:, 0:n], onesc[0:1, 0:P],
                lnrb[0:1, TW + c0:TW + c0 + n], start=True, stop=True),
                sem="trp_" + tag)
            E.op("tensor", lambda e, ci=ci, c0=c0, n=n: e.matmul(
                pb[5 + ci][:, 0:n], onesc[0:1, 0:P],
                lnrb[0:1, 2 * TW + c0:2 * TW + c0 + n], start=True, stop=True),
                sem="trp_" + tag)
        trp0 = E.val("trp_" + tag) - 2 * nch
        _ln_marks[tag] = (E.val("ln_" + tag), tch)
        for ci, (c0, n) in enumerate(tch):
            E.wait("vector", "trp_" + tag, trp0 + 2 * (ci + 1))
            E.op("vector", lambda e, ci=ci, c0=c0, n=n: e.tensor_copy(
                rstd_rep[:, c0:c0 + n], pb[2 + ci][:, 0:n]), nowait=True)
            E.op("vector", lambda e, ci=ci, c0=c0, n=n: e.tensor_copy(
                mur_rep[:, c0:c0 + n], pb[5 + ci][:, 0:n]), sem="vrp_" + tag,
                nowait=True)
            for k in range(DT):
                E.op("vector", lambda e, k=k, c0=c0, n=n: e.tensor_tensor(
                    xlnt(k, c0, n), xlnt(k, c0, n), rstd_rep[:, c0:c0 + n],
                    ALU.mult), nowait=(k >= 1))
                E.op("vector", lambda e, k=k, c0=c0, n=n: e.tensor_tensor(
                    xlnt(k, c0, n), xlnt(k, c0, n), mur_rep[:, c0:c0 + n],
                    ALU.subtract), sem="ln_" + tag)

    def ffn_pass(tag, tok0, ntok, sub=8, lntag=None):
        tch = []
        c = tok0
        while c < tok0 + ntok:
            n = min(512, tok0 + ntok - c)
            tch.append((c, n))
            c += n
        nch = len(tch)
        tiles = ntok // P
        t0i = tok0 // P
        tv0 = E.val("tv_" + tag)
        ve0 = E.val("ve_" + tag)
        taq0 = E.val("taq_" + tag)
        vat0 = E.val("vat_" + tag)
        vrt0 = E.val("vrt_" + tag)
        vU0 = E.val("vU_" + tag)
        tb0 = E.val("tb_" + tag)
        if sub <= 0:
            for art in range(ART):
                dstA = wbufA if art % 2 == 0 else wbufB
                emit_strip(d_AlhsT, art * P, P, dstA, f"wA{art % 2}_" + tag)
            return
        gidx = 0
        for art in range(ART):
            dstA = wbufA if art % 2 == 0 else wbufB
            if art >= 2:
                E.wait("gpsimd", "tv_" + tag, tv0 + (art - 1) * nch)
            emit_strip(d_AlhsT, art * P, P, dstA, f"wA{art % 2}_" + tag)
            E.wait("tensor", f"wA{art % 2}_" + tag)
            for ci, (c0, n) in enumerate(tch):
                bank = pb[gidx % 4]
                if gidx >= 4:
                    E.wait("tensor", "ve_" + tag, ve0 + gidx - 3)
                for k in range(DT):
                    if art == 0 and lntag is not None:
                        E.wait("tensor", "ln_" + tag,
                               _ln_marks[lntag][0] + DT * ci + k + 1)
                    E.op("tensor", lambda e, dstA=dstA, k=k, c0=c0, n=n, bank=bank:
                         e.matmul(bank[:, 0:n], dstA[:, k * P:(k + 1) * P],
                                  xlnt(k, c0, n), start=(k == 0), stop=(k == DT - 1)),
                         sem=("tv_" + tag) if k == DT - 1 else None)
                E.wait("scalar", "tv_" + tag)
                E.op("scalar", lambda e, art=art, c0=c0, n=n, bank=bank: e.activation(
                    vecs[:, art * TW + c0:art * TW + c0 + n], bank[:, 0:n], AF.Copy),
                    sem="ve_" + tag, nowait=True)
                gidx += 1
        if sub <= 1:
            return
        E.wait("tensor", "ve_" + tag)
        for k in range(ART):
            E.wait("scalar", "ve_" + tag, ve0 + (k + 1) * nch)
            if k >= 2:
                E.wait("scalar", "taq_" + tag, taq0 + (k - 1) * nch)
            E.op("scalar", lambda e, k=k: e.activation(
                sqr[:, (k % 2) * TW + tok0:(k % 2) * TW + tok0 + ntok],
                vecs[:, k * TW + tok0:k * TW + tok0 + ntok], AF.Square),
                sem="sq_" + tag, nowait=True)
            E.wait("tensor", "sq_" + tag)
            for ci, (c0, n) in enumerate(tch):
                E.op("tensor", lambda e, k=k, ci=ci, c0=c0, n=n: e.matmul(
                    pb[2 + ci][0:NEXP, 0:n], selsq[:, k * NEXP:(k + 1) * NEXP],
                    sqr[:, (k % 2) * TW + c0 - tok0 + tok0:(k % 2) * TW + c0 - tok0 + tok0 + n],
                    start=(k == 0), stop=(k == ART - 1)), sem="taq_" + tag)
        E.wait("scalar", "taq_" + tag)
        for ci, (c0, n) in enumerate(tch):
            E.op("scalar", lambda e, ci=ci, c0=c0, n=n: e.activation(
                arr_fm[0:NEXP, c0:c0 + n], pb[2 + ci][0:NEXP, 0:n], AF.Copy),
                sem="va_" + tag)
        if sub <= 2:
            return
        E.wait("tensor", "va_" + tag)
        for i, ti in enumerate(range(t0i, t0i + tiles)):
            E.op("tensor", lambda e, i=i, ti=ti: e.transpose(
                pb[3].bitcast(BF16)[0:P, i * NEXP:(i + 1) * NEXP],
                arr_fm[0:NEXP, ti * P:(ti + 1) * P],
                ident[0:NEXP, 0:NEXP]),
                sem=("tt_" + tag) if i == tiles - 1 else None)
        E.wait("vector", "tt_" + tag)
        E.op("vector", lambda e: e.tensor_copy(
            arrT[:, t0i * NEXP:(t0i + tiles) * NEXP],
            pb[3].bitcast(BF16)[0:P, 0:tiles * NEXP]), sem="vat_" + tag)
        if sub <= 3:
            return
        E.wait("vector", "vat_" + tag)
        for i, ti in enumerate(range(t0i, t0i + tiles)):
            E.op("vector", lambda e, ti=ti: e.max(
                max8[:, ti * 8:(ti + 1) * 8], arrT[:, ti * NEXP:(ti + 1) * NEXP]),
                nowait=(i >= 1))
        for i, ti in enumerate(range(t0i, t0i + tiles)):
            E.op("vector", lambda e, ti=ti: e.tensor_copy(
                thr[:, ti:ti + 1], max8[:, ti * 8 + 3:ti * 8 + 4]),
                nowait=(i >= 1))
        for i, ti in enumerate(range(t0i, t0i + tiles)):
            E.op("vector", lambda e, ti=ti: e.tensor_scalar(
                routes[:, ti * NEXP:(ti + 1) * NEXP], arrT[:, ti * NEXP:(ti + 1) * NEXP],
                thr[:, ti:ti + 1], 0.25, ALU.is_ge, ALU.mult), sem="vro_" + tag,
                nowait=(i >= 1))
        if sub <= 4:
            return
        E.wait("tensor", "vro_" + tag)
        E.wait("tensor", "vat_" + tag)
        for i, ti in enumerate(range(t0i, t0i + tiles)):
            bk, bo = (pb[3], i) if i < 8 else (pb[2], i - 8)
            E.op("tensor", lambda e, ti=ti, bk=bk, bo=bo: e.transpose(
                bk.bitcast(BF16)[0:NEXP, bo * P:(bo + 1) * P],
                routes[:, ti * NEXP:(ti + 1) * NEXP],
                ident[:]), sem=("tt2_" + tag) if i == tiles - 1 else None)
        E.wait("vector", "tt2_" + tag)
        E.op("vector", lambda e: e.tensor_copy(
            routesT[0:NEXP, t0i * P:(t0i + min(tiles, 8)) * P],
            pb[3].bitcast(BF16)[0:NEXP, 0:min(tiles, 8) * P]), sem="vrt_" + tag)
        if tiles > 8:
            E.op("vector", lambda e: e.tensor_copy(
                routesT[0:NEXP, (t0i + 8) * P:(t0i + tiles) * P],
                pb[2].bitcast(BF16)[0:NEXP, 0:(tiles - 8) * P]), sem="vrt_" + tag)
        if sub <= 5:
            return
        E.wait("tensor", "vrt_" + tag)
        vm0 = E.val("vm_" + tag)
        midx = 0
        for ci, (c0, n) in enumerate(tch):
            for art in range(ART):
                rb = pb[midx % 4]
                if midx >= 4:
                    E.wait("tensor", "vm_" + tag, vm0 + midx - 3)
                E.op("tensor", lambda e, art=art, c0=c0, n=n, rb=rb: e.matmul(
                    rb[:, 0:n], reprm[0:NEXP, art * P:(art + 1) * P],
                    routesT[0:NEXP, c0:c0 + n], start=True, stop=True), sem="tre_" + tag)
                E.wait("vector", "tre_" + tag)
                E.op("vector", lambda e, art=art, c0=c0, n=n, rb=rb: e.scalar_tensor_tensor(
                    masked[:, art * 512:art * 512 + n], rb[:, 0:n], 1.0,
                    vecs[:, art * TW + c0:art * TW + c0 + n], ALU.mult, ALU.mult),
                    sem="vm_" + tag, nowait=True)
                midx += 1
            E.wait("tensor", "vm_" + tag)
            E.wait("tensor", "vvs_" + tag)
            for k in range(ART):
                E.op("tensor", lambda e, k=k, c0=c0, n=n: e.matmul(
                    pb[4][0:NST, 0:n], selr[:, k * NST:(k + 1) * NST],
                    masked[:, k * 512:k * 512 + n],
                    start=(k == 0), stop=(k == ART - 1)),
                    sem=("tvv_" + tag) if k == ART - 1 else None)
            E.wait("scalar", "tvv_" + tag)
            E.op("scalar", lambda e, c0=c0, n=n: e.activation(
                vsm[0:NST, c0:c0 + n], pb[4][0:NST, 0:n], AF.Copy), sem="vvs_" + tag,
                nowait=True)
        if sub <= 6:
            return
        E.wait("tensor", "vvs_" + tag)
        vvr0 = E.val("vvr_" + tag)
        vslot = (vrepr[:, 0:512], vrepr[:, 512:1024], masked[:, 0:512])
        for art in range(ART):
            for ci, (c0, n) in enumerate(tch):
                idx = art * nch + ci
                sl = idx % 3
                vb = pb[4 + idx % 4]
                rb = pb[idx % 4]
                if idx >= 4:
                    E.wait("tensor", "vvr_" + tag, vvr0 + idx - 3)
                    E.wait("tensor", "vU_" + tag, vU0 + idx - 3)
                E.op("tensor", lambda e, art=art, c0=c0, n=n, vb=vb: e.matmul(
                    vb[:, 0:n], selrep[0:NST, art * P:(art + 1) * P],
                    vsm[0:NST, c0:c0 + n], start=True, stop=True))
                E.op("tensor", lambda e, art=art, c0=c0, n=n, rb=rb: e.matmul(
                    rb[:, 0:n], reprm[0:NEXP, art * P:(art + 1) * P],
                    routesT[0:NEXP, c0:c0 + n], start=True, stop=True), sem="tre2_" + tag)
                E.wait("scalar", "tre2_" + tag)
                if idx >= 3:
                    E.wait("scalar", "vU_" + tag, vU0 + idx - 2)
                E.op("scalar", lambda e, sl=sl, n=n, vb=vb: e.activation(
                    vslot[sl][:, 0:n], vb[:, 0:n], AF.Copy),
                    sem="vvr_" + tag, nowait=True)
                E.wait("vector", "vvr_" + tag, vvr0 + idx + 1)
                E.wait("vector", "tre2_" + tag)
                E.op("vector", lambda e, art=art, c0=c0, n=n, sl=sl, rb=rb:
                     e.scalar_tensor_tensor(
                         U[:, art * TW + c0:art * TW + c0 + n], rb[:, 0:n], 1.0,
                         vslot[sl][:, 0:n], ALU.mult, ALU.mult),
                     sem="vU_" + tag, nowait=True)
        if sub <= 7:
            return
        jidx = 0
        E.wait("tensor", "vU_" + tag)
        E.wait("tensor", "vvr_" + tag)
        E.wait("tensor", "dbias", 16)
        for j in range(JT):
            dstW = wbufC if j % 2 == 0 else wbufD
            dstU = ubufA if j % 2 == 0 else ubufB
            if j >= 2:
                E.wait("gpsimd", "tb_" + tag, tb0 + (j - 1) * nch)
            emit_strip(d_WT, j * P, P, dstW, f"wW{j % 2}_" + tag)
            emit_strip(d_BfT, j * P, P, dstU, f"wB{j % 2}_" + tag, rows_tiles=ART)
            E.wait("tensor", f"wW{j % 2}_" + tag)
            E.wait("tensor", f"wB{j % 2}_" + tag)
            if j >= 2:
                E.wait("tensor", "vres_" + tag, (j - 1) * nch)
            for ci, (c0, n) in enumerate(tch):
                bank = pb[4 + jidx % 4]
                jidx += 1
                E.wait("tensor", "vres_" + tag, max(0, E.val("vres_" + tag) - 3))
                for k in range(DT):
                    E.op("tensor", lambda e, dstW=dstW, k=k, c0=c0, n=n, bank=bank:
                         e.matmul(bank[:, 0:n], dstW[:, k * P:(k + 1) * P],
                                  xlnt(k, c0, n), start=(k == 0), stop=False))
                for k in range(ART):
                    E.op("tensor", lambda e, dstU=dstU, k=k, c0=c0, n=n, bank=bank:
                         e.matmul(bank[:, 0:n], dstU[:, k * P:(k + 1) * P],
                                  U[:, k * TW + c0:k * TW + c0 + n],
                                  start=False, stop=False))
                E.op("tensor", lambda e, j=j, c0=c0, n=n, bank=bank: e.matmul(
                    bank[:, 0:n], biasrow[0:1, j * P:(j + 1) * P],
                    onesc[0:1, 0:n], start=False, stop=True), sem="tb_" + tag)
                E.wait("vector", "tb_" + tag)
                E.op("vector", lambda e, j=j, c0=c0, n=n, bank=bank:
                     e.scalar_tensor_tensor(
                         xt(j, c0, n), bank[:, 0:n], 0.5, xt(j, c0, n),
                         ALU.mult, ALU.add), sem="vres_" + tag, nowait=True)

    def _mamba(E):
        E.wait("gpsimd", "tb_f1")
        E.wait("gpsimd", "tv_f1")
        E.op("gpsimd", lambda e: e.dma_start(
            out=wxa[:, 0:DT * R].rearrange("p (k w) -> p k w", k=DT),
            in_=d_WxTa.ap()[0:DT * P, :].rearrange("(k p) w -> p k w", p=P)),
            sem="dm", dma=True)
        E.op("gpsimd", lambda e: e.dma_start(
            out=wxb[:, 0:DT * 32].rearrange("p (k w) -> p k w", k=DT),
            in_=d_WxTb.ap()[0:DT * P, :].rearrange("(k p) w -> p k w", p=P)),
            sem="dm", dma=True)
        E.op("gpsimd", lambda e: e.dma_start(
            out=wdt_w[:, 0:DT * P], in_=d_WdT.ap()[:, 0:DT * P]),
            sem="dm", dma=True)
        dm_all = E.val("dm")

        E.wait("tensor", "dm", dm_all)
        for ci, (c0, n) in enumerate(TCH):
            if ci >= 1:
                E.wait("tensor", "vxz", 2 * ci)
            for k in range(DT):
                E.wait("tensor", "ln_l2", _ln_marks["l2"][0] + DT * ci + k + 1)
                E.op("tensor", lambda e, k=k, c0=c0, n=n: e.matmul(
                    pb[0][:, 0:n], wxa[:, k * R:(k + 1) * R], xlnt(k, c0, n),
                    start=(k == 0), stop=(k == DT - 1)))
            for k in range(DT):
                E.op("tensor", lambda e, k=k, c0=c0, n=n: e.matmul(
                    pb[1][0:32, 0:n], wxb[:, k * 32:(k + 1) * 32], xlnt(k, c0, n),
                    start=(k == 0), stop=(k == DT - 1)),
                    sem="txz" if k == DT - 1 else None)
            E.wait("vector", "txz")
            E.op("vector", lambda e, c0=c0, n=n: e.tensor_copy(xz_a[:, c0:c0 + n],
                                                               pb[0][:, 0:n]), sem="vxz")
            E.op("vector", lambda e, c0=c0, n=n: e.tensor_copy(xz_b[0:32, c0:c0 + n],
                                                               pb[1][0:32, 0:n]), sem="vxz",
                 nowait=True)
        E.wait("sync", "tre")
        E.op("sync", lambda e: e.dma_start(out=vsm[0:NST, :], in_=d_bmask.ap()),
             sem="dxm", dma=True)
        E.wait("vector", "dxm", 16)
        E.op("vector", lambda e: e.tensor_tensor(
            xz_b[0:NST, :], xz_b[0:NST, :], vsm[0:NST, :], ALU.mult), sem="vxz")
        E.wait("gpsimd", "vxz")
        E.op("gpsimd", lambda e: e.dma_start(out=d_rows.ap(), in_=xz_b[:]),
             sem="gbd", dma=True)

        E.wait("scalar", "sA")
        vbu_cnt = 0
        vw_cnt = 0
        vh_cnt = 0
        bq_emitted = 0
        cp_emitted = 0
        td_cnt = 0
        sd_cnt = 0

        def bq_load(g):
            nonlocal bq_emitted
            q = g % 4
            buf = g % 2
            E.wait("gpsimd", "gbd", 16)
            if g >= 2:
                E.wait("gpsimd", "vbu", (g - 1) * 8)
            E.op("gpsimd", lambda e, q=q, buf=buf: e.dma_start(
                out=vecs[:, buf * 4 * TW:(buf + 1) * 4 * TW],
                in_=d_rows.ap()[4 * q:4 * q + 4, :].unsqueeze(0).broadcast_to((P, 4, TW))),
                sem=f"dBq{buf}", dma=True)
            bq_emitted += 1

        def cp_load(h):
            nonlocal cp_emitted
            p = h % 8
            dst = xbr if h % 2 == 0 else sqr
            E.wait("gpsimd", "gbd", 16)
            for r in range(2):
                if h >= 2:
                    E.wait("gpsimd", "vw", (h - 2) * 4 + 2 * (r + 1))
                E.op("gpsimd", lambda e, p=p, dst=dst, r=r: e.dma_start(
                    out=dst[:, r * TW:(r + 1) * TW].unsqueeze(1),
                    in_=d_rows.ap()[16 + 2 * p + r:16 + 2 * p + r + 1, :]
                    .unsqueeze(0).broadcast_to((P, 1, TW))),
                    sem=f"dCp{h % 2}r{r}", dma=True)
            cp_emitted += 1

        deltaf = mpool.bitcast(F32)
        tdi = 0

        def emit_td(dtb_t, lo=0, hi=6):
            nonlocal td_cnt
            items = [(k, c0, n) for k in (2 * dtb_t, 2 * dtb_t + 1)
                     for (c0, n) in TCH]
            for k, c0, n in items[lo:hi]:
                tdb = pb[2 + td_cnt % 2]
                if td_cnt > 1:
                    E.wait("tensor", "sd", td_cnt - 1)
                E.op("tensor", lambda e, k=k, c0=c0, n=n, tdb=tdb: e.matmul(
                    tdb[:, 0:n], wdt_w[:, k * P:(k + 1) * P], xz_a[:, c0:c0 + n],
                    start=True, stop=True), sem="td")
                td_cnt += 1

        E.wait("tensor", "vps_l2")
        E.wait("tensor", "vxz")
        bq_load(0)
        bq_load(1)
        cp_load(0)
        cp_load(1)

        emit_td(0)
        for dtb in range(8):
            dts = (2 * dtb, 2 * dtb + 1)
            if dtb >= 1:
                emit_td(dtb, lo=2)
            for pi, k in enumerate(dts):
                sl = pi
                if dtb >= 1 and pi == 0:
                    E.wait("scalar", "vdx", 2 * dtb)
                for ci, (c0, n) in enumerate(TCH):
                    tdi += 1
                    E.wait("scalar", "td", tdi)
                    tdb = pb[2 + (tdi - 1) % 2]
                    E.op("scalar", lambda e, k=k, sl=sl, c0=c0, n=n, tdb=tdb: e.activation(
                        deltaf[:, sl * TW + c0:sl * TW + c0 + n], tdb[:, 0:n], AF.Exp,
                        bias=bdt[:, k:k + 1]), nowait=True)
                    E.op("scalar", lambda e, sl=sl, c0=c0, n=n: e.activation(
                        deltaf[:, sl * TW + c0:sl * TW + c0 + n],
                        deltaf[:, sl * TW + c0:sl * TW + c0 + n], AF.Ln, bias=constv[:, 2:3]),
                        sem="sd")
                    sd_cnt += 1
                E.wait("vector", "sd", sd_cnt)
                dxoff = _dxt0 if sl == 0 else _dxt1
                E.op("vector", lambda e, sl=sl, k=k, dxoff=dxoff: e.tensor_tensor(
                    U[:, dxoff:dxoff + TW], deltaf[:, sl * TW:(sl + 1) * TW], xlnt(k),
                    ALU.mult), sem="vdx", nowait=True)
            banks = {0: (pb[4], pb[5]), 1: (pb[6], pb[7])}
            for pi, k in enumerate(dts):
                if dtb >= 2 and pi == 0:
                    E.wait("vector", "tns", 32 * (dtb - 1))
                E.op("vector", lambda e, pi=pi, k=k, dtb=dtb: e.tensor_scalar_mul(
                    mpool[:, 4608 + (dtb % 2) * 256 + pi * P:
                           4608 + (dtb % 2) * 256 + (pi + 1) * P],
                    ident[:], dcols[:, k:k + 1]), sem="vdg")
            if dtb >= 1:
                E.wait("tensor", "vy", dtb * 2)
            for ni in range(NST):
                q = ni // 4
                g = dtb * 4 + q
                h = dtb * 8 + ni // 2
                if ni % 2 == 0 and h + 2 < 64:
                    cp_load(h + 2)
                if ni % 4 == 1 and g + 2 < 32:
                    bq_load(g + 2)
                for pi, k in enumerate(dts):
                    step = (dtb * NST + ni) * 2 + pi
                    dab = (U[:, _dA0:_dA0 + TW], U[:, _dA1:_dA1 + TW],
                           wbufD[:, 0:TW])[step % 3]
                    if step >= 3:
                        E.wait("scalar", "vh", step - 2)
                    E.op("scalar", lambda e, k=k, ni=ni, dab=dab, pi=pi: e.activation(
                        dab, deltaf[:, pi * TW:(pi + 1) * TW], AF.Exp,
                        scale=acols[:, k * NST + ni:k * NST + ni + 1]), sem="sdA",
                        nowait=(step % 32 != 0))
                    dxoff = _dxt0 if pi == 0 else _dxt1
                    if ni % 4 == 0 and pi == 0:
                        E.wait("vector", f"dBq{g % 2}", 16 * (g // 2 + 1))
                    boff = (g % 2) * 4 * TW + (ni % 4) * TW
                    E.op("vector", lambda e, dxoff=dxoff, boff=boff: e.tensor_tensor(
                        U[:, _bu:_bu + TW], U[:, dxoff:dxoff + TW],
                        vecs[:, boff:boff + TW], ALU.mult), sem="vbu",
                        nowait=(step % 32 != 0))
                    vbu_cnt += 1
                    E.wait("vector", "sdA", step + 1)
                    E.op("vector", lambda e, dab=dab: e.tensor_tensor_scan(
                        U[:, _hh:_hh + TW], dab, U[:, _bu:_bu + TW],
                        0.0, ALU.mult, ALU.add), sem="vh")
                    vh_cnt += 1
                    wsl = step % 2
                    woff = _ww0 if wsl == 0 else _ww1
                    cbuf = xbr if h % 2 == 0 else sqr
                    coff = (ni % 2) * TW
                    if pi == 0:
                        E.wait("vector", f"dCp{h % 2}r{ni % 2}",
                               16 * (h // 2 + 1))
                    if step >= 2:
                        E.wait("vector", "tns", step - 1)
                    E.op("vector", lambda e, woff=woff, cbuf=cbuf, coff=coff: e.tensor_tensor(
                        U[:, woff:woff + TR], U[:, _hh + WU:_hh + TW],
                        cbuf[:, coff + WU:coff + TW], ALU.mult), sem="vw")
                    vw_cnt += 1
                    E.wait("tensor", "vw", vw_cnt)
                    b0, b1 = banks[pi]
                    E.op("tensor", lambda e, woff=woff, ni=ni, b0=b0: e.matmul(
                        b0[:, 0:512], ident[:], U[:, woff:woff + 512],
                        start=(ni == 0), stop=(ni == NST - 1)))
                    E.op("tensor", lambda e, woff=woff, ni=ni, b1=b1: e.matmul(
                        b1[:, 0:512], ident[:], U[:, woff + 512:woff + 1024],
                        start=(ni == 0), stop=(ni == NST - 1)), sem="tns")
                    if ni == 0:
                        E.wait("tensor", "vdg", E.val("vdg"))
                        dg = mpool[:, 4608 + (dtb % 2) * 256 + pi * P:
                                   4608 + (dtb % 2) * 256 + (pi + 1) * P]
                        E.op("tensor", lambda e, b0=b0, dg=dg, k=k: e.matmul(
                            b0[:, 0:512], dg, xlnt(k, WU, 512),
                            start=False, stop=False))
                        E.op("tensor", lambda e, b1=b1, dg=dg, k=k: e.matmul(
                            b1[:, 0:512], dg, xlnt(k, WU + 512, 512),
                            start=False, stop=False))
                    if ni == 15 and pi == 1 and dtb < 7:
                        emit_td(dtb + 1, hi=2)
            E.wait("vector", "tns", 32 * (dtb + 1))
            for pi, k in enumerate(dts):
                b0, b1 = banks[pi]
                E.op("vector", lambda e, k=k, b0=b0: e.scalar_tensor_tensor(
                    xt(k, WU, 512), b0[:, 0:512], 1.0, xt(k, WU, 512), ALU.mult, ALU.add),
                    nowait=True)
                E.op("vector", lambda e, k=k, b1=b1: e.scalar_tensor_tensor(
                    xt(k, WU + 512, 512), b1[:, 0:512], 1.0, xt(k, WU + 512, 512),
                    ALU.mult, ALU.add), sem="vy", nowait=True)

        E.wait("vector", "vy")

    E.wait("tensor", "dx", dx_all)
    if phases >= 1:
        ln_pass("l1", 0, TW, xdx=True)
    if phases >= 2:
        ffn_pass("f1", 0, TW, sub=ffn_sub, lntag="l1")
        if ffn_sub >= 8:
            E.wait("vector", "vres_f1")
    if phases >= 3:
        ln_pass("l2", 0, TW)

    if phases >= 4:
        _mamba(E)
    if phases >= 6:
        ln_pass("l3", WU, TR)
    if phases >= 7:
        E.wait("gpsimd", "txz")
        E.wait("gpsimd", "td", 48)
        vres_pre_f2 = E.val("vres_f2")
        ffn_pass("f2", WU, TR, lntag="l3")

    if phases == 2 and ffn_sub < 8:
        lastsem = {0: "wA0_f1", 1: "ve_f1", 2: "va_f1", 3: "vat_f1", 4: "vro_f1",
                   5: "vrt_f1", 6: "vvs_f1", 7: "vU_f1"}[ffn_sub]
    else:
        lastsem = {0: "dx", 1: "ln_l1", 2: "vres_f1", 3: "ln_l2", 4: "vxz",
                   5: "vy", 6: "ln_l3", 7: "vres_f2"}[min(phases, 7)]
    if out_xln:
        E.wait("sync", lastsem)
        for k in range(DT):
            E.op("sync", lambda e, k=k: e.dma_start(
                out=d_out.ap().bitcast(BF16)[k * P:(k + 1) * P, 0:TR],
                in_=xlnt(k, WU, TR)), sem="dout", dma=True)
    elif phases >= 7:
        for k in range(DT):
            E.wait("sync", "vres_f2", vres_pre_f2 + (k + 1) * 2)
            E.op("sync", lambda e, k=k: e.dma_start(
                out=d_out.ap()[k * P:(k + 1) * P, :], in_=xt(k, WU, TR)),
                sem="dout", dma=True)
    else:
        E.wait("sync", lastsem)
        E.op("sync", lambda e: e.dma_start(
            out=d_out.ap()[0:D, :].rearrange("(k p) t -> p k t", p=P),
            in_=x[:, 0:DT * TW].rearrange("p (k t) -> p k t", k=DT)[:, :, WU:WU + TR]),
            sem="dout", dma=True)
    E.wait("sync", "dout")

    with nc.Block() as blk:
        E.build(blk)
    return nc





_NC_CACHE = {}


def _get_nc():
    if "nc" not in _NC_CACHE:
        _NC_CACHE["nc"] = build_nc()
    return _NC_CACHE["nc"]


def _selsq():
    s = np.zeros((1024, NEXP), dtype=ml_dtypes.bfloat16)
    for i in range(1024):
        s[i, i // NST] = 1
    return s


def _repr_mat():
    s = np.zeros((NEXP, 1024), dtype=ml_dtypes.bfloat16)
    for j in range(1024):
        s[j // NST, j] = 1
    return s


def _selr():
    s = np.zeros((1024, NST), dtype=ml_dtypes.bfloat16)
    for i in range(1024):
        s[i, i % NST] = 1
    return s


def _selrep():
    s = np.zeros((NST, 1024), dtype=ml_dtypes.bfloat16)
    for j in range(1024):
        s[j % NST, j] = 1
    return s


def _host_inputs(inputs):
    x = np.asarray(inputs["L0_attn_x"])
    Wx = np.asarray(inputs["L0_attn_W_xproj"])
    Wd = np.asarray(inputs["L0_attn_W_dt"])
    bdt = np.asarray(inputs["L0_attn_b_dt"])
    Alog = np.asarray(inputs["L0_attn_A_log"])
    Dp = np.asarray(inputs["L0_attn_D_param"])
    A = np.asarray(inputs["L0_ffn_A"])
    B = np.asarray(inputs["L0_ffn_B"])
    W = np.asarray(inputs["L0_ffn_W"])
    bias = np.asarray(inputs["L0_ffn_bias"])

    shared = {
        "WxTa": np.ascontiguousarray(Wx[:R, :].T),
        "WxTb": np.ascontiguousarray(Wx[R:, :].T),
        "WdT": np.ascontiguousarray(Wd.T),
        "bdt_cols": np.ascontiguousarray(bdt.reshape(DT, P).T),
        "Alog_cols": np.ascontiguousarray(
            Alog.reshape(DT, P, NST).transpose(1, 0, 2).reshape(P, DT * NST)),
        "D_cols": np.ascontiguousarray(Dp.reshape(DT, P).T),
        "A_lhsT": np.ascontiguousarray(A.transpose(2, 0, 1).reshape(D, 1024)),
        "W_T": np.ascontiguousarray(W.T),
        "Bf_T": np.ascontiguousarray(B.transpose(0, 2, 1).reshape(1024, D)),
        "bias_row": np.ascontiguousarray(bias.reshape(1, D)),
        "identC": np.eye(P, dtype=ml_dtypes.bfloat16),
        "onesC": np.ones((P, 512), dtype=ml_dtypes.bfloat16),
        "selsqC": _selsq(),
        "reprC": _repr_mat(),
        "selrC": _selr(),
        "selrepC": _selrep(),
    }
    in_maps = []
    for c in range(8):
        b, q = c // 4, c % 4
        t0 = q * TR
        if q == 0:
            xs = np.concatenate([np.zeros((WU, D), np.float32), x[b, 0:TR]], axis=0)
        else:
            xs = x[b, t0 - WU:t0 + TR]
        m = dict(shared)
        m["x_fm"] = np.ascontiguousarray(xs.T)
        bm = np.ones((NST, TW), dtype=ml_dtypes.bfloat16)
        if q == 0:
            bm[:, 0:WU] = 0
        m["bmask"] = bm
        in_maps.append(m)
    return in_maps


def kernel(**inputs):
    in_maps = _host_inputs(inputs)
    nc = _get_nc()
    res = run_bass_kernel_spmd(nc, in_maps, core_ids=list(range(8)))
    out = np.empty((2, 4096, D), dtype=np.float32)
    for c in range(8):
        b, q = c // 4, c % 4
        out[b, q * TR:(q + 1) * TR, :] = res.results[c]["out_fm"].T
    return out


# version 29
# speedup vs baseline: 1.0205x; 1.0018x over previous
import numpy as np
import ml_dtypes
import concourse.bass as bass
import concourse.mybir as mybir
from concourse.bass_utils import run_bass_kernel_spmd

F32 = mybir.dt.float32
BF16 = mybir.dt.bfloat16
AF = mybir.ActivationFunctionType
ALU = mybir.AluOpType

P = 128
TW = 1152
TR = 1024
WU = 128
D = 2048
DT = 16
NST = 16
NEXP = 64
ART = 8
JT = 16
R = 128
TCH = [(0, 512), (512, 512), (1024, 128)]
TCHR = [(128, 512), (640, 512)]


class Plan:

    AUTOE = ("vector", "scalar")

    def __init__(self, nc):
        self.nc = nc
        self.ops = {e: [] for e in ("sync", "scalar", "vector", "tensor", "gpsimd")}
        self.sems = {}
        self.counts = {}
        self.auto = {e: 0 for e in self.AUTOE}
        self.autosem = {e: nc.alloc_semaphore("auto_" + e) for e in self.AUTOE}
        self.nameeng = {}
        self.namevals = {}

    _MERGE = {"vr1": "vr", "vr2": "vr", "vr3": "vr", "sr1": "sr", "sr2": "sr",
              "tt2": "tt", "tre2": "tre", "vc": "vmisc", "vps": "vmisc",
              "gb": "gx", "gbd": "gx2", "gcd": "gx3"}

    def canon(self, name):
        base = name.split("_")[0]
        return self._MERGE.get(base, base)

    def sem(self, name):
        name = self.canon(name)
        if name not in self.sems:
            self.sems[name] = self.nc.alloc_semaphore("s_" + name)
            self.counts[name] = 0
        return self.sems[name]

    def op(self, eng, opfn, sem=None, dma=False, nowait=False):
        if eng in self.AUTOE and not dma:
            idx = self.auto[eng] + 1
            self.auto[eng] = idx
            a = self.autosem[eng]
            if idx > 1 and not nowait:
                self.ops[eng].append(lambda e, a=a, v=idx - 1: e.wait_ge(a, v))
            if sem is not None:
                cn = self.canon(sem)
                self.counts.setdefault(cn, 0)
                self.counts[cn] += 1
                self.nameeng.setdefault(cn, eng)
                assert self.nameeng[cn] == eng, (cn, eng)
                self.namevals.setdefault(cn, []).append(idx)
            self.ops[eng].append(lambda e, opfn=opfn, a=a: opfn(e).then_inc(a, 1))
            return
        if sem is not None:
            s = self.sem(sem)
            cn = self.canon(sem)
            self.counts[cn] += 16 if dma else 1
            self.nameeng.setdefault(cn, None)
            self.ops[eng].append(lambda e, opfn=opfn, s=s, k=(16 if dma else 1):
                                 opfn(e).then_inc(s, k))
        else:
            self.ops[eng].append(lambda e, opfn=opfn: opfn(e))

    def wait(self, eng, sem, val=None):
        cn = self.canon(sem)
        self.counts.setdefault(cn, 0)
        v = self.counts[cn] if val is None else val
        if v <= 0:
            return

        def resolved(e, eng=eng, cn=cn, v=v):
            if self.nameeng.get(cn) in self.AUTOE:
                src_eng = self.nameeng[cn]
                vals = self.namevals[cn]
                assert v <= len(vals), (cn, v, len(vals))
                tgt = vals[v - 1]
                if eng == src_eng:
                    return
                e.wait_ge(self.autosem[src_eng], tgt)
            else:
                e.wait_ge(self.sem(cn), v)
        self.ops[eng].append(resolved)

    def val(self, sem):
        cn = self.canon(sem)
        self.counts.setdefault(cn, 0)
        return self.counts[cn]

    def build(self, blk):
        for eng_name, deco in (("sync", blk.sync), ("scalar", blk.scalar),
                               ("vector", blk.vector), ("tensor", blk.tensor),
                               ("gpsimd", blk.gpsimd)):
            ops = self.ops[eng_name]

            def section(e, ops=ops):
                for fn in ops:
                    fn(e)
            deco(section)


def build_nc(phases=7, out_xln=False, ffn_sub=8):
    nc = bass.Bass("TRN2", target_bir_lowering=False)

    d_x = nc.dram_tensor("x_fm", [D, TW], F32, kind="ExternalInput")
    d_WxTa = nc.dram_tensor("WxTa", [D, R], F32, kind="ExternalInput")
    d_WxTb = nc.dram_tensor("WxTb", [D, 32], F32, kind="ExternalInput")
    d_WdT = nc.dram_tensor("WdT", [R, D], F32, kind="ExternalInput")
    d_bdt = nc.dram_tensor("bdt_cols", [P, DT], F32, kind="ExternalInput")
    d_Alog = nc.dram_tensor("Alog_cols", [P, DT * NST], F32, kind="ExternalInput")
    d_Dp = nc.dram_tensor("D_cols", [P, DT], F32, kind="ExternalInput")
    d_AlhsT = nc.dram_tensor("A_lhsT", [D, 1024], F32, kind="ExternalInput")
    d_WT = nc.dram_tensor("W_T", [D, D], F32, kind="ExternalInput")
    d_BfT = nc.dram_tensor("Bf_T", [1024, D], F32, kind="ExternalInput")
    d_bias = nc.dram_tensor("bias_row", [1, D], F32, kind="ExternalInput")
    d_ident = nc.dram_tensor("identC", [P, P], BF16, kind="ExternalInput")
    d_ones = nc.dram_tensor("onesC", [P, 512], BF16, kind="ExternalInput")
    d_selsq = nc.dram_tensor("selsqC", [1024, NEXP], BF16, kind="ExternalInput")
    d_repr = nc.dram_tensor("reprC", [NEXP, 1024], BF16, kind="ExternalInput")
    d_selr = nc.dram_tensor("selrC", [1024, NST], BF16, kind="ExternalInput")
    d_selrep = nc.dram_tensor("selrepC", [NST, 1024], BF16, kind="ExternalInput")
    d_bmask = nc.dram_tensor("bmask", [NST, TW], BF16, kind="ExternalInput")
    d_out = nc.dram_tensor("out_fm", [D, TR], F32, kind="ExternalOutput")
    d_rows = nc.dram_tensor("rowsD", [32, TW], BF16)
    d_lnrows = nc.dram_tensor("lnrowsD", [2, TW], BF16)

    A = nc.alloc_sbuf_tensor
    x = A("x", [P, DT * TW], F32)
    xln = A("xln", [P, DT * TW], BF16)
    xbr = A("xbr", [P, 2 * TW], BF16)
    sqr = A("sqr", [P, 2 * TW], BF16)
    vecs = A("vecs", [P, ART * TW], BF16)
    U = A("U", [P, ART * TW], BF16)
    mpool = A("mpool", [P, ART * 512 + 1024], BF16)
    masked = mpool[:, 0:ART * 512]
    vrepr = mpool[:, ART * 512:ART * 512 + 1024]
    arrT = A("arrT", [P, 9 * NEXP], BF16)
    routes = A("routes", [P, 9 * NEXP], BF16)
    max8 = A("max8", [P, 9 * 8], BF16)
    thr = A("thr", [P, 16], F32)
    arr_fm = A("arr_fm", [NEXP, TW], BF16)
    vsm = A("vsm", [NST, TW], BF16)
    wbufA = A("wbufA", [P, DT * P], BF16)
    wbufB = A("wbufB", [P, DT * P], BF16)
    wbufC = A("wbufC", [P, DT * P], BF16)
    wbufD = A("wbufD", [P, DT * P], BF16)
    ubufA = A("ubufA", [P, ART * P], BF16)
    ubufB = A("ubufB", [P, ART * P], BF16)
    biasrow = A("biasrow", [1, D], BF16)
    xz_a = A("xz_a", [P, TW], BF16)
    xz_b = A("xz_b", [32, TW], BF16)
    bdt = A("bdt", [P, DT], F32)
    acols = A("acols", [P, DT * NST], F32)
    dcols = A("dcols", [P, DT], F32)
    ident = A("identS", [P, P], BF16)
    onesc = A("onesS", [P, 512], BF16)
    selsq = A("selsqS", [P, ART * NEXP], BF16)
    reprm = A("reprS", [NEXP, 1024], BF16)
    selr = A("selrS", [P, ART * NST], BF16)
    selrep = A("selrepS", [NST, 1024], BF16)
    constv = A("constv", [P, 4], F32)

    lnr = vecs[:, 0:4608].bitcast(F32)
    lnrb = vecs[:, 4608:4608 + 3 * TW]
    rstd_rep = U[:, 0:TW]
    routesT = arr_fm
    mur_rep = U[:, TW:2 * TW]
    wxa = wbufA
    wdt_w = wbufC
    wxb = ubufA
    _dA0, _dA1 = 0, TW
    _bu = 2 * TW
    _hh = 3 * TW
    _ww0, _ww1 = 4 * TW, 4 * TW + TR
    _dxt0 = 4 * TW + 2 * TR
    _dxt1 = _dxt0 + TW
    assert _dxt1 + TW <= ART * TW

    pb = [nc.alloc_psum_tensor(f"pb{i}", [P, 512], F32) for i in range(8)]

    E = Plan(nc)

    def xt(k, c0=0, n=TW):
        return x[:, k * TW + c0: k * TW + c0 + n]

    def xlnt(k, c0=0, n=TW):
        return xln[:, k * TW + c0: k * TW + c0 + n]

    E.op("sync", lambda e: e.dma_start(out=ident[:], in_=d_ident.ap()), sem="dx", dma=True)
    E.op("sync", lambda e: e.dma_start(out=onesc[:], in_=d_ones.ap()), sem="dx", dma=True)
    E.op("sync", lambda e: e.dma_start(
        out=selsq[:, 0:ART * NEXP].rearrange("p (k w) -> p k w", k=ART),
        in_=d_selsq.ap()[0:ART * P, :].rearrange("(k p) w -> p k w", p=P)),
        sem="dx", dma=True)
    E.op("sync", lambda e: e.dma_start(
        out=selr[:, 0:ART * NST].rearrange("p (k w) -> p k w", k=ART),
        in_=d_selr.ap()[0:ART * P, :].rearrange("(k p) w -> p k w", p=P)),
        sem="dx", dma=True)
    E.op("sync", lambda e: e.dma_start(out=reprm[:], in_=d_repr.ap()), sem="dx", dma=True)
    E.op("sync", lambda e: e.dma_start(out=selrep[:], in_=d_selrep.ap()), sem="dx", dma=True)
    for dst, src in [(bdt, d_bdt), (acols, d_Alog), (dcols, d_Dp)]:
        E.op("sync", lambda e, dst=dst, src=src: e.dma_start(out=dst[:], in_=src.ap()),
             sem="dx", dma=True)
    E.op("gpsimd", lambda e: e.dma_start(out=biasrow[:], in_=d_bias.ap()), sem="dbias", dma=True)
    for kg in range(8):
        E.op("sync", lambda e, kg=kg: e.dma_start(
            out=x[:, kg * 2 * TW:(kg + 1) * 2 * TW].rearrange(
                "p (k t) -> p k t", k=2),
            in_=d_x.ap()[kg * 2 * P:(kg + 1) * 2 * P, :].rearrange(
                "(k p) t -> p k t", p=P)),
            sem=f"dxg{kg}", dma=True)
    dx_all = E.val("dx")

    E.op("vector", lambda e: e.memset(constv[:, 0:1], 1e-5))
    E.op("vector", lambda e: e.memset(constv[:, 1:2], -0.5))
    E.op("vector", lambda e: e.memset(constv[:, 2:3], 1.0), sem="vcst")
    E.wait("scalar", "dx", dx_all)
    E.wait("scalar", "vcst")
    E.op("scalar", lambda e: e.activation(acols[:], acols[:], AF.Exp), sem="sAx")
    E.wait("vector", "sAx")
    E.op("vector", lambda e: e.tensor_scalar_mul(acols[:], acols[:], -1.0), sem="sA")

    def emit_strip(src, col0, width, dst, sem, rows_tiles=DT):
        h = rows_tiles // 2
        for k0, kn in ((0, h), (h, rows_tiles - h)):
            E.op("gpsimd", lambda e, src=src, col0=col0, width=width, dst=dst,
                 k0=k0, kn=kn: e.dma_start(
                     out=dst[:, k0 * width:(k0 + kn) * width].rearrange(
                         "p (k w) -> p k w", k=kn),
                     in_=src.ap()[k0 * P:(k0 + kn) * P, col0:col0 + width].rearrange(
                         "(k p) w -> p k w", p=P)),
                 sem=sem, dma=True)

    _ln_marks = {}
    def ln_pass(tag, tok0, ntok, xdx=False):
        tch = []
        c = tok0
        while c < tok0 + ntok:
            n = min(512, tok0 + ntok - c)
            tch.append((c, n))
            c += n
        nch = len(tch)
        tst0 = E.val("tst_" + tag)
        E.wait("vector", "tv")
        E.wait("vector", "tb")
        E.wait("vector", "txz")
        E.wait("vector", "td")
        for k in range(DT):
            sl = k % 2
            if xdx:
                E.wait("vector", f"dxg{k // 2}", 16)
            E.op("vector", lambda e, k=k: e.tensor_copy(
                xlnt(k, tok0, ntok), xt(k, tok0, ntok)), sem="vc_" + tag,
                nowait=(k >= 1))
            E.wait("scalar", "vc_" + tag)
            if k >= 2:
                E.wait("scalar", "tst_" + tag, tst0 + (k - 1) * 2 * nch)
            E.op("scalar", lambda e, k=k, sl=sl: e.activation(
                sqr[:, sl * TW:sl * TW + ntok], xlnt(k, tok0, ntok),
                AF.Square), sem="ss_" + tag)
            E.wait("tensor", "ss_" + tag)
            for ci, (c0, n) in enumerate(tch):
                E.op("tensor", lambda e, k=k, ci=ci, c0=c0, n=n: e.matmul(
                    pb[2 + ci][0:1, 0:n], onesc[0:P, 0:1],
                    xlnt(k, c0, n),
                    start=(k == 0), stop=(k == DT - 1)), sem="tst_" + tag)
                E.op("tensor", lambda e, k=k, sl=sl, ci=ci, c0=c0, n=n: e.matmul(
                    pb[5 + ci][0:1, 0:n], onesc[0:P, 0:1],
                    sqr[:, sl * TW + c0 - tok0:sl * TW + c0 - tok0 + n],
                    start=(k == 0), stop=(k == DT - 1)), sem="tst_" + tag)
        E.wait("vector", "tst_" + tag)
        for ci, (c0, n) in enumerate(tch):
            E.op("vector", lambda e, ci=ci, c0=c0, n=n: e.tensor_copy(
                lnr[0:1, c0:c0 + n], pb[2 + ci][0:1, 0:n]), nowait=(ci >= 1))
            E.op("vector", lambda e, ci=ci, c0=c0, n=n: e.tensor_copy(
                lnr[0:1, TW + c0:TW + c0 + n], pb[5 + ci][0:1, 0:n]), sem="vps_" + tag,
                nowait=True)
        mur = lnr[0:1, tok0:tok0 + ntok]
        varr = lnr[0:1, TW + tok0:TW + tok0 + ntok]
        mu2b = lnrb[0:1, tok0:tok0 + ntok]
        rstdb = lnrb[0:1, TW + tok0:TW + tok0 + ntok]
        murb = lnrb[0:1, 2 * TW + tok0:2 * TW + tok0 + ntok]
        E.op("vector", lambda e: e.tensor_scalar_mul(mur, mur, 1.0 / D))
        E.op("vector", lambda e: e.tensor_scalar_mul(varr, varr, 1.0 / D), sem="vr1_" + tag,
             nowait=True)
        E.wait("scalar", "vr1_" + tag)
        E.op("scalar", lambda e: e.activation(mu2b, mur, AF.Square), sem="sr1_" + tag)
        E.wait("vector", "sr1_" + tag)
        E.op("vector", lambda e: e.tensor_tensor(varr, varr, mu2b, ALU.subtract),
             sem="vr2_" + tag)
        E.wait("scalar", "vr2_" + tag)
        E.op("scalar", lambda e: e.activation(varr, varr, AF.Ln, bias=constv[0:1, 0:1]))
        E.op("scalar", lambda e: e.activation(varr, varr, AF.Exp, scale=constv[0:1, 1:2]),
             sem="sr2_" + tag)
        E.wait("vector", "sr2_" + tag)
        E.op("vector", lambda e: e.tensor_copy(rstdb, varr))
        E.op("vector", lambda e: e.tensor_tensor(mur, mur, varr, ALU.mult))
        E.op("vector", lambda e: e.tensor_copy(murb, mur), sem="vr3_" + tag)
        E.wait("tensor", "vr3_" + tag)
        for ci, (c0, n) in enumerate(tch):
            E.op("tensor", lambda e, ci=ci, c0=c0, n=n: e.matmul(
                pb[2 + ci][:, 0:n], onesc[0:1, 0:P],
                lnrb[0:1, TW + c0:TW + c0 + n], start=True, stop=True),
                sem="trp_" + tag)
            E.op("tensor", lambda e, ci=ci, c0=c0, n=n: e.matmul(
                pb[5 + ci][:, 0:n], onesc[0:1, 0:P],
                lnrb[0:1, 2 * TW + c0:2 * TW + c0 + n], start=True, stop=True),
                sem="trp_" + tag)
        trp0 = E.val("trp_" + tag) - 2 * nch
        _ln_marks[tag] = (E.val("ln_" + tag), tch)
        for ci, (c0, n) in enumerate(tch):
            E.wait("vector", "trp_" + tag, trp0 + 2 * (ci + 1))
            E.op("vector", lambda e, ci=ci, c0=c0, n=n: e.tensor_copy(
                rstd_rep[:, c0:c0 + n], pb[2 + ci][:, 0:n]), nowait=True)
            E.op("vector", lambda e, ci=ci, c0=c0, n=n: e.tensor_copy(
                mur_rep[:, c0:c0 + n], pb[5 + ci][:, 0:n]), sem="vrp_" + tag,
                nowait=True)
            for k in range(DT):
                E.op("vector", lambda e, k=k, c0=c0, n=n: e.tensor_tensor(
                    xlnt(k, c0, n), xlnt(k, c0, n), rstd_rep[:, c0:c0 + n],
                    ALU.mult), nowait=(k >= 1))
                E.op("vector", lambda e, k=k, c0=c0, n=n: e.tensor_tensor(
                    xlnt(k, c0, n), xlnt(k, c0, n), mur_rep[:, c0:c0 + n],
                    ALU.subtract), sem="ln_" + tag)

    def ffn_pass(tag, tok0, ntok, sub=8, lntag=None):
        tch = []
        c = tok0
        while c < tok0 + ntok:
            n = min(512, tok0 + ntok - c)
            tch.append((c, n))
            c += n
        nch = len(tch)
        tiles = ntok // P
        t0i = tok0 // P
        tv0 = E.val("tv_" + tag)
        ve0 = E.val("ve_" + tag)
        taq0 = E.val("taq_" + tag)
        vat0 = E.val("vat_" + tag)
        vrt0 = E.val("vrt_" + tag)
        vU0 = E.val("vU_" + tag)
        tb0 = E.val("tb_" + tag)
        if sub <= 0:
            for art in range(ART):
                dstA = wbufA if art % 2 == 0 else wbufB
                emit_strip(d_AlhsT, art * P, P, dstA, f"wA{art % 2}_" + tag)
            return
        gidx = 0
        for art in range(ART):
            dstA = wbufA if art % 2 == 0 else wbufB
            if art >= 2:
                E.wait("gpsimd", "tv_" + tag, tv0 + (art - 1) * nch)
            emit_strip(d_AlhsT, art * P, P, dstA, f"wA{art % 2}_" + tag)
            E.wait("tensor", f"wA{art % 2}_" + tag)
            for ci, (c0, n) in enumerate(tch):
                bank = pb[gidx % 4]
                if gidx >= 4:
                    E.wait("tensor", "ve_" + tag, ve0 + gidx - 3)
                for k in range(DT):
                    if art == 0 and lntag is not None:
                        E.wait("tensor", "ln_" + tag,
                               _ln_marks[lntag][0] + DT * ci + k + 1)
                    E.op("tensor", lambda e, dstA=dstA, k=k, c0=c0, n=n, bank=bank:
                         e.matmul(bank[:, 0:n], dstA[:, k * P:(k + 1) * P],
                                  xlnt(k, c0, n), start=(k == 0), stop=(k == DT - 1)),
                         sem=("tv_" + tag) if k == DT - 1 else None)
                E.wait("scalar", "tv_" + tag)
                E.op("scalar", lambda e, art=art, c0=c0, n=n, bank=bank: e.activation(
                    vecs[:, art * TW + c0:art * TW + c0 + n], bank[:, 0:n], AF.Copy),
                    sem="ve_" + tag, nowait=True)
                gidx += 1
        if sub <= 1:
            return
        E.wait("tensor", "ve_" + tag)
        for k in range(ART):
            E.wait("scalar", "ve_" + tag, ve0 + (k + 1) * nch)
            if k >= 2:
                E.wait("scalar", "taq_" + tag, taq0 + (k - 1) * nch)
            E.op("scalar", lambda e, k=k: e.activation(
                sqr[:, (k % 2) * TW + tok0:(k % 2) * TW + tok0 + ntok],
                vecs[:, k * TW + tok0:k * TW + tok0 + ntok], AF.Square),
                sem="sq_" + tag, nowait=True)
            E.wait("tensor", "sq_" + tag)
            for ci, (c0, n) in enumerate(tch):
                E.op("tensor", lambda e, k=k, ci=ci, c0=c0, n=n: e.matmul(
                    pb[2 + ci][0:NEXP, 0:n], selsq[:, k * NEXP:(k + 1) * NEXP],
                    sqr[:, (k % 2) * TW + c0 - tok0 + tok0:(k % 2) * TW + c0 - tok0 + tok0 + n],
                    start=(k == 0), stop=(k == ART - 1)), sem="taq_" + tag)
        E.wait("scalar", "taq_" + tag)
        for ci, (c0, n) in enumerate(tch):
            E.op("scalar", lambda e, ci=ci, c0=c0, n=n: e.activation(
                arr_fm[0:NEXP, c0:c0 + n], pb[2 + ci][0:NEXP, 0:n], AF.Copy),
                sem="va_" + tag)
        if sub <= 2:
            return
        E.wait("tensor", "va_" + tag)
        for i, ti in enumerate(range(t0i, t0i + tiles)):
            E.op("tensor", lambda e, i=i, ti=ti: e.transpose(
                pb[3].bitcast(BF16)[0:P, i * NEXP:(i + 1) * NEXP],
                arr_fm[0:NEXP, ti * P:(ti + 1) * P],
                ident[0:NEXP, 0:NEXP]),
                sem=("tt_" + tag) if i == tiles - 1 else None)
        E.wait("vector", "tt_" + tag)
        E.op("vector", lambda e: e.tensor_copy(
            arrT[:, t0i * NEXP:(t0i + tiles) * NEXP],
            pb[3].bitcast(BF16)[0:P, 0:tiles * NEXP]), sem="vat_" + tag)
        if sub <= 3:
            return
        E.wait("vector", "vat_" + tag)
        for i, ti in enumerate(range(t0i, t0i + tiles)):
            E.op("vector", lambda e, ti=ti: e.max(
                max8[:, ti * 8:(ti + 1) * 8], arrT[:, ti * NEXP:(ti + 1) * NEXP]),
                nowait=(i >= 1))
        for i, ti in enumerate(range(t0i, t0i + tiles)):
            E.op("vector", lambda e, ti=ti: e.tensor_copy(
                thr[:, ti:ti + 1], max8[:, ti * 8 + 3:ti * 8 + 4]),
                nowait=(i >= 1))
        for i, ti in enumerate(range(t0i, t0i + tiles)):
            E.op("vector", lambda e, ti=ti: e.tensor_scalar(
                routes[:, ti * NEXP:(ti + 1) * NEXP], arrT[:, ti * NEXP:(ti + 1) * NEXP],
                thr[:, ti:ti + 1], 0.25, ALU.is_ge, ALU.mult), sem="vro_" + tag,
                nowait=(i >= 1))
        if sub <= 4:
            return
        E.wait("tensor", "vro_" + tag)
        E.wait("tensor", "vat_" + tag)
        for i, ti in enumerate(range(t0i, t0i + tiles)):
            bk, bo = (pb[3], i) if i < 8 else (pb[2], i - 8)
            E.op("tensor", lambda e, ti=ti, bk=bk, bo=bo: e.transpose(
                bk.bitcast(BF16)[0:NEXP, bo * P:(bo + 1) * P],
                routes[:, ti * NEXP:(ti + 1) * NEXP],
                ident[:]), sem=("tt2_" + tag) if i == tiles - 1 else None)
        E.wait("vector", "tt2_" + tag)
        E.op("vector", lambda e: e.tensor_copy(
            routesT[0:NEXP, t0i * P:(t0i + min(tiles, 8)) * P],
            pb[3].bitcast(BF16)[0:NEXP, 0:min(tiles, 8) * P]), sem="vrt_" + tag)
        if tiles > 8:
            E.op("vector", lambda e: e.tensor_copy(
                routesT[0:NEXP, (t0i + 8) * P:(t0i + tiles) * P],
                pb[2].bitcast(BF16)[0:NEXP, 0:(tiles - 8) * P]), sem="vrt_" + tag)
        if sub <= 5:
            return
        E.wait("tensor", "vrt_" + tag)
        vm0 = E.val("vm_" + tag)
        midx = 0
        for ci, (c0, n) in enumerate(tch):
            for art in range(ART):
                rb = pb[midx % 4]
                if midx >= 4:
                    E.wait("tensor", "vm_" + tag, vm0 + midx - 3)
                E.op("tensor", lambda e, art=art, c0=c0, n=n, rb=rb: e.matmul(
                    rb[:, 0:n], reprm[0:NEXP, art * P:(art + 1) * P],
                    routesT[0:NEXP, c0:c0 + n], start=True, stop=True), sem="tre_" + tag)
                E.wait("vector", "tre_" + tag)
                E.op("vector", lambda e, art=art, c0=c0, n=n, rb=rb: e.scalar_tensor_tensor(
                    masked[:, art * 512:art * 512 + n], rb[:, 0:n], 1.0,
                    vecs[:, art * TW + c0:art * TW + c0 + n], ALU.mult, ALU.mult),
                    sem="vm_" + tag, nowait=True)
                midx += 1
            E.wait("tensor", "vm_" + tag)
            E.wait("tensor", "vvs_" + tag)
            for k in range(ART):
                E.op("tensor", lambda e, k=k, c0=c0, n=n: e.matmul(
                    pb[4][0:NST, 0:n], selr[:, k * NST:(k + 1) * NST],
                    masked[:, k * 512:k * 512 + n],
                    start=(k == 0), stop=(k == ART - 1)),
                    sem=("tvv_" + tag) if k == ART - 1 else None)
            E.wait("scalar", "tvv_" + tag)
            E.op("scalar", lambda e, c0=c0, n=n: e.activation(
                vsm[0:NST, c0:c0 + n], pb[4][0:NST, 0:n], AF.Copy), sem="vvs_" + tag,
                nowait=True)
        if sub <= 6:
            return
        E.wait("tensor", "vvs_" + tag)
        vvr0 = E.val("vvr_" + tag)
        vslot = (vrepr[:, 0:512], vrepr[:, 512:1024], masked[:, 0:512])
        for art in range(ART):
            for ci, (c0, n) in enumerate(tch):
                idx = art * nch + ci
                sl = idx % 3
                vb = pb[4 + idx % 4]
                rb = pb[idx % 4]
                if idx >= 4:
                    E.wait("tensor", "vvr_" + tag, vvr0 + idx - 3)
                    E.wait("tensor", "vU_" + tag, vU0 + idx - 3)
                E.op("tensor", lambda e, art=art, c0=c0, n=n, vb=vb: e.matmul(
                    vb[:, 0:n], selrep[0:NST, art * P:(art + 1) * P],
                    vsm[0:NST, c0:c0 + n], start=True, stop=True))
                E.op("tensor", lambda e, art=art, c0=c0, n=n, rb=rb: e.matmul(
                    rb[:, 0:n], reprm[0:NEXP, art * P:(art + 1) * P],
                    routesT[0:NEXP, c0:c0 + n], start=True, stop=True), sem="tre2_" + tag)
                E.wait("scalar", "tre2_" + tag)
                if idx >= 3:
                    E.wait("scalar", "vU_" + tag, vU0 + idx - 2)
                E.op("scalar", lambda e, sl=sl, n=n, vb=vb: e.activation(
                    vslot[sl][:, 0:n], vb[:, 0:n], AF.Copy),
                    sem="vvr_" + tag, nowait=True)
                E.wait("vector", "vvr_" + tag, vvr0 + idx + 1)
                E.wait("vector", "tre2_" + tag)
                E.op("vector", lambda e, art=art, c0=c0, n=n, sl=sl, rb=rb:
                     e.scalar_tensor_tensor(
                         U[:, art * TW + c0:art * TW + c0 + n], rb[:, 0:n], 1.0,
                         vslot[sl][:, 0:n], ALU.mult, ALU.mult),
                     sem="vU_" + tag, nowait=True)
        if sub <= 7:
            return
        jidx = 0
        E.wait("tensor", "vU_" + tag)
        E.wait("tensor", "vvr_" + tag)
        E.wait("tensor", "dbias", 16)
        for j in range(JT):
            dstW = wbufC if j % 2 == 0 else wbufD
            dstU = ubufA if j % 2 == 0 else ubufB
            if j >= 2:
                E.wait("gpsimd", "tb_" + tag, tb0 + (j - 1) * nch)
            emit_strip(d_WT, j * P, P, dstW, f"wW{j % 2}_" + tag)
            emit_strip(d_BfT, j * P, P, dstU, f"wB{j % 2}_" + tag, rows_tiles=ART)
            E.wait("tensor", f"wW{j % 2}_" + tag)
            E.wait("tensor", f"wB{j % 2}_" + tag)
            if j >= 2:
                E.wait("tensor", "vres_" + tag, (j - 1) * nch)
            for ci, (c0, n) in enumerate(tch):
                bank = pb[4 + jidx % 4]
                jidx += 1
                E.wait("tensor", "vres_" + tag, max(0, E.val("vres_" + tag) - 3))
                for k in range(DT):
                    E.op("tensor", lambda e, dstW=dstW, k=k, c0=c0, n=n, bank=bank:
                         e.matmul(bank[:, 0:n], dstW[:, k * P:(k + 1) * P],
                                  xlnt(k, c0, n), start=(k == 0), stop=False))
                for k in range(ART):
                    E.op("tensor", lambda e, dstU=dstU, k=k, c0=c0, n=n, bank=bank:
                         e.matmul(bank[:, 0:n], dstU[:, k * P:(k + 1) * P],
                                  U[:, k * TW + c0:k * TW + c0 + n],
                                  start=False, stop=False))
                E.op("tensor", lambda e, j=j, c0=c0, n=n, bank=bank: e.matmul(
                    bank[:, 0:n], biasrow[0:1, j * P:(j + 1) * P],
                    onesc[0:1, 0:n], start=False, stop=True), sem="tb_" + tag)
                E.wait("vector", "tb_" + tag)
                E.op("vector", lambda e, j=j, c0=c0, n=n, bank=bank:
                     e.scalar_tensor_tensor(
                         xt(j, c0, n), bank[:, 0:n], 0.5, xt(j, c0, n),
                         ALU.mult, ALU.add), sem="vres_" + tag, nowait=True)

    def _mamba(E):
        E.wait("gpsimd", "tb_f1")
        E.wait("gpsimd", "tv_f1")
        E.op("gpsimd", lambda e: e.dma_start(
            out=wxa[:, 0:DT * R].rearrange("p (k w) -> p k w", k=DT),
            in_=d_WxTa.ap()[0:DT * P, :].rearrange("(k p) w -> p k w", p=P)),
            sem="dm", dma=True)
        E.op("gpsimd", lambda e: e.dma_start(
            out=wxb[:, 0:DT * 32].rearrange("p (k w) -> p k w", k=DT),
            in_=d_WxTb.ap()[0:DT * P, :].rearrange("(k p) w -> p k w", p=P)),
            sem="dm", dma=True)
        E.op("gpsimd", lambda e: e.dma_start(
            out=wdt_w[:, 0:DT * P], in_=d_WdT.ap()[:, 0:DT * P]),
            sem="dm", dma=True)
        dm_all = E.val("dm")

        E.wait("tensor", "dm", dm_all)
        for ci, (c0, n) in enumerate(TCH):
            if ci >= 1:
                E.wait("tensor", "vxz", 2 * ci)
            for k in range(DT):
                E.wait("tensor", "ln_l2", _ln_marks["l2"][0] + DT * ci + k + 1)
                E.op("tensor", lambda e, k=k, c0=c0, n=n: e.matmul(
                    pb[0][:, 0:n], wxa[:, k * R:(k + 1) * R], xlnt(k, c0, n),
                    start=(k == 0), stop=(k == DT - 1)))
            for k in range(DT):
                E.op("tensor", lambda e, k=k, c0=c0, n=n: e.matmul(
                    pb[1][0:32, 0:n], wxb[:, k * 32:(k + 1) * 32], xlnt(k, c0, n),
                    start=(k == 0), stop=(k == DT - 1)),
                    sem="txz" if k == DT - 1 else None)
            E.wait("vector", "txz")
            E.op("vector", lambda e, c0=c0, n=n: e.tensor_copy(xz_a[:, c0:c0 + n],
                                                               pb[0][:, 0:n]), sem="vxz")
            E.op("vector", lambda e, c0=c0, n=n: e.tensor_copy(xz_b[0:32, c0:c0 + n],
                                                               pb[1][0:32, 0:n]), sem="vxz",
                 nowait=True)
        E.wait("sync", "tre")
        E.op("sync", lambda e: e.dma_start(out=vsm[0:NST, :], in_=d_bmask.ap()),
             sem="dxm", dma=True)
        E.wait("vector", "dxm", 16)
        E.op("vector", lambda e: e.tensor_tensor(
            xz_b[0:NST, :], xz_b[0:NST, :], vsm[0:NST, :], ALU.mult), sem="vxz")
        E.wait("gpsimd", "vxz")
        E.op("gpsimd", lambda e: e.dma_start(out=d_rows.ap(), in_=xz_b[:]),
             sem="gbd", dma=True)

        E.wait("scalar", "sA")
        vbu_cnt = 0
        vw_cnt = 0
        vh_cnt = 0
        bq_emitted = 0
        cp_emitted = 0
        td_cnt = 0
        sd_cnt = 0

        def bq_load(g):
            nonlocal bq_emitted
            q = g % 4
            buf = g % 2
            E.wait("gpsimd", "gbd", 16)
            if g >= 2:
                E.wait("gpsimd", "vbu", (g - 1) * 8)
            E.op("gpsimd", lambda e, q=q, buf=buf: e.dma_start(
                out=vecs[:, buf * 4 * TW:(buf + 1) * 4 * TW],
                in_=d_rows.ap()[4 * q:4 * q + 4, :].unsqueeze(0).broadcast_to((P, 4, TW))),
                sem=f"dBq{buf}", dma=True)
            bq_emitted += 1

        def cp_load(h):
            nonlocal cp_emitted
            p = h % 8
            dst = xbr if h % 2 == 0 else sqr
            E.wait("gpsimd", "gbd", 16)
            for r in range(2):
                if h >= 2:
                    E.wait("gpsimd", "vw", (h - 2) * 4 + 2 * (r + 1))
                E.op("gpsimd", lambda e, p=p, dst=dst, r=r: e.dma_start(
                    out=dst[:, r * TW:(r + 1) * TW].unsqueeze(1),
                    in_=d_rows.ap()[16 + 2 * p + r:16 + 2 * p + r + 1, :]
                    .unsqueeze(0).broadcast_to((P, 1, TW))),
                    sem=f"dCp{h % 2}r{r}", dma=True)
            cp_emitted += 1

        deltaf = mpool.bitcast(F32)
        tdi = 0

        def emit_td(dtb_t, lo=0, hi=6):
            nonlocal td_cnt
            items = [(k, c0, n) for k in (2 * dtb_t, 2 * dtb_t + 1)
                     for (c0, n) in TCH]
            for k, c0, n in items[lo:hi]:
                tdb = pb[2 + td_cnt % 2]
                if td_cnt > 1:
                    E.wait("tensor", "sd", td_cnt - 1)
                E.op("tensor", lambda e, k=k, c0=c0, n=n, tdb=tdb: e.matmul(
                    tdb[:, 0:n], wdt_w[:, k * P:(k + 1) * P], xz_a[:, c0:c0 + n],
                    start=True, stop=True), sem="td")
                td_cnt += 1

        E.wait("tensor", "vps_l2")
        E.wait("tensor", "vxz")
        bq_load(0)
        bq_load(1)
        cp_load(0)
        cp_load(1)

        emit_td(0)
        for dtb in range(8):
            dts = (2 * dtb, 2 * dtb + 1)
            if dtb >= 1:
                emit_td(dtb, lo=2)
            for pi, k in enumerate(dts):
                sl = pi
                if dtb >= 1 and pi == 0:
                    E.wait("scalar", "vdx", 2 * dtb)
                for ci, (c0, n) in enumerate(TCH):
                    tdi += 1
                    E.wait("scalar", "td", tdi)
                    tdb = pb[2 + (tdi - 1) % 2]
                    E.op("scalar", lambda e, k=k, sl=sl, c0=c0, n=n, tdb=tdb: e.activation(
                        deltaf[:, sl * TW + c0:sl * TW + c0 + n], tdb[:, 0:n], AF.Exp,
                        bias=bdt[:, k:k + 1]), nowait=True)
                    E.op("scalar", lambda e, sl=sl, c0=c0, n=n: e.activation(
                        deltaf[:, sl * TW + c0:sl * TW + c0 + n],
                        deltaf[:, sl * TW + c0:sl * TW + c0 + n], AF.Ln, bias=constv[:, 2:3]),
                        sem="sd")
                    sd_cnt += 1
                E.wait("vector", "sd", sd_cnt)
                dxoff = _dxt0 if sl == 0 else _dxt1
                E.op("vector", lambda e, sl=sl, k=k, dxoff=dxoff: e.tensor_tensor(
                    U[:, dxoff:dxoff + TW], deltaf[:, sl * TW:(sl + 1) * TW], xlnt(k),
                    ALU.mult), sem="vdx", nowait=True)
            banks = {0: (pb[4], pb[5]), 1: (pb[6], pb[7])}
            for pi, k in enumerate(dts):
                if dtb >= 2 and pi == 0:
                    E.wait("vector", "tns", 32 * (dtb - 1))
                E.op("vector", lambda e, pi=pi, k=k, dtb=dtb: e.tensor_scalar_mul(
                    mpool[:, 4608 + (dtb % 2) * 256 + pi * P:
                           4608 + (dtb % 2) * 256 + (pi + 1) * P],
                    ident[:], dcols[:, k:k + 1]), sem="vdg")
            if dtb >= 1:
                E.wait("tensor", "vy", dtb * 2)
            for ni in range(NST):
                q = ni // 4
                g = dtb * 4 + q
                h = dtb * 8 + ni // 2
                if ni % 2 == 0 and h + 2 < 64:
                    cp_load(h + 2)
                if ni % 4 == 1 and g + 2 < 32:
                    bq_load(g + 2)
                for pi, k in enumerate(dts):
                    step = (dtb * NST + ni) * 2 + pi
                    dab = (U[:, _dA0:_dA0 + TW], U[:, _dA1:_dA1 + TW],
                           wbufD[:, 0:TW])[step % 3]
                    if step >= 3:
                        E.wait("scalar", "vh", step - 2)
                    E.op("scalar", lambda e, k=k, ni=ni, dab=dab, pi=pi: e.activation(
                        dab, deltaf[:, pi * TW:(pi + 1) * TW], AF.Exp,
                        scale=acols[:, k * NST + ni:k * NST + ni + 1]), sem="sdA",
                        nowait=(step % 32 != 0))
                    dxoff = _dxt0 if pi == 0 else _dxt1
                    if ni % 4 == 0 and pi == 0:
                        E.wait("vector", f"dBq{g % 2}", 16 * (g // 2 + 1))
                    boff = (g % 2) * 4 * TW + (ni % 4) * TW
                    E.op("vector", lambda e, dxoff=dxoff, boff=boff: e.tensor_tensor(
                        U[:, _bu:_bu + TW], U[:, dxoff:dxoff + TW],
                        vecs[:, boff:boff + TW], ALU.mult), sem="vbu",
                        nowait=(step % 32 != 0))
                    vbu_cnt += 1
                    E.wait("vector", "sdA", step + 1)
                    E.op("vector", lambda e, dab=dab: e.tensor_tensor_scan(
                        U[:, _hh:_hh + TW], dab, U[:, _bu:_bu + TW],
                        0.0, ALU.mult, ALU.add), sem="vh")
                    vh_cnt += 1
                    wsl = step % 2
                    woff = _ww0 if wsl == 0 else _ww1
                    cbuf = xbr if h % 2 == 0 else sqr
                    coff = (ni % 2) * TW
                    if pi == 0:
                        E.wait("vector", f"dCp{h % 2}r{ni % 2}",
                               16 * (h // 2 + 1))
                    if step >= 2:
                        E.wait("vector", "tns", step - 1)
                    E.op("vector", lambda e, woff=woff, cbuf=cbuf, coff=coff: e.tensor_tensor(
                        U[:, woff:woff + TR], U[:, _hh + WU:_hh + TW],
                        cbuf[:, coff + WU:coff + TW], ALU.mult), sem="vw")
                    vw_cnt += 1
                    E.wait("tensor", "vw", vw_cnt)
                    b0, b1 = banks[pi]
                    E.op("tensor", lambda e, woff=woff, ni=ni, b0=b0: e.matmul(
                        b0[:, 0:512], ident[:], U[:, woff:woff + 512],
                        start=(ni == 0), stop=(ni == NST - 1)))
                    E.op("tensor", lambda e, woff=woff, ni=ni, b1=b1: e.matmul(
                        b1[:, 0:512], ident[:], U[:, woff + 512:woff + 1024],
                        start=(ni == 0), stop=(ni == NST - 1)), sem="tns")
                    if ni == 0:
                        E.wait("tensor", "vdg", E.val("vdg"))
                        dg = mpool[:, 4608 + (dtb % 2) * 256 + pi * P:
                                   4608 + (dtb % 2) * 256 + (pi + 1) * P]
                        E.op("tensor", lambda e, b0=b0, dg=dg, k=k: e.matmul(
                            b0[:, 0:512], dg, xlnt(k, WU, 512),
                            start=False, stop=False))
                        E.op("tensor", lambda e, b1=b1, dg=dg, k=k: e.matmul(
                            b1[:, 0:512], dg, xlnt(k, WU + 512, 512),
                            start=False, stop=False))
                    if ni == 15 and pi == 1 and dtb < 7:
                        emit_td(dtb + 1, hi=2)
            E.wait("vector", "tns", 32 * (dtb + 1))
            for pi, k in enumerate(dts):
                b0, b1 = banks[pi]
                E.op("vector", lambda e, k=k, b0=b0: e.scalar_tensor_tensor(
                    xt(k, WU, 512), b0[:, 0:512], 1.0, xt(k, WU, 512), ALU.mult, ALU.add),
                    nowait=True)
                E.op("vector", lambda e, k=k, b1=b1: e.scalar_tensor_tensor(
                    xt(k, WU + 512, 512), b1[:, 0:512], 1.0, xt(k, WU + 512, 512),
                    ALU.mult, ALU.add), sem="vy", nowait=True)

        E.wait("vector", "vy")

    E.wait("tensor", "dx", dx_all)
    if phases >= 1:
        ln_pass("l1", 0, TW, xdx=True)
    if phases >= 2:
        ffn_pass("f1", 0, TW, sub=ffn_sub, lntag="l1")
        if ffn_sub >= 8:
            E.wait("vector", "vres_f1")
    if phases >= 3:
        ln_pass("l2", 0, TW)

    if phases >= 4:
        _mamba(E)
    if phases >= 6:
        ln_pass("l3", WU, TR)
    if phases >= 7:
        E.wait("gpsimd", "txz")
        E.wait("gpsimd", "td", 48)
        vres_pre_f2 = E.val("vres_f2")
        ffn_pass("f2", WU, TR, lntag="l3")

    if phases == 2 and ffn_sub < 8:
        lastsem = {0: "wA0_f1", 1: "ve_f1", 2: "va_f1", 3: "vat_f1", 4: "vro_f1",
                   5: "vrt_f1", 6: "vvs_f1", 7: "vU_f1"}[ffn_sub]
    else:
        lastsem = {0: "dx", 1: "ln_l1", 2: "vres_f1", 3: "ln_l2", 4: "vxz",
                   5: "vy", 6: "ln_l3", 7: "vres_f2"}[min(phases, 7)]
    if out_xln:
        E.wait("sync", lastsem)
        for k in range(DT):
            E.op("sync", lambda e, k=k: e.dma_start(
                out=d_out.ap().bitcast(BF16)[k * P:(k + 1) * P, 0:TR],
                in_=xlnt(k, WU, TR)), sem="dout", dma=True)
    elif phases >= 7:
        for k in range(DT):
            E.wait("sync", "vres_f2", vres_pre_f2 + (k + 1) * 2)
            E.op("sync", lambda e, k=k: e.dma_start(
                out=d_out.ap()[k * P:(k + 1) * P, :], in_=xt(k, WU, TR)),
                sem="dout", dma=True)
    else:
        E.wait("sync", lastsem)
        E.op("sync", lambda e: e.dma_start(
            out=d_out.ap()[0:D, :].rearrange("(k p) t -> p k t", p=P),
            in_=x[:, 0:DT * TW].rearrange("p (k t) -> p k t", k=DT)[:, :, WU:WU + TR]),
            sem="dout", dma=True)
    E.wait("sync", "dout")

    with nc.Block() as blk:
        E.build(blk)
    return nc





_NC_CACHE = {}


def _get_nc():
    if "nc" not in _NC_CACHE:
        _NC_CACHE["nc"] = build_nc()
    return _NC_CACHE["nc"]


def _selsq():
    s = np.zeros((1024, NEXP), dtype=ml_dtypes.bfloat16)
    for i in range(1024):
        s[i, i // NST] = 1
    return s


def _repr_mat():
    s = np.zeros((NEXP, 1024), dtype=ml_dtypes.bfloat16)
    for j in range(1024):
        s[j // NST, j] = 1
    return s


def _selr():
    s = np.zeros((1024, NST), dtype=ml_dtypes.bfloat16)
    for i in range(1024):
        s[i, i % NST] = 1
    return s


def _selrep():
    s = np.zeros((NST, 1024), dtype=ml_dtypes.bfloat16)
    for j in range(1024):
        s[j % NST, j] = 1
    return s


def _host_inputs(inputs):
    x = np.asarray(inputs["L0_attn_x"])
    Wx = np.asarray(inputs["L0_attn_W_xproj"])
    Wd = np.asarray(inputs["L0_attn_W_dt"])
    bdt = np.asarray(inputs["L0_attn_b_dt"])
    Alog = np.asarray(inputs["L0_attn_A_log"])
    Dp = np.asarray(inputs["L0_attn_D_param"])
    A = np.asarray(inputs["L0_ffn_A"])
    B = np.asarray(inputs["L0_ffn_B"])
    W = np.asarray(inputs["L0_ffn_W"])
    bias = np.asarray(inputs["L0_ffn_bias"])

    shared = {
        "WxTa": np.ascontiguousarray(Wx[:R, :].T),
        "WxTb": np.ascontiguousarray(Wx[R:, :].T),
        "WdT": np.ascontiguousarray(Wd.T),
        "bdt_cols": np.ascontiguousarray(bdt.reshape(DT, P).T),
        "Alog_cols": np.ascontiguousarray(
            Alog.reshape(DT, P, NST).transpose(1, 0, 2).reshape(P, DT * NST)),
        "D_cols": np.ascontiguousarray(Dp.reshape(DT, P).T),
        "A_lhsT": np.ascontiguousarray(A.transpose(2, 0, 1).reshape(D, 1024)),
        "W_T": np.ascontiguousarray(W.T),
        "Bf_T": np.ascontiguousarray(B.transpose(0, 2, 1).reshape(1024, D)),
        "bias_row": np.ascontiguousarray(bias.reshape(1, D)),
        "identC": np.eye(P, dtype=ml_dtypes.bfloat16),
        "onesC": np.ones((P, 512), dtype=ml_dtypes.bfloat16),
        "selsqC": _selsq(),
        "reprC": _repr_mat(),
        "selrC": _selr(),
        "selrepC": _selrep(),
    }
    in_maps = []
    for c in range(8):
        b, q = c // 4, c % 4
        t0 = q * TR
        if q == 0:
            xs = np.concatenate([np.zeros((WU, D), np.float32), x[b, 0:TR]], axis=0)
        else:
            xs = x[b, t0 - WU:t0 + TR]
        m = dict(shared)
        m["x_fm"] = np.ascontiguousarray(xs.T)
        bm = np.ones((NST, TW), dtype=ml_dtypes.bfloat16)
        if q == 0:
            bm[:, 0:WU] = 0
        m["bmask"] = bm
        in_maps.append(m)
    return in_maps


def kernel(**inputs):
    in_maps = _host_inputs(inputs)
    nc = _get_nc()
    res = run_bass_kernel_spmd(nc, in_maps, core_ids=list(range(8)))
    out = np.empty((2, 4096, D), dtype=np.float32)
    for c in range(8):
        b, q = c // 4, c % 4
        out[b, q * TR:(q + 1) * TR, :] = res.results[c]["out_fm"].T
    return out
